# version 01
# baseline (speedup 1.0000x reference)
import numpy as np
import ml_dtypes as mld
import concourse.bass as bass
import concourse.mybir as mybir
import concourse.library_config as libcfg
from contextlib import ExitStack

F32 = mybir.dt.float32
BF16 = mybir.dt.bfloat16
I16 = mybir.dt.int16
I32 = mybir.dt.int32
U16 = mybir.dt.uint16
AF = mybir.ActivationFunctionType
ALU = mybir.AluOpType
AX = mybir.AxisListType

NSEQ, S, D = 4, 2048, 1024
T = NSEQ * S
NT = T // 128


def set_dims(nseq):
    global NSEQ, T, NT, NPMAX, PML, DVE_SEQ, ACT_SEQ, PE_SEQ
    NSEQ = nseq
    T = NSEQ * S
    NT = T // 128
    NPMAX = 2 * T // M2 + E
    PML = NPMAX * M2
    DVE_SEQ = 16 * A_DVE + VK_DVE + 4 * (B_DVE + 4 * EP_DVE)
    ACT_SEQ = 16 * A_ACT + Q_ACT + 4 * (B_ACT + 4 * EP_ACT)
    PE_SEQ = 16 * 3 + 8 + 4 * (17 + 12)
H = 32
E, F = 16, 1408
FC, DC = 11, 8
M2 = 512
NPMAX = 2 * T // M2 + E
PML = NPMAX * M2
CAP = 10240.0
EPS_LN = 1e-5
EPS_ATT = 1e-15
INV_D = 1.0 / D
RACE_CHECK = False

A_DVE, A_ACT = 7, 4
EP_DVE, EP_ACT = 17, 5
B_DVE, B_ACT = 9, 17
VK_DVE = 1
Q_ACT = 8

DVE_SEQ = 16 * A_DVE + VK_DVE + 4 * (B_DVE + 4 * EP_DVE)
ACT_SEQ = 16 * A_ACT + Q_ACT + 4 * (B_ACT + 4 * EP_ACT)
PE_SEQ = 16 * 3 + 8 + 4 * (17 + 12)


def dve_A(g, k=A_DVE):
    s, t = divmod(g, 16)
    return s * DVE_SEQ + t * A_DVE + k


def dve_vk(s):
    return s * DVE_SEQ + 16 * A_DVE + VK_DVE


def dve_B(s, gq, k=B_DVE):
    return dve_vk(s) + gq * (B_DVE + 4 * EP_DVE) + k


def dve_ep(g, k=EP_DVE):
    s, r = divmod(g, 16)
    gq, tt = divmod(r, 4)
    return dve_B(s, gq) + tt * EP_DVE + k


def act_A(g, k=A_ACT):
    s, t = divmod(g, 16)
    return s * ACT_SEQ + t * A_ACT + k


def act_q(s, o):
    return s * ACT_SEQ + 16 * A_ACT + (o + 1)


def act_B(s, gq, k=B_ACT):
    return act_q(s, Q_ACT - 1) + gq * (B_ACT + 4 * EP_ACT) + k


def act_ep(g, k=EP_ACT):
    s, r = divmod(g, 16)
    gq, tt = divmod(r, 4)
    return act_B(s, gq) + tt * EP_ACT + k


def pe_A(g, k):
    s, t = divmod(g, 16)
    return s * PE_SEQ + t * 3 + k


def pe_q(s, o):
    return s * PE_SEQ + 48 + (o + 1)


def pe_B(s, gq, k=17):
    return pe_q(s, 7) + gq * 29 + k


def pe_ep(g, k):
    s, r = divmod(g, 16)
    gq, tt = divmod(r, 4)
    return pe_B(s, gq) + tt * 3 + k


def RTD(name, g=0):
    tbl = {"oh12": 1, "base": 2, "cbrep": 3}
    if name in tbl:
        return tbl[name]
    if name == "carry":
        return 4 + 3 * g
    if name == "bev":
        return 5 + 3 * g
    if name == "runadd":
        return 6 + 3 * g
    off = 3 + 3 * NT
    return off + {"e2": 1, "expw": 2, "padt": 3, "pofft": 4,
                  "padr": 5, "pofr": 6, "scat": 7}[name]


def RT(name, g=0):
    if name == "base":
        return 1
    if name == "brep":
        return 2
    if name == "B":
        return 3 + 2 * g
    if name == "ct":
        return 4 + 2 * g
    off = 2 + 2 * NT
    return off + {"poffmm": 1, "padrtr": 2, "pofrtr": 3, "prep": 4}[name]


MP_PE = 34
MA = 19
MD = 15
CMB_DVE, CMB_ACT = 11, 3


def build(emit_ffn: bool):
    nc = bass.Bass("TRN2", target_bir_lowering=False,
                   detect_race_conditions=RACE_CHECK)
    din = lambda n, sh, dt: nc.dram_tensor(n, sh, dt, kind="ExternalInput").ap()
    dou = lambda n, sh, dt: nc.dram_tensor(n, sh, dt, kind="ExternalOutput").ap()
    dit = lambda n, sh, dt: nc.dram_tensor(n, sh, dt, kind="Internal").ap()

    xin = din("xin", [T, D], F32)
    wqkvT = din("wqkvT", [D, 3 * D], F32)
    wprojT = din("wprojT", [D, D], F32)
    pbias = din("pbias", [128, DC], F32)
    rw = din("rw", [D, E], F32)
    rbrep = din("rbrep", [128, E], F32)
    cidB = din("cidB", [128, 128], BF16)
    cidF = din("cidF", [128, 128], F32)
    cstrict = din("cstrict", [128, 128], F32)
    cones = din("cones", [128, 128], F32)
    ciota16 = din("ciota16", [128, E], F32)
    cexp32 = din("cexp32", [32, DC * 128], BF16)
    ceps = din("ceps", [128, 2], F32)
    if emit_ffn:
        wg = din("wg", [E, D, F], F32)
        wu = din("wu", [E, D, F], F32)
        wdn = din("wdn", [E, F, D], F32)
        swg = din("swg", [D, F], F32)
        swu = din("swu", [D, F], F32)
        swd = din("swd", [F, D], F32)
        ctid = din("ctid", [128, NT], I16)
        cshidx = din("cshidx", [128, T // 16], I16)
        cmask = din("cmask", [8, 1], F32)
        cntall = din("cntall", [8, E], F32)
        cntmeT = din("cntmeT", [16, 1], F32)
        wgbf = dit("wgbf", [E, D, F], BF16)
        wubf = dit("wubf", [E, D, F], BF16)
        wdbf = dit("wdbf", [E, F, D], BF16)
        swgbf = dit("swgbf", [D, F], BF16)
        swubf = dit("swubf", [D, F], BF16)
        swdbf = dit("swdbf", [F, D], BF16)
        xmid_d = dit("xmid_d", [T, D], F32)
        hrows_d = dit("hrows_d", [T, D], BF16)
        shout_d = dit("shout_d", [T, D], BF16)
        packed_d = dit("packed_d", [PML, D], BF16)
        plist_d = dit("plist_d", [16, PML // 16], I16)
        out_d = dou("out", [T, D], F32)
    else:
        counts_out = dou("counts_out", [1, E], F32)

    est = ExitStack()
    sb = lambda n, sh, dt: est.enter_context(nc.sbuf_tensor(n, sh, dt))
    psa = lambda n, sh, dt: est.enter_context(nc.psum_tensor(n, sh, dt))
    sem = lambda n: est.enter_context(nc.semaphore(n))

    with est:
        idB = sb("idB", [128, 128], BF16)
        idF = sb("idF", [128, 128], F32)
        strictL = sb("strictL", [128, 128], F32)
        ones = sb("ones", [128, 128], F32)
        iota16 = sb("iota16", [128, E], F32)
        exp32_sb = sb("exp32_sb", [32, DC * 128], BF16)
        rbrep_sb = sb("rbrep_sb", [128, E], F32)
        eps_sb = sb("eps_sb", [128, 2], F32)
        logits_w = sb("logits_w", [128, NT, E], F32)
        ln3a = sb("ln3a", [128, NT], F32)
        ln3b = sb("ln3b", [128, NT], F32)
        xm2 = sb("xm2", [128, 2, D], F32)
        i8 = sb("i8", [128, NT, 8], U16)
        l8 = sb("l8", [128, NT, 8], F32)
        oh12 = sb("oh12", [128, NT, E], F32)
        cnt_sb = sb("cnt_sb", [1, E], F32)
        run_sb = sb("run_sb", [1, E], F32)
        bk = [psa(f"bk{i}", [128, 512], F32) for i in range(6)]
        poolB = psa("poolB", [128, 2048], BF16)

        s_w0 = sem("s_w0")
        s_wc = sem("s_wc")
        s_x = [sem("s_x0"), sem("s_x1")]
        s_x2 = [sem("s_x2a"), sem("s_x2b")]
        s_dve = sem("s_dve")
        s_act = sem("s_act")
        s_pe = sem("s_pe")
        s_st = [sem("s_st0"), sem("s_st1")]
        s_hst = [sem("s_hst0"), sem("s_hst1")]
        s_rt = sem("s_rt")
        s_rtd = sem("s_rtd")
        s_rta = sem("s_rta")
        s_fin = sem("s_fin")
        s_ini = sem("s_ini")
        if emit_ffn:
            g1w = sb("g1w", [128, NT], F32)
            g2w = sb("g2w", [128, NT], F32)
            p1i = sb("p1i", [128, NT], I32)
            p2i = sb("p2i", [128, NT], I32)
            plist_sb = sb("plist_sb", [128, PML // 16], I16)
            nwin_i = sb("nwin_i", [16, 1], I32)
            woff_i = sb("woff_i", [16, 1], I32)
            nptot_i = sb("nptot_i", [1, 4], I32)
            cntT = sb("cntT", [16, 1], F32)
            cntall_sb = sb("cntall_sb", [8, E], F32)
            cmask_sb = sb("cmask_sb", [8, 1], F32)
            ctid_sb = sb("ctid_sb", [128, NT], I16)
            shidx_sb = sb("shidx_sb", [128, T // 16], I16)
            s_pl = sem("s_pl")
            s_wl = [sem("s_wl0"), sem("s_wl1")]
            s_wcS = sem("s_wcS")
            s_wcE = [sem("s_wcE0"), sem("s_wcE1")]
            s_zst = sem("s_zst")
            s_wfree = sem("s_wfree")
            s_gat = sem("s_gat")
            s_gfree = sem("s_gfree")
            s_mp = sem("s_mp")
            s_ma = sem("s_ma")
            s_md = sem("s_md")
            s_ost = sem("s_ost")
            s_cg = sem("s_cg")
            s_cl = sem("s_cl")
            s_cd = sem("s_cd")
            s_ca = sem("s_ca")
            s_so = sem("s_so")

        NW0 = 9 + (5 if emit_ffn else 0)
        NS2 = T // M2
        NWC = 3

        with ExitStack() as st1:
            sb1 = lambda n, sh, dt: st1.enter_context(nc.sbuf_tensor(n, sh, dt))
            wqkv_sb = sb1("wqkv_sb", [128, DC, 3 * D], BF16)
            wproj_sb = sb1("wproj_sb", [128, DC, D], BF16)
            pbias_sb = sb1("pbias_sb", [128, DC], F32)
            rw_sb = sb1("rw_sb", [128, DC, E], BF16)
            lnx_t = sb1("lnx_t", [128, 2, D], BF16)
            lnx_fm = sb1("lnx_fm", [128, DC, S], BF16)
            q_fm = sb1("q_fm", [128, DC, S], BF16)
            xring = sb1("xring", [128, 2, D], F32)
            k_tok = sb1("k_tok", [128, D], BF16)
            vpad = sb1("vpad", [128, H * 33], BF16)
            nstat = sb1("nstat", [128, DC, 128], BF16)
            dstat = sb1("dstat", [128, DC, 32], BF16)
            den_sb = sb1("den_sb", [32, 512], BF16)
            den2_sb = sb1("den2_sb", [32, 512], BF16)
            rep_sb = sb1("rep_sb", [128, 2, 512], BF16)
            proj_sb = sb1("proj_sb", [128, DC, 512], BF16)
            y_fm = sb1("y_fm", [128, DC, 512], BF16)
            ytok = sb1("ytok", [128, D], F32)
            ln1tmp = sb1("ln1tmp", [128, D], F32)
            hbf = sb1("hbf", [128, 2, D], BF16)
            sta = sb1("sta", [128, 4, 6], F32)

            kv_ps = [bk[0][:, :], bk[1][:, :], bk[2][:, :], bk[3][:, :]]
            vk_ps = bk[4][:, 0:264]
            den_ps = bk[4][0:32, 0:512]
            rep_ps = [bk[0][:, :], bk[1][:, :]]
            num_ps = [bk[2][:, :], bk[3][:, :]]
            y_ps = [bk[5][:, :], bk[4][:, :]]
            lgt_ps = bk[3][:, 128:144]
            tp_ps = [poolB[:, 0:1024], poolB[:, 1024:2048]]

            def ln_stats(vector, sem_, src_ap, stw, base, tmp):
                vector.wait_ge(sem_, base)
                vector.tensor_mul(tmp, src_ap, src_ap).then_inc(sem_, 1)
                vector.reduce_sum(stw[:, 0:1], src_ap, axis=AX.X).then_inc(sem_, 1)
                vector.wait_ge(sem_, base + 2)
                vector.scalar_tensor_tensor(stw[:, 2:3], stw[:, 0:1], INV_D,
                                            stw[:, 0:1], ALU.mult,
                                            ALU.mult).then_inc(sem_, 1)
                vector.reduce_sum(stw[:, 1:2], tmp, axis=AX.X).then_inc(sem_, 1)
                vector.wait_ge(sem_, base + 4)
                vector.tensor_sub(stw[:, 2:3], stw[:, 1:2],
                                  stw[:, 2:3]).then_inc(sem_, 1)

            def ln_finish(vector, sem_, stw, dsta, dstb, base,
                          sqrt_sem, sqrt_done):
                vector.wait_ge(sqrt_sem, sqrt_done)
                vector.reciprocal(dsta, stw[:, 3:4]).then_inc(sem_, 1)
                vector.wait_ge(sem_, base + 6)
                vector.scalar_tensor_tensor(dstb, stw[:, 0:1], -INV_D, dsta,
                                            ALU.mult, ALU.mult).then_inc(sem_, 1)

            with nc.Block() as blk:
                @blk.sync
                def _(sync):
                    for dst, src in [
                        (idB[:], cidB[:, :]), (idF[:], cidF[:, :]),
                        (strictL[:], cstrict[:, :]), (ones[:], cones[:, :]),
                        (iota16[:], ciota16[:, :]), (exp32_sb[:], cexp32[:, :]),
                        (pbias_sb[:], pbias[:, :]),
                        (rbrep_sb[:], rbrep[:, :]), (eps_sb[:], ceps[:, :]),
                    ]:
                        sync.dma_start(out=dst, in_=src).then_inc(s_w0, 16)
                    if emit_ffn:
                        for dst, src in [
                            (ctid_sb[:], ctid[:, :]), (shidx_sb[:], cshidx[:, :]),
                            (cmask_sb[:], cmask[:, :]), (cntall_sb[:], cntall[:, :]),
                            (cntT[:], cntmeT[:, :]),
                        ]:
                            sync.dma_start(out=dst, in_=src).then_inc(s_w0, 16)
                    for s in range(NSEQ):
                        for t in range(16):
                            g = 16 * s + t
                            if g >= 2:
                                sync.wait_ge(s_act, act_A(g - 2, 2))
                            sync.dma_start(out=xring[:, g % 2, :],
                                           in_=xin[g * 128:(g + 1) * 128, :]
                                           ).then_inc(s_x[g % 2], 16)
                        for t in range(16):
                            g = 16 * s + t
                            if g >= 2:
                                sync.wait_ge(s_act, act_ep(g - 2, 4))
                                if emit_ffn:
                                    sync.wait_ge(s_st[g % 2], 16 * (g // 2))
                            sync.dma_start(out=xm2[:, g % 2, :],
                                           in_=xin[g * 128:(g + 1) * 128, :]
                                           ).then_inc(s_x2[g % 2], 16)

                @blk.gpsimd
                def _(gpsimd):
                    gpsimd.dma_start(out=wqkv_sb[:, :, :],
                                     in_=wqkvT.rearrange("(c p) o -> p c o", p=128)
                                     ).then_inc(s_wc, 16)
                    gpsimd.dma_start(out=wproj_sb[:, :, :],
                                     in_=wprojT.rearrange("(c p) o -> p c o", p=128)
                                     ).then_inc(s_wc, 16)
                    gpsimd.dma_start(out=rw_sb[:, :, :],
                                     in_=rw.rearrange("(c p) o -> p c o", p=128)
                                     ).then_inc(s_wc, 16)
                    if emit_ffn:
                        for g in range(NT):
                            gpsimd.wait_ge(s_act, act_ep(g, 4))
                            gpsimd.dma_start(out=hrows_d[g * 128:(g + 1) * 128, :],
                                             in_=hbf[:, g % 2, :]).then_inc(s_hst[g % 2], 16)

                @blk.vector
                def _(vector):
                    vector.memset(vpad[:], 1.0)
                    vector.memset(nstat[:].rearrange("p c o -> p (c o)"), 0.0)
                    vector.memset(dstat[:].rearrange("p c o -> p (c o)"), 0.0)
                    vector.drain().then_inc(s_ini, 1)
                    for s in range(NSEQ):
                        for t in range(16):
                            g = 16 * s + t
                            vector.wait_ge(s_x[g % 2], 16 * (g // 2 + 1))
                            ln_stats(vector, s_dve, xring[:, g % 2, :],
                                     sta[:, g % 2, :], dve_A(g, 0), ln1tmp[:])
                            ln_finish(vector, s_dve, sta[:, g % 2, :],
                                      sta[:, g % 2, 4:5], sta[:, g % 2, 5:6],
                                      dve_A(g, 0), s_act, act_A(g, 1))
                        vector.wait_ge(s_pe, pe_A(16 * s + 15, 3))
                        for h in range(H):
                            c, j = h // 4, h % 4
                            vector.tensor_copy(
                                nstat[32 * j:32 * j + 32, c, 32 * j:32 * j + 32],
                                vk_ps[32 * j:32 * j + 32, 33 * c:33 * c + 32])
                            vector.tensor_copy(
                                dstat[32 * j:32 * j + 32, c, 4 * c + j:4 * c + j + 1],
                                vk_ps[32 * j:32 * j + 32, 33 * c + 32:33 * c + 33])
                        vector.drain().then_inc(s_dve, VK_DVE)
                        for gq in range(4):
                            vector.wait_ge(s_act, act_B(s, gq, 1))
                            with nc.allow_low_precision(reason="bf16 matmul operand"):
                                vector.reciprocal(den2_sb[:], den_sb[:]).then_inc(s_dve, 1)
                            for c in range(DC):
                                vector.wait_ge(s_pe, pe_B(s, gq, 1 + (c + 1)))
                                vector.wait_ge(s_act, act_B(s, gq, 1 + (c + 1)))
                                vector.tensor_tensor(
                                    proj_sb[:, c, :], num_ps[c % 2],
                                    rep_sb[:, c % 2, :], ALU.mult).then_inc(s_dve, 1)
                            for tt in range(4):
                                g = 16 * s + 4 * gq + tt
                                slot = 2 + g % 2
                                stw = sta[:, slot, :]
                                vector.wait_ge(s_pe, pe_ep(g, 1))
                                vector.tensor_copy(ytok[:],
                                                   tp_ps[g % 2]).then_inc(s_dve, 1)
                                ln_stats(vector, s_dve, ytok[:], stw,
                                         dve_ep(g, 1), ln1tmp[:])
                                ln_finish(vector, s_dve, stw,
                                          sta[:, slot, 4:5], sta[:, slot, 5:6],
                                          dve_ep(g, 1), s_act, act_ep(g, 1))
                                vector.wait_ge(s_act, act_ep(g, 2))
                                vector.wait_ge(s_x2[g % 2], 16 * (g // 2 + 1))
                                vector.tensor_add(xm2[:, g % 2, :], xm2[:, g % 2, :],
                                                  ytok[:]).then_inc(s_dve, 1)
                                ln_stats(vector, s_dve, xm2[:, g % 2, :], stw,
                                         dve_ep(g, 9), ln1tmp[:])
                                ln_finish(vector, s_dve, stw,
                                          ln3a[:, g:g + 1], ln3b[:, g:g + 1],
                                          dve_ep(g, 9), s_act, act_ep(g, 3))
                                vector.wait_ge(s_pe, pe_ep(g, 3))
                                vector.tensor_add(logits_w[:, g, :], lgt_ps,
                                                  rbrep_sb[:]).then_inc(s_dve, 1)
                    vector.nop().then_inc(s_fin, 1)

                @blk.scalar
                def _(scalar):
                    scalar.wait_ge(s_w0, 16 * NW0)
                    for s in range(NSEQ):
                        for t in range(16):
                            g = 16 * s + t
                            scalar.wait_ge(s_dve, dve_A(g, 5))
                            scalar.activation(out=sta[:, g % 2, 3:4],
                                              in_=sta[:, g % 2, 2:3], func=AF.Sqrt,
                                              scale=INV_D,
                                              bias=eps_sb[:, 0:1]).then_inc(s_act, 1)
                            scalar.wait_ge(s_dve, dve_A(g, 7))
                            if t < 2 and s > 0:
                                scalar.wait_ge(s_pe, pe_ep(16 * s - 2 + t, 3))
                            elif g >= 2:
                                scalar.wait_ge(s_pe, pe_A(g - 2, 1))
                            scalar.activation(out=lnx_t[:, g % 2, :],
                                              in_=xring[:, g % 2, :], func=AF.Identity,
                                              scale=sta[:, g % 2, 4:5],
                                              bias=sta[:, g % 2, 5:6]).then_inc(s_act, 1)
                            scalar.wait_ge(s_pe, pe_A(g, 1))
                            scalar.copy(out=lnx_fm[:, :, t * 128:(t + 1) * 128],
                                        in_=tp_ps[g % 2].rearrange(
                                            "p (c x) -> p c x", x=128)).then_inc(s_act, 1)
                            scalar.wait_ge(s_pe, pe_A(g, 2))
                            if t > 0:
                                scalar.wait_ge(s_pe, pe_A(g - 1, 3))
                            scalar.activation(out=k_tok[:, 0:512], in_=bk[0][:, :],
                                              func=AF.Relu)
                            scalar.activation(out=k_tok[:, 512:1024], in_=bk[1][:, :],
                                              func=AF.Relu)
                            scalar.activation(
                                out=vpad[:].rearrange("p (h e) -> p h e", e=33)
                                [:, 0:16, 0:32],
                                in_=bk[2][:, :].rearrange("p (h e) -> p h e", e=32),
                                func=AF.Relu)
                            scalar.activation(
                                out=vpad[:].rearrange("p (h e) -> p h e", e=33)
                                [:, 16:32, 0:32],
                                in_=bk[3][:, :].rearrange("p (h e) -> p h e", e=32),
                                func=AF.Relu)
                            scalar.drain().then_inc(s_act, 1)
                        for o in range(DC):
                            scalar.wait_ge(s_pe, pe_q(s, o))
                            for gq in range(4):
                                scalar.activation(out=q_fm[:, o, gq * 512:(gq + 1) * 512],
                                                  in_=kv_ps[gq], func=AF.Relu)
                            scalar.drain().then_inc(s_act, 1)
                        for gq in range(4):
                            scalar.wait_ge(s_pe, pe_B(s, gq, 1))
                            scalar.activation(out=den_sb[:], in_=den_ps,
                                              func=AF.Identity,
                                              bias=eps_sb[0:32, 1:2]).then_inc(s_act, 1)
                            for c in range(DC):
                                scalar.wait_ge(s_pe, pe_B(s, gq, 1 + (c + 1)))
                                scalar.copy(out=rep_sb[:, c % 2, :],
                                            in_=rep_ps[c % 2]).then_inc(s_act, 1)
                            for og in range(DC):
                                scalar.wait_ge(s_pe, pe_B(s, gq, 9 + (og + 1)))
                                scalar.activation(out=y_fm[:, og, :],
                                                  in_=y_ps[og % 2], func=AF.Identity,
                                                  bias=pbias_sb[:, og:og + 1]
                                                  ).then_inc(s_act, 1)
                            for tt in range(4):
                                g = 16 * s + 4 * gq + tt
                                slot = 2 + g % 2
                                scalar.wait_ge(s_dve, dve_ep(g, 6))
                                scalar.activation(out=sta[:, slot, 3:4],
                                                  in_=sta[:, slot, 2:3], func=AF.Sqrt,
                                                  scale=INV_D,
                                                  bias=eps_sb[:, 0:1]).then_inc(s_act, 1)
                                scalar.wait_ge(s_dve, dve_ep(g, 8))
                                scalar.activation(out=ytok[:],
                                                  in_=ytok[:],
                                                  func=AF.Identity,
                                                  scale=sta[:, slot, 4:5],
                                                  bias=sta[:, slot, 5:6]).then_inc(s_act, 1)
                                if emit_ffn:
                                    scalar.wait_ge(s_dve, dve_ep(g, 9))
                                    scalar.dma_start(
                                        out=xmid_d[g * 128:(g + 1) * 128, :],
                                        in_=xm2[:, g % 2, :]).then_inc(s_st[g % 2], 16)
                                scalar.wait_ge(s_dve, dve_ep(g, 14))
                                scalar.activation(out=sta[:, slot, 3:4],
                                                  in_=sta[:, slot, 2:3], func=AF.Sqrt,
                                                  scale=INV_D,
                                                  bias=eps_sb[:, 0:1]).then_inc(s_act, 1)
                                scalar.wait_ge(s_dve, dve_ep(g, 16))
                                if g >= 2:
                                    scalar.wait_ge(s_pe, pe_ep(g - 2, 2))
                                    if emit_ffn:
                                        scalar.wait_ge(s_hst[g % 2], 16 * (g // 2))
                                scalar.activation(out=hbf[:, g % 2, :],
                                                  in_=xm2[:, g % 2, :],
                                                  func=AF.Identity,
                                                  scale=ln3a[:, g:g + 1],
                                                  bias=ln3b[:, g:g + 1]).then_inc(s_act, 1)
                                scalar.wait_ge(s_pe, pe_ep(g, 2))
                                scalar.copy(out=lnx_t[:, g % 2, :],
                                            in_=tp_ps[g % 2]).then_inc(s_act, 1)
                    scalar.nop().then_inc(s_fin, 1)

                @blk.tensor
                def _(tensor):
                    tensor.wait_ge(s_wc, 16 * NWC)
                    tensor.wait_ge(s_w0, 16 * NW0)
                    tensor.wait_ge(s_ini, 1)
                    for s in range(NSEQ):
                        if s > 0:
                            tensor.wait_ge(s_dve, s * DVE_SEQ)
                            tensor.wait_ge(s_act, s * ACT_SEQ)
                        for t in range(16):
                            g = 16 * s + t
                            tensor.wait_ge(s_act, act_A(g, 2))
                            if g >= 2:
                                tensor.wait_ge(s_act, act_A(g - 2, 3))
                            for c in range(DC):
                                tensor.transpose(tp_ps[g % 2][:, c * 128:(c + 1) * 128],
                                                 lnx_t[:, g % 2, c * 128:(c + 1) * 128],
                                                 idB[:])
                            tensor.drain().then_inc(s_pe, 1)
                            tensor.wait_ge(s_act, act_A(g, 3))
                            if t > 0:
                                tensor.wait_ge(s_act, act_A(g - 1, 4))
                            for c in range(DC):
                                tensor.ldweights(lnx_fm[:, c, t * 128:(t + 1) * 128])
                                for og in range(4):
                                    tensor.matmul(
                                        kv_ps[og],
                                        lnx_fm[:, c, t * 128:(t + 1) * 128],
                                        wqkv_sb[:, c, D + og * 512:D + (og + 1) * 512],
                                        start=(c == 0), stop=(c == DC - 1))
                            tensor.drain().then_inc(s_pe, 1)
                            tensor.wait_ge(s_act, act_A(g, 4))
                            if t == 0 and s > 0:
                                tensor.wait_ge(s_dve, dve_vk(s - 1))
                            for h in range(H):
                                c, j = h // 4, h % 4
                                tensor.matmul(vk_ps[32 * j:32 * j + 32,
                                                    33 * c:33 * c + 33],
                                              k_tok[:, 32 * h:32 * h + 32],
                                              vpad[:, 33 * h:33 * h + 33],
                                              start=(t == 0 and c == 0),
                                              stop=(t == 15 and c == DC - 1),
                                              skip_group_check=True,
                                              tile_position=(0, 32 * j))
                            tensor.drain().then_inc(s_pe, 1)
                        for o in range(DC):
                            if o >= 1:
                                tensor.wait_ge(s_act, act_q(s, o - 1))
                            else:
                                tensor.wait_ge(s_act, act_A(16 * s + 15, 4))
                            for c in range(DC):
                                tensor.ldweights(wqkv_sb[:, c, o * 128:(o + 1) * 128])
                                for gq in range(4):
                                    tensor.matmul(kv_ps[gq],
                                                  wqkv_sb[:, c, o * 128:(o + 1) * 128],
                                                  lnx_fm[:, c, gq * 512:(gq + 1) * 512],
                                                  start=(c == 0), stop=(c == DC - 1))
                            tensor.drain().then_inc(s_pe, 1)
                        for gq in range(4):
                            tensor.wait_ge(s_dve, dve_vk(s))
                            tensor.wait_ge(s_act, act_q(s, 7))
                            if gq > 0:
                                tensor.wait_ge(s_dve, dve_ep(16 * s + 4 * gq - 1, 17))
                            for c in range(DC):
                                tensor.ldweights(dstat[:, c, :])
                                tensor.matmul(den_ps, dstat[:, c, :],
                                              q_fm[:, c, gq * 512:(gq + 1) * 512],
                                              start=(c == 0), stop=(c == DC - 1))
                            tensor.drain().then_inc(s_pe, 1)
                            for c in range(DC):
                                tensor.wait_ge(s_dve, dve_B(s, gq, 1))
                                if c >= 2:
                                    tensor.wait_ge(s_act, act_B(s, gq, c))
                                    tensor.wait_ge(s_dve, dve_B(s, gq, 0) + c)
                                tensor.ldweights(exp32_sb[:, c * 128:(c + 1) * 128])
                                tensor.matmul(rep_ps[c % 2],
                                              exp32_sb[:, c * 128:(c + 1) * 128],
                                              den2_sb[:], start=True, stop=True)
                                tensor.ldweights(nstat[:, c, :])
                                tensor.matmul(num_ps[c % 2], nstat[:, c, :],
                                              q_fm[:, c, gq * 512:(gq + 1) * 512],
                                              start=True, stop=True)
                                tensor.drain().then_inc(s_pe, 1)
                            for og in range(DC):
                                tensor.wait_ge(s_dve, dve_B(s, gq, 9))
                                if og >= 2:
                                    tensor.wait_ge(s_act, act_B(s, gq, 9 + og - 1))
                                for c in range(DC):
                                    tensor.ldweights(wproj_sb[:, c,
                                                              og * 128:(og + 1) * 128])
                                    tensor.matmul(y_ps[og % 2],
                                                  wproj_sb[:, c, og * 128:(og + 1) * 128],
                                                  proj_sb[:, c, :],
                                                  start=(c == 0), stop=(c == DC - 1))
                                tensor.drain().then_inc(s_pe, 1)
                            for tt in range(4):
                                g = 16 * s + 4 * gq + tt
                                tensor.wait_ge(s_act, act_B(s, gq, 17))
                                if g >= 2:
                                    tensor.wait_ge(s_act, act_ep(g - 2, 5))
                                for c in range(DC):
                                    tensor.transpose(
                                        tp_ps[g % 2][:, c * 128:(c + 1) * 128],
                                        y_fm[:, c, tt * 128:(tt + 1) * 128], idB[:])
                                tensor.drain().then_inc(s_pe, 1)
                                tensor.wait_ge(s_dve, dve_ep(g, 1))
                                tensor.wait_ge(s_act, act_ep(g, 4))
                                for c in range(DC):
                                    tensor.transpose(
                                        tp_ps[g % 2][:, c * 128:(c + 1) * 128],
                                        hbf[:, g % 2, c * 128:(c + 1) * 128], idB[:])
                                tensor.drain().then_inc(s_pe, 1)
                                tensor.wait_ge(s_act, act_ep(g, 5))
                                if g >= 1:
                                    tensor.wait_ge(s_dve, dve_ep(g - 1, 17))
                                for c in range(DC):
                                    tensor.ldweights(
                                        lnx_t[:, g % 2, c * 128:(c + 1) * 128])
                                    tensor.matmul(lgt_ps,
                                                  lnx_t[:, g % 2, c * 128:(c + 1) * 128],
                                                  rw_sb[:, c, :],
                                                  start=(c == 0), stop=(c == DC - 1))
                                tensor.drain().then_inc(s_pe, 1)
                    tensor.nop().then_inc(s_fin, 1)

        with ExitStack() as st2:
            sb2 = lambda n, sh, dt: st2.enter_context(nc.sbuf_tensor(n, sh, dt))
            tmpw = sb2("tmpw", [128, NT, E], F32)
            idx1f = sb2("idx1f", [128, NT], F32)
            idx2f = sb2("idx2f", [128, NT], F32)
            ct_ps = bk[3][0:1, 0:16]
            if emit_ffn:
                oh1 = sb2("oh1", [128, NT, E], F32)
                expw = sb2("expw", [128, NT, E], F32)
                e2w = sb2("e2w", [128, NT], F32)
                zw = sb2("zw", [128, NT], F32)
                v1w = sb2("v1w", [128, NT], F32)
                v2w = sb2("v2w", [128, NT], F32)
                r1w = sb2("r1w", [128, NT], F32)
                r2w = sb2("r2w", [128, NT], F32)
                base_sb = sb2("base_sb", [1, E], F32)
                cbase_rep = sb2("cbase_rep", [128, E], F32)
                carry_sb = sb2("carry_sb", [1, E], F32)
                Bw = sb2("Bw", [128, NT, E], F32)
                cTi = sb2("cTi", [16, 1], I32)
                padTi = sb2("padTi", [16, 1], I32)
                padT = sb2("padT", [16, 1], F32)
                poffT = sb2("poffT", [16, 1], F32)
                poffTi = sb2("poffTi", [16, 1], I32)
                pad_row = sb2("pad_row", [1, E], F32)
                poff_row = sb2("poff_row", [1, E], F32)
                poff_rep = sb2("poff_rep", [128, E], F32)
                d1i = sb2("d1i", [128, NT], I32)
                d2i = sb2("d2i", [128, NT], I32)
                dti = sb2("dti", [128, NT], I32)
                zrow = sb2("zrow", [16, PML // 16], I16)
                B_ps = [bk[4][:, 0:16], bk[5][:, 0:16]]
                base_ps = bk[1][0:1, 0:16]
                brep_ps = bk[2][:, 0:16]
                poff_ps = bk[1][0:16, 64:65]
                prow_ps = bk[1][0:1, 96:112]
                prep_ps = bk[2][:, 128:144]

            with nc.Block() as blk2:
                @blk2.vector
                def _(vector):
                    vector.drain()
                    for g in range(NT):
                        vector.max(l8[:, g, :], logits_w[:, g, :])
                        vector.drain()
                        vector.max_index(i8[:, g, :], l8[:, g, :],
                                         logits_w[:, g, :])
                    vector.drain()
                    vector.tensor_copy(idx1f[:], i8[:, :, 0])
                    vector.tensor_copy(idx2f[:], i8[:, :, 1])
                    vector.drain()
                    io_b = iota16[:].unsqueeze(1).broadcast_to([128, NT, E])
                    i1_b = idx1f[:].unsqueeze(2).broadcast_to([128, NT, E])
                    i2_b = idx2f[:].unsqueeze(2).broadcast_to([128, NT, E])
                    if emit_ffn:
                        vector.tensor_tensor(oh1[:], io_b, i1_b, ALU.is_equal)
                        vector.tensor_tensor(tmpw[:], io_b, i2_b, ALU.is_equal)
                        vector.drain()
                        vector.tensor_add(oh12[:], oh1[:], tmpw[:])
                    else:
                        vector.tensor_tensor(oh12[:], io_b, i1_b, ALU.is_equal)
                        vector.tensor_tensor(tmpw[:], io_b, i2_b, ALU.is_equal)
                        vector.drain()
                        vector.tensor_add(oh12[:], oh12[:], tmpw[:])
                    vector.drain().then_inc(s_rtd, 1)
                    if not emit_ffn:
                        vector.memset(run_sb[:], 0.0)
                        vector.drain()
                        for g in range(NT):
                            vector.wait_ge(s_rt, g + 1)
                            vector.tensor_add(run_sb[:], run_sb[:], ct_ps)
                            vector.drain().then_inc(s_rtd, 1)
                        vector.tensor_copy(cnt_sb[:], run_sb[:]).then_inc(s_rtd, 1)
                        vector.nop().then_inc(s_fin, 1)
                        return
                    vector.wait_ge(s_rt, RT("base"))
                    vector.tensor_copy(base_sb[:], base_ps).then_inc(s_rtd, 1)
                    vector.wait_ge(s_rt, RT("brep"))
                    vector.tensor_copy(cbase_rep[:], brep_ps).then_inc(s_rtd, 1)
                    vector.memset(run_sb[:], 0.0)
                    vector.drain()
                    for g in range(NT):
                        vector.tensor_add(carry_sb[:], base_sb[:], run_sb[:])
                        vector.drain().then_inc(s_rtd, 1)
                        vector.wait_ge(s_rt, RT("B", g))
                        vector.tensor_copy(Bw[:, g, :], B_ps[g % 2]).then_inc(s_rtd, 1)
                        vector.wait_ge(s_rt, RT("ct", g))
                        vector.tensor_add(run_sb[:], run_sb[:], ct_ps)
                        vector.drain().then_inc(s_rtd, 1)
                    vector.tensor_mul(tmpw[:], oh1[:], Bw[:])
                    vector.drain()
                    vector.reduce_sum(r1w[:], tmpw[:], axis=AX.X)
                    vector.drain()
                    vector.tensor_sub(tmpw[:], oh12[:], oh1[:])
                    vector.drain()
                    vector.tensor_mul(expw[:], tmpw[:], Bw[:])
                    vector.drain()
                    vector.reduce_sum(r2w[:], expw[:], axis=AX.X)
                    vector.tensor_sub(e2w[:], l8[:, :, 1], l8[:, :, 0])
                    vector.drain().then_inc(s_rtd, 1)
                    l1_b = l8[:, :, 0:1].broadcast_to([128, NT, E])
                    vector.tensor_sub(expw[:], logits_w[:], l1_b)
                    vector.drain().then_inc(s_rtd, 1)
                    vector.wait_ge(s_rta, 2)
                    vector.reduce_sum(zw[:], expw[:], axis=AX.X)
                    vector.drain()
                    vector.scalar_tensor_tensor(zw[:], zw[:], 1e-9, e2w[:],
                                                ALU.mult, ALU.add)
                    vector.drain()
                    vector.tensor_scalar_add(zw[:], zw[:], 1.0)
                    vector.drain()
                    vector.reciprocal(v1w[:], zw[:])
                    vector.drain()
                    vector.tensor_mul(v2w[:], e2w[:], v1w[:])
                    vector.tensor_single_scalar(g1w[:], r1w[:], CAP, ALU.is_lt)
                    vector.tensor_single_scalar(g2w[:], r2w[:], CAP, ALU.is_lt)
                    vector.drain()
                    vector.tensor_mul(g1w[:], g1w[:], v1w[:])
                    vector.tensor_mul(g2w[:], g2w[:], v2w[:])
                    vector.tensor_copy(cTi[:], cntT[:])
                    vector.drain()
                    vector.tensor_scalar_add(padTi[:], cTi[:], M2 - 1)
                    vector.drain()
                    vector.tensor_single_scalar(padTi[:], padTi[:], 9,
                                                ALU.logical_shift_right)
                    vector.drain()
                    vector.tensor_copy(nwin_i[:], padTi[:])
                    vector.drain()
                    vector.tensor_single_scalar(padTi[:], padTi[:], 9,
                                                ALU.logical_shift_left)
                    vector.drain()
                    vector.tensor_copy(padT[:], padTi[:])
                    vector.drain().then_inc(s_rtd, 1)
                    vector.wait_ge(s_rt, RT("poffmm"))
                    vector.tensor_copy(poffT[:], poff_ps)
                    vector.drain()
                    vector.tensor_copy(poffTi[:], poffT[:])
                    vector.drain()
                    vector.tensor_single_scalar(woff_i[:], poffTi[:], 9,
                                                ALU.logical_shift_right)
                    vector.drain().then_inc(s_rtd, 1)
                    vector.wait_ge(s_rt, RT("padrtr"))
                    vector.tensor_copy(pad_row[:], prow_ps)
                    vector.drain().then_inc(s_rtd, 1)
                    vector.wait_ge(s_rt, RT("pofrtr"))
                    vector.tensor_copy(poff_row[:], prow_ps)
                    vector.drain().then_inc(s_rtd, 1)
                    vector.wait_ge(s_rt, RT("prep"))
                    vector.tensor_copy(poff_rep[:], prep_ps)
                    vector.tensor_add(pad_row[0:1, 0:1], poff_row[0:1, 15:16],
                                      pad_row[0:1, 15:16])
                    vector.drain()
                    vector.tensor_copy(nptot_i[0:1, 0:1], pad_row[0:1, 0:1])
                    vector.drain()
                    vector.tensor_single_scalar(nptot_i[0:1, 0:1], nptot_i[0:1, 0:1],
                                                9, ALU.logical_shift_right)
                    pr_b = poff_rep[:].unsqueeze(1).broadcast_to([128, NT, E])
                    cb_b = cbase_rep[:].unsqueeze(1).broadcast_to([128, NT, E])
                    vector.tensor_mul(tmpw[:], oh1[:], cb_b)
                    vector.drain()
                    vector.reduce_sum(v1w[:], tmpw[:], axis=AX.X)
                    vector.drain()
                    vector.tensor_sub(r1w[:], r1w[:], v1w[:])
                    vector.tensor_mul(tmpw[:], oh1[:], pr_b)
                    vector.drain()
                    vector.reduce_sum(v1w[:], tmpw[:], axis=AX.X)
                    vector.drain()
                    vector.tensor_add(r1w[:], r1w[:], v1w[:])
                    vector.drain()
                    vector.tensor_copy(p1i[:], r1w[:])
                    vector.tensor_sub(tmpw[:], oh12[:], oh1[:])
                    vector.drain()
                    vector.tensor_mul(expw[:], tmpw[:], cb_b)
                    vector.drain()
                    vector.reduce_sum(v2w[:], expw[:], axis=AX.X)
                    vector.drain()
                    vector.tensor_sub(r2w[:], r2w[:], v2w[:])
                    vector.tensor_mul(expw[:], tmpw[:], pr_b)
                    vector.drain()
                    vector.reduce_sum(v2w[:], expw[:], axis=AX.X)
                    vector.drain()
                    vector.tensor_add(r2w[:], r2w[:], v2w[:])
                    vector.drain()
                    vector.tensor_copy(p2i[:], r2w[:])
                    for pi, ddst in ((p1i, d1i), (p2i, d2i)):
                        vector.drain()
                        vector.tensor_single_scalar(dti[:], pi[:], 15, ALU.bitwise_and)
                        vector.drain()
                        vector.tensor_single_scalar(dti[:], dti[:], PML // 16, ALU.mult)
                        vector.tensor_single_scalar(ddst[:], pi[:], 4,
                                                    ALU.logical_shift_right)
                        vector.drain()
                        vector.tensor_add(ddst[:], ddst[:], dti[:])
                    vector.memset(zrow[:], 0)
                    vector.drain().then_inc(s_rtd, 1)
                    vector.nop().then_inc(s_fin, 1)

                @blk2.scalar
                def _(scalar):
                    if emit_ffn:
                        scalar.wait_ge(s_rtd, RTD("e2"))
                        scalar.activation(out=e2w[:], in_=e2w[:], func=AF.Exp) \
                            .then_inc(s_rta, 1)
                        scalar.wait_ge(s_rtd, RTD("expw"))
                        scalar.activation(out=expw[:].rearrange("p t e -> p (t e)"),
                                          in_=expw[:].rearrange("p t e -> p (t e)"),
                                          func=AF.Exp).then_inc(s_rta, 1)
                    else:
                        scalar.wait_ge(s_rtd, NT + 2)
                        scalar.dma_start(out=counts_out[:, :],
                                         in_=cnt_sb[:]).then_inc(s_st[0], 16)
                        scalar.wait_ge(s_st[0], 16)
                    scalar.nop().then_inc(s_fin, 1)

                @blk2.tensor
                def _(tensor):
                    tensor.wait_ge(s_rtd, RTD("oh12"))
                    if not emit_ffn:
                        for g in range(NT):
                            if g >= 1:
                                tensor.wait_ge(s_rtd, 1 + g)
                            tensor.matmul(ct_ps, ones[:, 0:1], oh12[:, g, :],
                                          start=True, stop=True)
                            tensor.drain().then_inc(s_rt, 1)
                        tensor.nop().then_inc(s_fin, 1)
                        return
                    tensor.matmul(base_ps, cmask_sb[:], cntall_sb[:],
                                  start=True, stop=True)
                    tensor.drain().then_inc(s_rt, 1)
                    tensor.wait_ge(s_rtd, RTD("base"))
                    tensor.matmul(brep_ps, ones[0:1, :], base_sb[:],
                                  start=True, stop=True)
                    tensor.drain().then_inc(s_rt, 1)
                    for g in range(NT):
                        tensor.wait_ge(s_rtd, RTD("carry", g))
                        if g >= 2:
                            tensor.wait_ge(s_rtd, RTD("bev", g - 2))
                        tensor.matmul(B_ps[g % 2], strictL[:], oh12[:, g, :],
                                      start=True, stop=False)
                        tensor.matmul(B_ps[g % 2], ones[0:1, :], carry_sb[:],
                                      start=False, stop=True)
                        tensor.drain().then_inc(s_rt, 1)
                        if g >= 1:
                            tensor.wait_ge(s_rtd, RTD("runadd", g - 1))
                        tensor.matmul(ct_ps, ones[:, 0:1], oh12[:, g, :],
                                      start=True, stop=True)
                        tensor.drain().then_inc(s_rt, 1)
                    tensor.wait_ge(s_rtd, RTD("padt"))
                    tensor.matmul(poff_ps, strictL[0:16, 0:16], padT[:],
                                  start=True, stop=True)
                    tensor.drain().then_inc(s_rt, 1)
                    tensor.wait_ge(s_rtd, RTD("pofft"))
                    tensor.matmul(prow_ps, padT[:], idF[0:16, 0:16],
                                  start=True, stop=True)
                    tensor.drain().then_inc(s_rt, 1)
                    tensor.wait_ge(s_rtd, RTD("padr"))
                    tensor.matmul(prow_ps, poffT[:], idF[0:16, 0:16],
                                  start=True, stop=True)
                    tensor.drain().then_inc(s_rt, 1)
                    tensor.wait_ge(s_rtd, RTD("pofr"))
                    tensor.matmul(prep_ps, ones[0:1, :], poff_row[:],
                                  start=True, stop=True)
                    tensor.drain().then_inc(s_rt, 1)
                    tensor.nop().then_inc(s_fin, 1)

                @blk2.gpsimd
                def _(gpsimd):
                    if emit_ffn:
                        gpsimd.load_library(libcfg.mlp)
                        gpsimd.wait_ge(s_rtd, RTD("scat"))
                        gpsimd.dma_start(out=plist_d[:, :],
                                         in_=zrow[:]).then_inc(s_pl, 16)
                        gpsimd.wait_ge(s_pl, 16)
                        for g in range(NT):
                            for ddst in (d1i, d2i):
                                gpsimd.indirect_dma_start(
                                    out=plist_d.rearrange("a b -> (a b)").unsqueeze(1),
                                    out_offset=bass.IndirectOffsetOnAxis(
                                        ddst[:, g:g + 1], 0),
                                    in_=ctid_sb[:, g:g + 1], in_offset=None,
                                ).then_inc(s_pl, 16)
                        gpsimd.wait_ge(s_pl, 16 * (1 + 2 * NT))
                        for j in range(8):
                            gpsimd.dma_start(out=plist_sb[16 * j:16 * j + 16, :],
                                             in_=plist_d[:, :]).then_inc(s_pl, 16)
                        gpsimd.wait_ge(s_pl, 16 * (9 + 2 * NT))
                    gpsimd.nop().then_inc(s_fin, 1)

        if emit_ffn:
            with ExitStack() as st3:
                sb3 = lambda n, sh, dt: st3.enter_context(nc.sbuf_tensor(n, sh, dt))
                wgr = sb3("wgr", [128, 2, DC, F], BF16)
                wur = sb3("wur", [128, 2, DC, F], BF16)
                wdr = sb3("wdr", [128, 2, FC, D], BF16)
                gath = sb3("gath", [128, DC, M2], BF16)
                gu_sb = sb3("gu_sb", [128, FC, M2], BF16)
                dout_sb = sb3("dout_sb", [128, DC, M2], BF16)
                otok = sb3("otok", [128, 4, D], BF16)
                g_ps = [bk[0][:, :], bk[1][:, :]]
                u_ps = [bk[2][:, :], bk[3][:, :]]
                d_ps = [bk[4][:, :], bk[5][:, :]]
                tb_ps = [poolB[:, 0:1024], poolB[:, 1024:2048]]

                class ConstW:
                    def __init__(self, eng):
                        self.eng = eng

                    def begin(self, Pg):
                        self.Pg = Pg

                    def wait(self, sem2, base_stride, off):
                        self.eng.wait_ge(sem2, base_stride * self.Pg + off)

                    def end(self):
                        pass

                class ChainW:
                    def __init__(self, eng, first_pair, cfg):
                        self.eng = eng
                        self.regs = {}
                        self.stride = {}
                        self.first_off = {}
                        self.cur = {}
                        for i, (sem2, stride, foff) in enumerate(cfg):
                            key = id(sem2)
                            r = eng.alloc_register(f"cw{i}_{eng.engine}")
                            eng.reg_mov(r, stride * first_pair + foff)
                            self.regs[key] = r
                            self.stride[key] = stride
                            self.first_off[key] = foff
                            self.cur[key] = foff

                    def begin(self, Pg=None):
                        pass

                    def wait(self, sem2, stride, off):
                        key = id(sem2)
                        r = self.regs[key]
                        d = off - self.cur[key]
                        if d:
                            self.eng.reg_add(r, r, d)
                            self.cur[key] = off
                        self.eng.wait_ge(sem2, r)

                    def end(self):
                        for key, r in self.regs.items():
                            d = self.stride[key] - self.cur[key] + self.first_off[key]
                            if d:
                                self.eng.reg_add(r, r, d)
                            self.cur[key] = self.first_off[key]

                def pair_pe(tensor, slot, W):
                    W.wait(s_gat, 16, 16)
                    mv = gath[:, :, :]
                    for f in range(FC):
                        if f >= 2:
                            W.wait(s_ma, MA, f - 1)
                        for c in range(DC):
                            tensor.ldweights(wgr[:, slot, c, f * 128:(f + 1) * 128])
                            tensor.matmul(g_ps[f % 2],
                                          wgr[:, slot, c, f * 128:(f + 1) * 128],
                                          mv[:, c, :], start=(c == 0),
                                          stop=(c == DC - 1))
                        tensor.drain().then_inc(s_mp, 1)
                        if f >= 2:
                            W.wait(s_md, MD, f - 1)
                        for c in range(DC):
                            tensor.ldweights(wur[:, slot, c, f * 128:(f + 1) * 128])
                            tensor.matmul(u_ps[f % 2],
                                          wur[:, slot, c, f * 128:(f + 1) * 128],
                                          mv[:, c, :], start=(c == 0),
                                          stop=(c == DC - 1))
                        tensor.drain().then_inc(s_mp, 1)
                    tensor.drain().then_inc(s_gfree, 1)
                    W.wait(s_md, MD, FC)
                    for c in range(DC):
                        if c >= 2:
                            W.wait(s_ma, MA, FC + c - 1)
                        for f in range(FC):
                            tensor.ldweights(wdr[:, slot, f, c * 128:(c + 1) * 128])
                            tensor.matmul(d_ps[c % 2],
                                          wdr[:, slot, f, c * 128:(c + 1) * 128],
                                          gu_sb[:, f, :], start=(f == 0),
                                          stop=(f == FC - 1))
                        tensor.drain().then_inc(s_mp, 1)
                    W.wait(s_ma, MA, FC + DC)
                    W.wait(s_ost, 16, 0)
                    for tt in range(4):
                        if tt >= 2:
                            W.wait(s_md, MD, FC + tt - 1)
                        for c in range(DC):
                            tensor.transpose(
                                tb_ps[tt % 2][:, c * 128:(c + 1) * 128],
                                dout_sb[:, c, tt * 128:(tt + 1) * 128], idB[:])
                        tensor.drain().then_inc(s_mp, 1)

                def pair_act(scalar, W):
                    for f in range(FC):
                        W.wait(s_mp, MP_PE, 2 * f + 1)
                        scalar.activation(out=gu_sb[:, f, :], in_=g_ps[f % 2],
                                          func=AF.Silu).then_inc(s_ma, 1)
                    for c in range(DC):
                        W.wait(s_mp, MP_PE, 2 * FC + c + 1)
                        scalar.copy(out=dout_sb[:, c, :],
                                    in_=d_ps[c % 2]).then_inc(s_ma, 1)

                def pair_dve(vector, W):
                    for f in range(FC):
                        W.wait(s_mp, MP_PE, 2 * f + 2)
                        W.wait(s_ma, MA, f + 1)
                        vector.tensor_mul(gu_sb[:, f, :], gu_sb[:, f, :],
                                          u_ps[f % 2]).then_inc(s_md, 1)
                    for tt in range(4):
                        W.wait(s_mp, MP_PE, 2 * FC + DC + tt + 1)
                        vector.tensor_copy(otok[:, tt, :],
                                           tb_ps[tt % 2]).then_inc(s_md, 1)

                from contextlib import contextmanager
                from concourse.ordered_engine_set import OrderedEngineSet

                @contextmanager
                def reg_loop(eng, r_t, r_end):
                    lid = nc.next_id()
                    ls, le = f"rl{lid}_s", f"rl{lid}_e"
                    engs = OrderedEngineSet([eng.engine])
                    eng.reg_mov(r_t, 0)
                    eng.br(ls)
                    with nc.body(ls, valid_engines=engs):
                        yield
                        eng.reg_add(r_t, r_t, 1)
                        eng.br_lt(r_t, r_end, on_true=ls, on_false=le)
                    nc.switch_bb(le)

                def store_ap(P):
                    if P < NS2:
                        return shout_d[P * M2:(P + 1) * M2, :].rearrange(
                            "(tt p) d2 -> p tt d2", p=128)
                    p = P - NS2
                    return packed_d[p * M2:(p + 1) * M2, :].rearrange(
                        "(tt p) d2 -> p tt d2", p=128)

                with nc.Block() as blk3:
                    @blk3.gpsimd
                    def _(gpsimd):
                        gpsimd.memset(otok[:].rearrange("p a b -> p (a b)"), 0.0)
                        gpsimd.drain()
                        for p0 in range(NPMAX):
                            gpsimd.dma_start(out=store_ap(NS2 + p0),
                                             in_=otok[:, :, :]).then_inc(s_zst, 16)
                        for src_, dst_ in ((swg, swgbf), (swu, swubf), (swd, swdbf)):
                            gpsimd.dma_start(out=dst_[:, :],
                                             in_=src_[:, :]).then_inc(s_wcS, 16)
                        for e in range(E):
                            if e >= 2:
                                gpsimd.wait_ge(s_wcE[e % 2], 48 * (e // 2))
                            for src_, dst_ in ((wg, wgbf), (wu, wubf), (wdn, wdbf)):
                                gpsimd.dma_start(out=dst_[e], in_=src_[e]
                                                 ).then_inc(s_wcE[e % 2], 16)
                        gpsimd.wait_ge(s_pl, 16 * (9 + 2 * NT))
                        npreg = gpsimd.to_reg(gpsimd.value_load(
                            nptot_i[0:1, 0:1], min_val=0, max_val=NPMAX))
                        r_m2 = gpsimd.alloc_register("gat_n")
                        gpsimd.reg_mov(r_m2, M2)
                        NP_ALL = NS2 + NPMAX
                        for P in range(NP_ALL):
                            live = None
                            if P < NS2:
                                if P >= 1:
                                    gpsimd.wait_ge(s_gfree, P)
                                gpsimd.wait_ge(s_hst[0], 16 * (2 * P + 2))
                                gpsimd.wait_ge(s_hst[1], 16 * (2 * P + 2))
                                gpsimd.dma_gather(gath[:, :, :], hrows_d[:, :],
                                                  shidx_sb[:, 32 * P:32 * (P + 1)],
                                                  M2, r_m2, D, elem_step=D,
                                                  transpose=True).then_inc(s_gat, 16)
                            else:
                                p = P - NS2
                                with gpsimd.If_cmp(npreg, p, "IS_GT"):
                                    gpsimd.wait_ge(s_gfree, P)
                                    gpsimd.dma_gather(
                                        gath[:, :, :], hrows_d[:, :],
                                        plist_sb[:, 32 * p:32 * (p + 1)],
                                        M2, r_m2, D, elem_step=D,
                                        transpose=True).then_inc(s_gat, 16)
                                with gpsimd.Else():
                                    gpsimd.wait_ge(s_gat, 16 * P)
                                    gpsimd.inc_swdge_sem([s_gat], [16])
                            if P >= 1:
                                Q = P - 1
                                if Q < NS2:
                                    gpsimd.wait_ge(s_md, MD * (Q + 1))
                                    gpsimd.dma_start(out=store_ap(Q),
                                                     in_=otok[:, :, :]
                                                     ).then_inc(s_ost, 16)
                                else:
                                    with gpsimd.If_cmp(npreg, Q - NS2, "IS_GT"):
                                        gpsimd.wait_ge(s_md, MD * (Q + 1))
                                        gpsimd.dma_start(out=store_ap(Q),
                                                         in_=otok[:, :, :]
                                                         ).then_inc(s_ost, 16)
                                    with gpsimd.Else():
                                        gpsimd.wait_ge(s_ost, 16 * Q)
                                        gpsimd.inc_swdge_sem([s_ost], [16])
                        Q = NP_ALL - 1
                        with gpsimd.If_cmp(npreg, Q - NS2, "IS_GT"):
                            gpsimd.wait_ge(s_md, MD * (Q + 1))
                            gpsimd.dma_start(out=store_ap(Q), in_=otok[:, :, :]
                                             ).then_inc(s_ost, 16)
                        with gpsimd.Else():
                            gpsimd.wait_ge(s_ost, 16 * Q)
                            gpsimd.inc_swdge_sem([s_ost], [16])
                        gpsimd.nop().then_inc(s_fin, 1)

                    @blk3.sync
                    def _(sync):
                        sync.wait_ge(s_wcS, 16 * 3)
                        for wsrc, wdst in ((swgbf, wgr), (swubf, wur)):
                            sync.dma_start(
                                out=wdst[:, 0, :, :],
                                in_=wsrc.rearrange("(c p) f -> p c f", p=128)
                            ).then_inc(s_wl[0], 16)
                        sync.dma_start(
                            out=wdr[:, 0, :, :],
                            in_=swdbf.rearrange("(c p) f -> p c f", p=128)
                        ).then_inc(s_wl[0], 16)
                        for e in range(E):
                            slot = (e + 1) % 2
                            sync.wait_ge(s_wcE[e % 2], 48 * (e // 2 + 1))
                            if e >= 1:
                                sync.wait_ge(s_wfree, e)
                            for wsrc, wdst in ((wgbf, wgr), (wubf, wur)):
                                sync.dma_start(
                                    out=wdst[:, slot, :, :],
                                    in_=wsrc[e].rearrange("(c p) f -> p c f", p=128)
                                ).then_inc(s_wl[slot], 16)
                            sync.dma_start(
                                out=wdr[:, slot, :, :],
                                in_=wdbf[e].rearrange("(c p) f -> p c f", p=128)
                            ).then_inc(s_wl[slot], 16)

                    @blk3.tensor
                    def _(tensor):
                        tensor.wait_ge(s_wl[0], 16 * 3)
                        tensor.wait_ge(s_zst, 16 * NPMAX)
                        Wc = ConstW(tensor)
                        for k in range(NS2):
                            Wc.begin(k)
                            pair_pe(tensor, 0, Wc)
                            Wc.end()
                        tensor.drain().then_inc(s_wfree, 1)
                        Wx = ChainW(tensor, NS2, [
                            (s_gat, 16, 16), (s_ma, MA, 1), (s_md, MD, 1),
                            (s_ost, 16, 0)])
                        r_t = tensor.alloc_register("pe_rt")
                        r_nw = tensor.alloc_register("pe_rnw")
                        for e in range(E):
                            slot = (e + 1) % 2
                            slot_fills = (1 + (e + 1) // 2) if slot == 0 else (e // 2 + 1)
                            tensor.wait_ge(s_wl[slot], 48 * slot_fills)
                            tensor.reg_load(r_nw, nwin_i[e:e + 1, 0:1])
                            with reg_loop(tensor, r_t, r_nw):
                                Wx.begin()
                                pair_pe(tensor, slot, Wx)
                                Wx.end()
                            tensor.drain().then_inc(s_wfree, 1)
                        tensor.nop().then_inc(s_fin, 1)

                    @blk3.scalar
                    def _(scalar):
                        Wc = ConstW(scalar)
                        for k in range(NS2):
                            Wc.begin(k)
                            pair_act(scalar, Wc)
                            Wc.end()
                        scalar.wait_ge(s_rtd, RTD("padt"))
                        Wx = ChainW(scalar, NS2, [(s_mp, MP_PE, 1)])
                        r_t = scalar.alloc_register("ac_rt")
                        r_nw = scalar.alloc_register("ac_rnw")
                        for e in range(E):
                            scalar.reg_load(r_nw, nwin_i[e:e + 1, 0:1])
                            with reg_loop(scalar, r_t, r_nw):
                                Wx.begin()
                                pair_act(scalar, Wx)
                                Wx.end()
                        scalar.nop().then_inc(s_fin, 1)

                    @blk3.vector
                    def _(vector):
                        Wc = ConstW(vector)
                        for k in range(NS2):
                            Wc.begin(k)
                            pair_dve(vector, Wc)
                            Wc.end()
                        Wx = ChainW(vector, NS2, [(s_mp, MP_PE, 2), (s_ma, MA, 1)])
                        r_t = vector.alloc_register("dv_rt")
                        r_nw = vector.alloc_register("dv_rnw")
                        for e in range(E):
                            vector.reg_load(r_nw, nwin_i[e:e + 1, 0:1])
                            with reg_loop(vector, r_t, r_nw):
                                Wx.begin()
                                pair_dve(vector, Wx)
                                Wx.end()
                        vector.nop().then_inc(s_fin, 1)

            with ExitStack() as st4:
                sb4 = lambda n, sh, dt: st4.enter_context(nc.sbuf_tensor(n, sh, dt))
                row1 = sb4("row1", [128, 2, D], BF16)
                row2 = sb4("row2", [128, 2, D], BF16)
                shr = sb4("shr", [128, 2, D], BF16)
                cacc = sb4("cacc", [128, 2, D], F32)
                ctmp = sb4("ctmp", [128, 2, D], F32)
                sta4 = sb4("sta4", [128, 2, 6], F32)
                ln4tmp = sb4("ln4tmp", [128, D], F32)

                with nc.Block() as blk4:
                    @blk4.sync
                    def _(sync):
                        for g in range(NT):
                            if g >= 1:
                                sync.wait_ge(s_cd, CMB_DVE * g)
                                sync.wait_ge(s_ca, CMB_ACT * g)
                            sync.wait_ge(s_ost, 16 * (g // 32 + 1))
                            sync.wait_ge(s_st[0], 16 * (NT // 2))
                            sync.wait_ge(s_st[1], 16 * (NT // 2))
                            sync.dma_start(out=shr[:, g % 2, :],
                                           in_=shout_d[g * 128:(g + 1) * 128, :]
                                           ).then_inc(s_cl, 16)
                            sync.dma_start(out=xm2[:, g % 2, :],
                                           in_=xmid_d[g * 128:(g + 1) * 128, :]
                                           ).then_inc(s_cl, 16)

                    @blk4.gpsimd
                    def _(gpsimd):
                        nptot = gpsimd.value_load(nptot_i[0:1, 0:1],
                                                  min_val=0, max_val=NPMAX)
                        gpsimd.wait_ge(s_ost, 16 * (NS2 + nptot))
                        for g in range(NT):
                            if g >= 1:
                                gpsimd.wait_ge(s_cd, CMB_DVE * (g - 1) + 2)
                            gpsimd.indirect_dma_start(
                                out=row1[:, g % 2, :], out_offset=None,
                                in_=packed_d[:, :],
                                in_offset=bass.IndirectOffsetOnAxis(p1i[:, g:g + 1], 0),
                            ).then_inc(s_cg, 16)
                            gpsimd.indirect_dma_start(
                                out=row2[:, g % 2, :], out_offset=None,
                                in_=packed_d[:, :],
                                in_offset=bass.IndirectOffsetOnAxis(p2i[:, g:g + 1], 0),
                            ).then_inc(s_cg, 16)
                        gpsimd.nop().then_inc(s_fin, 1)

                    @blk4.vector
                    def _(vector):
                        for g in range(NT):
                            vector.wait_ge(s_cg, 32 * (g + 1))
                            vector.wait_ge(s_cl, 32 * (g + 1))
                            vector.scalar_tensor_tensor(
                                cacc[:, g % 2, :], row1[:, g % 2, :], g1w[:, g:g + 1],
                                shr[:, g % 2, :], ALU.mult, ALU.add).then_inc(s_cd, 1)
                            vector.wait_ge(s_cd, CMB_DVE * g + 1)
                            vector.scalar_tensor_tensor(
                                cacc[:, g % 2, :], row2[:, g % 2, :], g2w[:, g:g + 1],
                                cacc[:, g % 2, :], ALU.mult, ALU.add).then_inc(s_cd, 1)
                            vector.wait_ge(s_ca, CMB_ACT * g + 1)
                            vector.wait_ge(s_cd, CMB_DVE * g + 2)
                            vector.tensor_add(cacc[:, g % 2, :], cacc[:, g % 2, :],
                                              ctmp[:, g % 2, :]).then_inc(s_cd, 1)
                            stw = sta4[:, g % 2, :]
                            ln_stats(vector, s_cd, cacc[:, g % 2, :], stw,
                                     CMB_DVE * g + 3, ln4tmp[:])
                            ln_finish(vector, s_cd, stw, stw[:, 4:5], stw[:, 5:6],
                                      CMB_DVE * g + 3, s_ca, CMB_ACT * g + 2)
                            vector.wait_ge(s_ca, CMB_ACT * g + 3)
                            vector.tensor_add(ctmp[:, g % 2, :], ctmp[:, g % 2, :],
                                              xm2[:, g % 2, :]).then_inc(s_cd, 1)
                        vector.nop().then_inc(s_fin, 1)

                    @blk4.scalar
                    def _(scalar):
                        for g in range(NT):
                            scalar.wait_ge(s_cl, 32 * (g + 1))
                            if g >= 2:
                                scalar.wait_ge(s_so, 16 * (g - 1))
                            scalar.activation(out=ctmp[:, g % 2, :],
                                              in_=xm2[:, g % 2, :], func=AF.Identity,
                                              scale=ln3a[:, g:g + 1],
                                              bias=ln3b[:, g:g + 1]).then_inc(s_ca, 1)
                            scalar.wait_ge(s_cd, CMB_DVE * g + 8)
                            scalar.activation(out=sta4[:, g % 2, 3:4],
                                              in_=sta4[:, g % 2, 2:3], func=AF.Sqrt,
                                              scale=INV_D,
                                              bias=eps_sb[:, 0:1]).then_inc(s_ca, 1)
                            scalar.wait_ge(s_cd, CMB_DVE * g + 10)
                            scalar.activation(out=ctmp[:, g % 2, :],
                                              in_=cacc[:, g % 2, :], func=AF.Identity,
                                              scale=sta4[:, g % 2, 4:5],
                                              bias=sta4[:, g % 2, 5:6]).then_inc(s_ca, 1)
                            scalar.wait_ge(s_cd, CMB_DVE * (g + 1))
                            if g >= 1:
                                scalar.wait_ge(s_so, 16 * g)
                            scalar.dma_start(out=out_d[g * 128:(g + 1) * 128, :],
                                             in_=ctmp[:, g % 2, :]).then_inc(s_so, 16)
                        scalar.wait_ge(s_so, 16 * NT)
                        scalar.nop().then_inc(s_fin, 1)

    mybir.codegen_inst_isa_subclasses(nc)
    return nc


def make_consts():
    c = {}
    c["cidB"] = np.eye(128, dtype=mld.bfloat16)
    c["cidF"] = np.eye(128, dtype=np.float32)
    c["cstrict"] = np.triu(np.ones((128, 128), np.float32), 1)
    c["cones"] = np.ones((128, 128), np.float32)
    c["ciota16"] = np.tile(np.arange(E, dtype=np.float32), (128, 1))
    e32 = np.zeros((32, DC * 128), mld.bfloat16)
    for cc in range(DC):
        for p in range(128):
            e32[4 * cc + p // 32, cc * 128 + p] = 1
    c["cexp32"] = e32
    c["ceps"] = np.tile(np.array([[EPS_LN, EPS_ATT]], np.float32), (128, 1))
    tid = np.zeros((128, NT), np.int16)
    for g in range(NT):
        tid[:, g] = g * 128 + np.arange(128)
    c["ctid"] = tid
    ident = np.arange(T, dtype=np.int16)
    c["cshidx"] = np.tile(ident.reshape(-1, 16).T, (8, 1))
    return c


_CONSTS = None


def core_inputs_a(inp, core):
    global _CONSTS
    if _CONSTS is None:
        _CONSTS = make_consts()
    c = _CONSTS
    return {
        "xin": np.ascontiguousarray(inp["L0_attn_x"][NSEQ * core:NSEQ * (core + 1)]
                                    ).reshape(T, D),
        "wqkvT": np.ascontiguousarray(inp["L0_attn_qkv_weight"].T),
        "wprojT": np.ascontiguousarray(inp["L0_attn_proj_weight"].T),
        "pbias": np.ascontiguousarray(np.asarray(inp["L0_attn_proj_bias"]).reshape(8, 128).T),
        "rw": np.asarray(inp["L0_ffn_router_w"]),
        "rbrep": np.ascontiguousarray(np.tile(
            np.asarray(inp["L0_ffn_router_b"]).reshape(1, 16), (128, 1))),
        "cidB": c["cidB"], "cidF": c["cidF"], "cstrict": c["cstrict"],
        "cones": c["cones"], "ciota16": c["ciota16"], "cexp32": c["cexp32"],
        "ceps": c["ceps"],
    }


def core_inputs_b(inp, core, counts_all):
    m = core_inputs_a(inp, core)
    c = _CONSTS
    mask = np.zeros((8, 1), np.float32)
    mask[:core, 0] = 1.0
    m.update({
        "wg": np.asarray(inp["L0_ffn_w_gate"]),
        "wu": np.asarray(inp["L0_ffn_w_up"]),
        "wdn": np.asarray(inp["L0_ffn_w_down"]),
        "swg": np.asarray(inp["L0_ffn_sw_gate"]),
        "swu": np.asarray(inp["L0_ffn_sw_up"]),
        "swd": np.asarray(inp["L0_ffn_sw_down"]),
        "ctid": c["ctid"], "cshidx": c["cshidx"],
        "cmask": mask,
        "cntall": counts_all,
        "cntmeT": np.ascontiguousarray(counts_all[core].reshape(16, 1)),
    })
    return m


from concourse.bass_utils import run_bass_kernel_spmd


def kernel(**inputs):
    ncA = build(emit_ffn=False)
    mapsA = [core_inputs_a(inputs, c) for c in range(8)]
    resA = run_bass_kernel_spmd(ncA, mapsA, core_ids=list(range(8)))
    counts_all = np.stack([np.asarray(resA.results[c]["counts_out"])[0]
                           for c in range(8)]).astype(np.float32)

    ncB = build(emit_ffn=True)
    mapsB = [core_inputs_b(inputs, c, counts_all) for c in range(8)]
    resB = run_bass_kernel_spmd(ncB, mapsB, core_ids=list(range(8)))
    out = np.concatenate([np.asarray(resB.results[c]["out"], dtype=np.float32)
                          for c in range(8)], axis=0)
    return out.reshape(32, 2048, 1024)


# version 02
# speedup vs baseline: 1.1650x; 1.1650x over previous
import numpy as np
import ml_dtypes as mld
import concourse.bass as bass
import concourse.mybir as mybir
import concourse.library_config as libcfg
from contextlib import ExitStack

F32 = mybir.dt.float32
BF16 = mybir.dt.bfloat16
I16 = mybir.dt.int16
I32 = mybir.dt.int32
U16 = mybir.dt.uint16
AF = mybir.ActivationFunctionType
ALU = mybir.AluOpType
AX = mybir.AxisListType

NSEQ, S, D = 4, 2048, 1024
T = NSEQ * S
NT = T // 128


def set_dims(nseq):
    global NSEQ, T, NT, NPMAX, PML, DVE_SEQ, ACT_SEQ, PE_SEQ
    NSEQ = nseq
    T = NSEQ * S
    NT = T // 128
    NPMAX = 2 * T // M2 + E
    PML = NPMAX * M2
    DVE_SEQ = 16 * A_DVE + VK_DVE + 4 * (B_DVE + 4 * EP_DVE)
    ACT_SEQ = 16 * A_ACT + Q_ACT + 4 * (B_ACT + 4 * EP_ACT)
    PE_SEQ = 16 * 3 + 8 + 4 * (17 + 12)
H = 32
E, F = 16, 1408
FC, DC = 11, 8
M2 = 512
NPMAX = 2 * T // M2 + E
PML = NPMAX * M2
CAP = 10240.0
EPS_LN = 1e-5
EPS_ATT = 1e-15
INV_D = 1.0 / D
RACE_CHECK = False

A_DVE, A_ACT = 7, 4
EP_DVE, EP_ACT = 17, 5
B_DVE, B_ACT = 9, 17
VK_DVE = 1
Q_ACT = 8

DVE_SEQ = 16 * A_DVE + VK_DVE + 4 * (B_DVE + 4 * EP_DVE)
ACT_SEQ = 16 * A_ACT + Q_ACT + 4 * (B_ACT + 4 * EP_ACT)
PE_SEQ = 16 * 3 + 8 + 4 * (17 + 12)


def dve_A(g, k=A_DVE):
    s, t = divmod(g, 16)
    return s * DVE_SEQ + t * A_DVE + k


def dve_vk(s):
    return s * DVE_SEQ + 16 * A_DVE + VK_DVE


def dve_B(s, gq, k=B_DVE):
    return dve_vk(s) + gq * (B_DVE + 4 * EP_DVE) + k


def dve_ep(g, k=EP_DVE):
    s, r = divmod(g, 16)
    gq, tt = divmod(r, 4)
    return dve_B(s, gq) + tt * EP_DVE + k


def act_A(g, k=A_ACT):
    s, t = divmod(g, 16)
    return s * ACT_SEQ + t * A_ACT + k


def act_q(s, o):
    return s * ACT_SEQ + 16 * A_ACT + (o + 1)


def act_B(s, gq, k=B_ACT):
    return act_q(s, Q_ACT - 1) + gq * (B_ACT + 4 * EP_ACT) + k


def act_ep(g, k=EP_ACT):
    s, r = divmod(g, 16)
    gq, tt = divmod(r, 4)
    return act_B(s, gq) + tt * EP_ACT + k


def pe_A(g, k):
    s, t = divmod(g, 16)
    return s * PE_SEQ + t * 3 + k


def pe_q(s, o):
    return s * PE_SEQ + 48 + (o + 1)


def pe_B(s, gq, k=17):
    return pe_q(s, 7) + gq * 29 + k


def pe_ep(g, k):
    s, r = divmod(g, 16)
    gq, tt = divmod(r, 4)
    return pe_B(s, gq) + tt * 3 + k


def RTD(name, g=0):
    tbl = {"oh12": 1, "base": 2, "cbrep": 3}
    if name in tbl:
        return tbl[name]
    if name == "carry":
        return 4 + 3 * g
    if name == "bev":
        return 5 + 3 * g
    if name == "runadd":
        return 6 + 3 * g
    off = 3 + 3 * NT
    return off + {"e2": 1, "expw": 2, "padt": 3, "pofft": 4,
                  "padr": 5, "pofr": 6, "scat": 7}[name]


def RT(name, g=0):
    if name == "base":
        return 1
    if name == "brep":
        return 2
    if name == "B":
        return 3 + 2 * g
    if name == "ct":
        return 4 + 2 * g
    off = 2 + 2 * NT
    return off + {"poffmm": 1, "padrtr": 2, "pofrtr": 3, "prep": 4}[name]


MP_PE = 34
MA = 19
MD = 15
CMB_DVE, CMB_ACT = 11, 3


def build(emit_ffn: bool):
    nc = bass.Bass("TRN2", target_bir_lowering=False,
                   detect_race_conditions=RACE_CHECK)
    din = lambda n, sh, dt: nc.dram_tensor(n, sh, dt, kind="ExternalInput").ap()
    dou = lambda n, sh, dt: nc.dram_tensor(n, sh, dt, kind="ExternalOutput").ap()
    dit = lambda n, sh, dt: nc.dram_tensor(n, sh, dt, kind="Internal").ap()

    xin = din("xin", [T, D], F32)
    wqkvT = din("wqkvT", [D, 3 * D], F32)
    wprojT = din("wprojT", [D, D], F32)
    pbias = din("pbias", [128, DC], F32)
    rw = din("rw", [D, E], F32)
    rbrep = din("rbrep", [128, E], F32)
    cidB = din("cidB", [128, 128], BF16)
    cidF = din("cidF", [128, 128], F32)
    cstrict = din("cstrict", [128, 128], F32)
    cones = din("cones", [128, 128], F32)
    ciota16 = din("ciota16", [128, E], F32)
    cexp32 = din("cexp32", [32, DC * 128], BF16)
    ceps = din("ceps", [128, 2], F32)
    if emit_ffn:
        wg = din("wg", [E, D, F], F32)
        wu = din("wu", [E, D, F], F32)
        wdn = din("wdn", [E, F, D], F32)
        swg = din("swg", [D, F], F32)
        swu = din("swu", [D, F], F32)
        swd = din("swd", [F, D], F32)
        ctid = din("ctid", [128, NT], I16)
        cshidx = din("cshidx", [128, T // 16], I16)
        cmask = din("cmask", [8, 1], F32)
        cntall = din("cntall", [8, E], F32)
        cntmeT = din("cntmeT", [16, 1], F32)
        wgbf = dit("wgbf", [E, D, F], BF16)
        wubf = dit("wubf", [E, D, F], BF16)
        wdbf = dit("wdbf", [E, F, D], BF16)
        swgbf = dit("swgbf", [D, F], BF16)
        swubf = dit("swubf", [D, F], BF16)
        swdbf = dit("swdbf", [F, D], BF16)
        xmid_d = dit("xmid_d", [T, D], F32)
        hrows_d = dit("hrows_d", [T, D], BF16)
        shout_d = dit("shout_d", [T, D], BF16)
        packed_d = dit("packed_d", [PML, D], BF16)
        plist_d = dit("plist_d", [16, PML // 16], I16)
        out_d = dou("out", [T, D], F32)
    else:
        counts_out = dou("counts_out", [1, E], F32)

    est = ExitStack()
    sb = lambda n, sh, dt: est.enter_context(nc.sbuf_tensor(n, sh, dt))
    psa = lambda n, sh, dt: est.enter_context(nc.psum_tensor(n, sh, dt))
    sem = lambda n: est.enter_context(nc.semaphore(n))

    with est:
        idB = sb("idB", [128, 128], BF16)
        idF = sb("idF", [128, 128], F32)
        strictL = sb("strictL", [128, 128], F32)
        ones = sb("ones", [128, 128], F32)
        iota16 = sb("iota16", [128, E], F32)
        exp32_sb = sb("exp32_sb", [32, DC * 128], BF16)
        rbrep_sb = sb("rbrep_sb", [128, E], F32)
        eps_sb = sb("eps_sb", [128, 2], F32)
        logits_w = sb("logits_w", [128, NT, E], F32)
        ln3a = sb("ln3a", [128, NT], F32)
        ln3b = sb("ln3b", [128, NT], F32)
        xm2 = sb("xm2", [128, 2, D], F32)
        i8 = sb("i8", [128, NT, 8], U16)
        l8 = sb("l8", [128, NT, 8], F32)
        oh12 = sb("oh12", [128, NT, E], F32)
        cnt_sb = sb("cnt_sb", [1, E], F32)
        run_sb = sb("run_sb", [1, E], F32)
        bk = [psa(f"bk{i}", [128, 512], F32) for i in range(6)]
        poolB = psa("poolB", [128, 2048], BF16)

        s_w0 = sem("s_w0")
        s_wc = sem("s_wc")
        s_x = [sem("s_x0"), sem("s_x1")]
        s_x2 = [sem("s_x2a"), sem("s_x2b")]
        s_dve = sem("s_dve")
        s_act = sem("s_act")
        s_pe = sem("s_pe")
        s_st = [sem("s_st0"), sem("s_st1")]
        s_hst = [sem("s_hst0"), sem("s_hst1")]
        s_rt = sem("s_rt")
        s_rtd = sem("s_rtd")
        s_rta = sem("s_rta")
        s_fin = sem("s_fin")
        s_ini = sem("s_ini")
        if emit_ffn:
            g1w = sb("g1w", [128, NT], F32)
            g2w = sb("g2w", [128, NT], F32)
            p1i = sb("p1i", [128, NT], I32)
            p2i = sb("p2i", [128, NT], I32)
            plist_sb = sb("plist_sb", [128, PML // 16], I16)
            nwin_i = sb("nwin_i", [16, 1], I32)
            woff_i = sb("woff_i", [16, 1], I32)
            nptot_i = sb("nptot_i", [1, 4], I32)
            cntT = sb("cntT", [16, 1], F32)
            cntall_sb = sb("cntall_sb", [8, E], F32)
            cmask_sb = sb("cmask_sb", [8, 1], F32)
            ctid_sb = sb("ctid_sb", [128, NT], I16)
            shidx_sb = sb("shidx_sb", [128, T // 16], I16)
            s_pl = sem("s_pl")
            s_wl = [sem("s_wl0"), sem("s_wl1")]
            s_wcS = sem("s_wcS")
            s_wcE = [sem("s_wcE0"), sem("s_wcE1")]
            s_wfree = sem("s_wfree")
            s_gat = sem("s_gat")
            s_gfree = sem("s_gfree")
            s_mp = sem("s_mp")
            s_ma = sem("s_ma")
            s_md = sem("s_md")
            s_ost = sem("s_ost")
            s_cg = sem("s_cg")
            s_cl = sem("s_cl")
            s_cd = sem("s_cd")
            s_ca = sem("s_ca")
            s_so = sem("s_so")

        NW0 = 9 + (5 if emit_ffn else 0)
        NS2 = T // M2
        NWC = 3

        with ExitStack() as st1:
            sb1 = lambda n, sh, dt: st1.enter_context(nc.sbuf_tensor(n, sh, dt))
            wqkv_sb = sb1("wqkv_sb", [128, DC, 3 * D], BF16)
            wproj_sb = sb1("wproj_sb", [128, DC, D], BF16)
            pbias_sb = sb1("pbias_sb", [128, DC], F32)
            rw_sb = sb1("rw_sb", [128, DC, E], BF16)
            lnx_t = sb1("lnx_t", [128, 2, D], BF16)
            lnx_fm = sb1("lnx_fm", [128, DC, S], BF16)
            q_fm = sb1("q_fm", [128, DC, S], BF16)
            xring = sb1("xring", [128, 2, D], F32)
            k_tok = sb1("k_tok", [128, D], BF16)
            vpad = sb1("vpad", [128, H * 33], BF16)
            nstat = sb1("nstat", [128, DC, 128], BF16)
            dstat = sb1("dstat", [128, DC, 32], BF16)
            den_sb = sb1("den_sb", [32, 512], BF16)
            den2_sb = sb1("den2_sb", [32, 512], BF16)
            rep_sb = sb1("rep_sb", [128, 2, 512], BF16)
            proj_sb = sb1("proj_sb", [128, DC, 512], BF16)
            y_fm = sb1("y_fm", [128, DC, 512], BF16)
            ytok = sb1("ytok", [128, D], F32)
            ln1tmp = sb1("ln1tmp", [128, D], F32)
            hbf = sb1("hbf", [128, 2, D], BF16)
            sta = sb1("sta", [128, 4, 6], F32)

            kv_ps = [bk[0][:, :], bk[1][:, :], bk[2][:, :], bk[3][:, :]]
            vk_ps = bk[4][:, 0:264]
            den_ps = bk[4][0:32, 0:512]
            rep_ps = [bk[0][:, :], bk[1][:, :]]
            num_ps = [bk[2][:, :], bk[3][:, :]]
            y_ps = [bk[5][:, :], bk[4][:, :]]
            lgt_ps = bk[3][:, 128:144]
            tp_ps = [poolB[:, 0:1024], poolB[:, 1024:2048]]

            def ln_stats(vector, sem_, src_ap, stw, base, tmp):
                vector.wait_ge(sem_, base)
                vector.tensor_mul(tmp, src_ap, src_ap).then_inc(sem_, 1)
                vector.reduce_sum(stw[:, 0:1], src_ap, axis=AX.X).then_inc(sem_, 1)
                vector.wait_ge(sem_, base + 2)
                vector.scalar_tensor_tensor(stw[:, 2:3], stw[:, 0:1], INV_D,
                                            stw[:, 0:1], ALU.mult,
                                            ALU.mult).then_inc(sem_, 1)
                vector.reduce_sum(stw[:, 1:2], tmp, axis=AX.X).then_inc(sem_, 1)
                vector.wait_ge(sem_, base + 4)
                vector.tensor_sub(stw[:, 2:3], stw[:, 1:2],
                                  stw[:, 2:3]).then_inc(sem_, 1)

            def ln_finish(vector, sem_, stw, dsta, dstb, base,
                          sqrt_sem, sqrt_done):
                vector.wait_ge(sqrt_sem, sqrt_done)
                vector.reciprocal(dsta, stw[:, 3:4]).then_inc(sem_, 1)
                vector.wait_ge(sem_, base + 6)
                vector.scalar_tensor_tensor(dstb, stw[:, 0:1], -INV_D, dsta,
                                            ALU.mult, ALU.mult).then_inc(sem_, 1)

            with nc.Block() as blk:
                @blk.sync
                def _(sync):
                    for dst, src in [
                        (idB[:], cidB[:, :]), (idF[:], cidF[:, :]),
                        (strictL[:], cstrict[:, :]), (ones[:], cones[:, :]),
                        (iota16[:], ciota16[:, :]), (exp32_sb[:], cexp32[:, :]),
                        (pbias_sb[:], pbias[:, :]),
                        (rbrep_sb[:], rbrep[:, :]), (eps_sb[:], ceps[:, :]),
                    ]:
                        sync.dma_start(out=dst, in_=src).then_inc(s_w0, 16)
                    if emit_ffn:
                        for dst, src in [
                            (ctid_sb[:], ctid[:, :]), (shidx_sb[:], cshidx[:, :]),
                            (cmask_sb[:], cmask[:, :]), (cntall_sb[:], cntall[:, :]),
                            (cntT[:], cntmeT[:, :]),
                        ]:
                            sync.dma_start(out=dst, in_=src).then_inc(s_w0, 16)
                    for s in range(NSEQ):
                        for t in range(16):
                            g = 16 * s + t
                            if g >= 2:
                                sync.wait_ge(s_act, act_A(g - 2, 2))
                            sync.dma_start(out=xring[:, g % 2, :],
                                           in_=xin[g * 128:(g + 1) * 128, :]
                                           ).then_inc(s_x[g % 2], 16)
                        for t in range(16):
                            g = 16 * s + t
                            if g >= 2:
                                sync.wait_ge(s_act, act_ep(g - 2, 4))
                                if emit_ffn:
                                    sync.wait_ge(s_st[g % 2], 16 * (g // 2))
                            sync.dma_start(out=xm2[:, g % 2, :],
                                           in_=xin[g * 128:(g + 1) * 128, :]
                                           ).then_inc(s_x2[g % 2], 16)

                @blk.gpsimd
                def _(gpsimd):
                    gpsimd.dma_start(out=wqkv_sb[:, :, :],
                                     in_=wqkvT.rearrange("(c p) o -> p c o", p=128)
                                     ).then_inc(s_wc, 16)
                    gpsimd.dma_start(out=wproj_sb[:, :, :],
                                     in_=wprojT.rearrange("(c p) o -> p c o", p=128)
                                     ).then_inc(s_wc, 16)
                    gpsimd.dma_start(out=rw_sb[:, :, :],
                                     in_=rw.rearrange("(c p) o -> p c o", p=128)
                                     ).then_inc(s_wc, 16)
                    if emit_ffn:
                        for src_, dst_ in ((swg, swgbf), (swu, swubf),
                                           (swd, swdbf)):
                            gpsimd.dma_start(out=dst_[:, :],
                                             in_=src_[:, :]).then_inc(s_wcS, 16)
                        for e in range(E):
                            if e >= 2:
                                gpsimd.wait_ge(s_wcE[e % 2], 48 * (e // 2))
                            for src_, dst_ in ((wg, wgbf), (wu, wubf),
                                               (wdn, wdbf)):
                                gpsimd.dma_start(out=dst_[e], in_=src_[e]
                                                 ).then_inc(s_wcE[e % 2], 16)

                @blk.vector
                def _(vector):
                    vector.memset(vpad[:], 1.0)
                    vector.memset(nstat[:].rearrange("p c o -> p (c o)"), 0.0)
                    vector.memset(dstat[:].rearrange("p c o -> p (c o)"), 0.0)
                    vector.drain().then_inc(s_ini, 1)
                    for s in range(NSEQ):
                        for t in range(16):
                            g = 16 * s + t
                            vector.wait_ge(s_x[g % 2], 16 * (g // 2 + 1))
                            ln_stats(vector, s_dve, xring[:, g % 2, :],
                                     sta[:, g % 2, :], dve_A(g, 0), ln1tmp[:])
                            ln_finish(vector, s_dve, sta[:, g % 2, :],
                                      sta[:, g % 2, 4:5], sta[:, g % 2, 5:6],
                                      dve_A(g, 0), s_act, act_A(g, 1))
                        vector.wait_ge(s_pe, pe_A(16 * s + 15, 3))
                        for h in range(H):
                            c, j = h // 4, h % 4
                            vector.tensor_copy(
                                nstat[32 * j:32 * j + 32, c, 32 * j:32 * j + 32],
                                vk_ps[32 * j:32 * j + 32, 33 * c:33 * c + 32])
                            vv = vector.tensor_copy(
                                dstat[32 * j:32 * j + 32, c, 4 * c + j:4 * c + j + 1],
                                vk_ps[32 * j:32 * j + 32, 33 * c + 32:33 * c + 33])
                        vv.then_inc(s_dve, VK_DVE)
                        for gq in range(4):
                            vector.wait_ge(s_act, act_B(s, gq, 1))
                            with nc.allow_low_precision(reason="bf16 matmul operand"):
                                vector.reciprocal(den2_sb[:], den_sb[:]).then_inc(s_dve, 1)
                            for c in range(DC):
                                vector.wait_ge(s_pe, pe_B(s, gq, 1 + (c + 1)))
                                vector.wait_ge(s_act, act_B(s, gq, 1 + (c + 1)))
                                vector.tensor_tensor(
                                    proj_sb[:, c, :], num_ps[c % 2],
                                    rep_sb[:, c % 2, :], ALU.mult).then_inc(s_dve, 1)
                            for tt in range(4):
                                g = 16 * s + 4 * gq + tt
                                slot = 2 + g % 2
                                stw = sta[:, slot, :]
                                vector.wait_ge(s_pe, pe_ep(g, 1))
                                vector.tensor_copy(ytok[:],
                                                   tp_ps[g % 2]).then_inc(s_dve, 1)
                                ln_stats(vector, s_dve, ytok[:], stw,
                                         dve_ep(g, 1), ln1tmp[:])
                                ln_finish(vector, s_dve, stw,
                                          sta[:, slot, 4:5], sta[:, slot, 5:6],
                                          dve_ep(g, 1), s_act, act_ep(g, 1))
                                vector.wait_ge(s_act, act_ep(g, 2))
                                vector.wait_ge(s_x2[g % 2], 16 * (g // 2 + 1))
                                vector.tensor_add(xm2[:, g % 2, :], xm2[:, g % 2, :],
                                                  ytok[:]).then_inc(s_dve, 1)
                                ln_stats(vector, s_dve, xm2[:, g % 2, :], stw,
                                         dve_ep(g, 9), ln1tmp[:])
                                ln_finish(vector, s_dve, stw,
                                          ln3a[:, g:g + 1], ln3b[:, g:g + 1],
                                          dve_ep(g, 9), s_act, act_ep(g, 3))
                                vector.wait_ge(s_pe, pe_ep(g, 3))
                                vector.tensor_add(logits_w[:, g, :], lgt_ps,
                                                  rbrep_sb[:]).then_inc(s_dve, 1)
                    vector.nop().then_inc(s_fin, 1)

                @blk.scalar
                def _(scalar):
                    scalar.wait_ge(s_w0, 16 * NW0)
                    for s in range(NSEQ):
                        for t in range(16):
                            g = 16 * s + t
                            scalar.wait_ge(s_dve, dve_A(g, 5))
                            scalar.activation(out=sta[:, g % 2, 3:4],
                                              in_=sta[:, g % 2, 2:3], func=AF.Sqrt,
                                              scale=INV_D,
                                              bias=eps_sb[:, 0:1]).then_inc(s_act, 1)
                            scalar.wait_ge(s_dve, dve_A(g, 7))
                            if t < 2 and s > 0:
                                scalar.wait_ge(s_pe, pe_ep(16 * s - 2 + t, 3))
                            elif g >= 2:
                                scalar.wait_ge(s_pe, pe_A(g - 2, 1))
                            scalar.activation(out=lnx_t[:, g % 2, :],
                                              in_=xring[:, g % 2, :], func=AF.Identity,
                                              scale=sta[:, g % 2, 4:5],
                                              bias=sta[:, g % 2, 5:6]).then_inc(s_act, 1)
                            scalar.wait_ge(s_pe, pe_A(g, 1))
                            scalar.copy(out=lnx_fm[:, :, t * 128:(t + 1) * 128],
                                        in_=tp_ps[g % 2].rearrange(
                                            "p (c x) -> p c x", x=128)).then_inc(s_act, 1)
                            scalar.wait_ge(s_pe, pe_A(g, 2))
                            if t > 0:
                                scalar.wait_ge(s_pe, pe_A(g - 1, 3))
                            scalar.activation(out=k_tok[:, 0:512], in_=bk[0][:, :],
                                              func=AF.Relu)
                            scalar.activation(out=k_tok[:, 512:1024], in_=bk[1][:, :],
                                              func=AF.Relu)
                            scalar.activation(
                                out=vpad[:].rearrange("p (h e) -> p h e", e=33)
                                [:, 0:16, 0:32],
                                in_=bk[2][:, :].rearrange("p (h e) -> p h e", e=32),
                                func=AF.Relu)
                            scalar.activation(
                                out=vpad[:].rearrange("p (h e) -> p h e", e=33)
                                [:, 16:32, 0:32],
                                in_=bk[3][:, :].rearrange("p (h e) -> p h e", e=32),
                                func=AF.Relu).then_inc(s_act, 1)
                        for o in range(DC):
                            scalar.wait_ge(s_pe, pe_q(s, o))
                            for gq in range(4):
                                aa = scalar.activation(
                                    out=q_fm[:, o, gq * 512:(gq + 1) * 512],
                                    in_=kv_ps[gq], func=AF.Relu)
                            aa.then_inc(s_act, 1)
                        for gq in range(4):
                            scalar.wait_ge(s_pe, pe_B(s, gq, 1))
                            scalar.activation(out=den_sb[:], in_=den_ps,
                                              func=AF.Identity,
                                              bias=eps_sb[0:32, 1:2]).then_inc(s_act, 1)
                            for c in range(DC):
                                scalar.wait_ge(s_pe, pe_B(s, gq, 1 + (c + 1)))
                                scalar.copy(out=rep_sb[:, c % 2, :],
                                            in_=rep_ps[c % 2]).then_inc(s_act, 1)
                            for og in range(DC):
                                scalar.wait_ge(s_pe, pe_B(s, gq, 9 + (og + 1)))
                                scalar.activation(out=y_fm[:, og, :],
                                                  in_=y_ps[og % 2], func=AF.Identity,
                                                  bias=pbias_sb[:, og:og + 1]
                                                  ).then_inc(s_act, 1)
                            for tt in range(4):
                                g = 16 * s + 4 * gq + tt
                                slot = 2 + g % 2
                                scalar.wait_ge(s_dve, dve_ep(g, 6))
                                scalar.activation(out=sta[:, slot, 3:4],
                                                  in_=sta[:, slot, 2:3], func=AF.Sqrt,
                                                  scale=INV_D,
                                                  bias=eps_sb[:, 0:1]).then_inc(s_act, 1)
                                scalar.wait_ge(s_dve, dve_ep(g, 8))
                                scalar.activation(out=ytok[:],
                                                  in_=ytok[:],
                                                  func=AF.Identity,
                                                  scale=sta[:, slot, 4:5],
                                                  bias=sta[:, slot, 5:6]).then_inc(s_act, 1)
                                if emit_ffn:
                                    scalar.wait_ge(s_dve, dve_ep(g, 9))
                                    scalar.dma_start(
                                        out=xmid_d[g * 128:(g + 1) * 128, :],
                                        in_=xm2[:, g % 2, :]).then_inc(s_st[g % 2], 16)
                                scalar.wait_ge(s_dve, dve_ep(g, 14))
                                scalar.activation(out=sta[:, slot, 3:4],
                                                  in_=sta[:, slot, 2:3], func=AF.Sqrt,
                                                  scale=INV_D,
                                                  bias=eps_sb[:, 0:1]).then_inc(s_act, 1)
                                scalar.wait_ge(s_dve, dve_ep(g, 16))
                                if g >= 2:
                                    scalar.wait_ge(s_pe, pe_ep(g - 2, 2))
                                    if emit_ffn:
                                        scalar.wait_ge(s_hst[g % 2], 16 * (g // 2))
                                scalar.activation(out=hbf[:, g % 2, :],
                                                  in_=xm2[:, g % 2, :],
                                                  func=AF.Identity,
                                                  scale=ln3a[:, g:g + 1],
                                                  bias=ln3b[:, g:g + 1]).then_inc(s_act, 1)
                                if emit_ffn:
                                    scalar.wait_ge(s_act, act_ep(g, 4))
                                    scalar.dma_start(
                                        out=hrows_d[g * 128:(g + 1) * 128, :],
                                        in_=hbf[:, g % 2, :]).then_inc(s_hst[g % 2], 16)
                                scalar.wait_ge(s_pe, pe_ep(g, 2))
                                scalar.copy(out=lnx_t[:, g % 2, :],
                                            in_=tp_ps[g % 2]).then_inc(s_act, 1)
                    scalar.nop().then_inc(s_fin, 1)

                @blk.tensor
                def _(tensor):
                    tensor.wait_ge(s_wc, 16 * NWC)
                    tensor.wait_ge(s_w0, 16 * NW0)
                    tensor.wait_ge(s_ini, 1)
                    for s in range(NSEQ):
                        if s > 0:
                            tensor.wait_ge(s_dve, s * DVE_SEQ)
                            tensor.wait_ge(s_act, s * ACT_SEQ)
                        for t in range(16):
                            g = 16 * s + t
                            tensor.wait_ge(s_act, act_A(g, 2))
                            if g >= 2:
                                tensor.wait_ge(s_act, act_A(g - 2, 3))
                            for c in range(DC):
                                mm = tensor.transpose(
                                    tp_ps[g % 2][:, c * 128:(c + 1) * 128],
                                    lnx_t[:, g % 2, c * 128:(c + 1) * 128],
                                    idB[:])
                            mm.then_inc(s_pe, 1)
                            tensor.wait_ge(s_act, act_A(g, 3))
                            if t > 0:
                                tensor.wait_ge(s_act, act_A(g - 1, 4))
                            for c in range(DC):
                                tensor.ldweights(lnx_fm[:, c, t * 128:(t + 1) * 128])
                                for og in range(4):
                                    mm = tensor.matmul(
                                        kv_ps[og],
                                        lnx_fm[:, c, t * 128:(t + 1) * 128],
                                        wqkv_sb[:, c, D + og * 512:D + (og + 1) * 512],
                                        start=(c == 0), stop=(c == DC - 1))
                            mm.then_inc(s_pe, 1)
                            tensor.wait_ge(s_act, act_A(g, 4))
                            if t == 0 and s > 0:
                                tensor.wait_ge(s_dve, dve_vk(s - 1))
                            for h in range(H):
                                c, j = h // 4, h % 4
                                mm = tensor.matmul(vk_ps[32 * j:32 * j + 32,
                                                         33 * c:33 * c + 33],
                                                   k_tok[:, 32 * h:32 * h + 32],
                                                   vpad[:, 33 * h:33 * h + 33],
                                                   start=(t == 0 and c == 0),
                                                   stop=(t == 15 and c == DC - 1),
                                                   skip_group_check=True,
                                                   tile_position=(0, 32 * j))
                            mm.then_inc(s_pe, 1)
                        for o in range(DC):
                            if o >= 1:
                                tensor.wait_ge(s_act, act_q(s, o - 1))
                            else:
                                tensor.wait_ge(s_act, act_A(16 * s + 15, 4))
                            for c in range(DC):
                                tensor.ldweights(wqkv_sb[:, c, o * 128:(o + 1) * 128])
                                for gq in range(4):
                                    mm = tensor.matmul(
                                        kv_ps[gq],
                                        wqkv_sb[:, c, o * 128:(o + 1) * 128],
                                        lnx_fm[:, c, gq * 512:(gq + 1) * 512],
                                        start=(c == 0), stop=(c == DC - 1))
                            mm.then_inc(s_pe, 1)
                        for gq in range(4):
                            tensor.wait_ge(s_dve, dve_vk(s))
                            tensor.wait_ge(s_act, act_q(s, 7))
                            if gq > 0:
                                tensor.wait_ge(s_dve, dve_ep(16 * s + 4 * gq - 1, 17))
                            for c in range(DC):
                                tensor.ldweights(dstat[:, c, :])
                                mm = tensor.matmul(den_ps, dstat[:, c, :],
                                                   q_fm[:, c, gq * 512:(gq + 1) * 512],
                                                   start=(c == 0), stop=(c == DC - 1))
                            mm.then_inc(s_pe, 1)
                            for c in range(DC):
                                tensor.wait_ge(s_dve, dve_B(s, gq, 1))
                                if c >= 2:
                                    tensor.wait_ge(s_act, act_B(s, gq, c))
                                    tensor.wait_ge(s_dve, dve_B(s, gq, 0) + c)
                                tensor.ldweights(exp32_sb[:, c * 128:(c + 1) * 128])
                                tensor.matmul(rep_ps[c % 2],
                                              exp32_sb[:, c * 128:(c + 1) * 128],
                                              den2_sb[:], start=True, stop=True)
                                tensor.ldweights(nstat[:, c, :])
                                tensor.matmul(num_ps[c % 2], nstat[:, c, :],
                                              q_fm[:, c, gq * 512:(gq + 1) * 512],
                                              start=True, stop=True
                                              ).then_inc(s_pe, 1)
                            for og in range(DC):
                                tensor.wait_ge(s_dve, dve_B(s, gq, 9))
                                if og >= 2:
                                    tensor.wait_ge(s_act, act_B(s, gq, 9 + og - 1))
                                for c in range(DC):
                                    tensor.ldweights(wproj_sb[:, c,
                                                              og * 128:(og + 1) * 128])
                                    mm = tensor.matmul(
                                        y_ps[og % 2],
                                        wproj_sb[:, c, og * 128:(og + 1) * 128],
                                        proj_sb[:, c, :],
                                        start=(c == 0), stop=(c == DC - 1))
                                mm.then_inc(s_pe, 1)
                            for tt in range(4):
                                g = 16 * s + 4 * gq + tt
                                tensor.wait_ge(s_act, act_B(s, gq, 17))
                                if g >= 2:
                                    tensor.wait_ge(s_act, act_ep(g - 2, 5))
                                for c in range(DC):
                                    mm = tensor.transpose(
                                        tp_ps[g % 2][:, c * 128:(c + 1) * 128],
                                        y_fm[:, c, tt * 128:(tt + 1) * 128], idB[:])
                                mm.then_inc(s_pe, 1)
                                tensor.wait_ge(s_dve, dve_ep(g, 1))
                                tensor.wait_ge(s_act, act_ep(g, 4))
                                for c in range(DC):
                                    mm = tensor.transpose(
                                        tp_ps[g % 2][:, c * 128:(c + 1) * 128],
                                        hbf[:, g % 2, c * 128:(c + 1) * 128], idB[:])
                                mm.then_inc(s_pe, 1)
                                tensor.wait_ge(s_act, act_ep(g, 5))
                                if g >= 1:
                                    tensor.wait_ge(s_dve, dve_ep(g - 1, 17))
                                for c in range(DC):
                                    tensor.ldweights(
                                        lnx_t[:, g % 2, c * 128:(c + 1) * 128])
                                    mm = tensor.matmul(
                                        lgt_ps,
                                        lnx_t[:, g % 2, c * 128:(c + 1) * 128],
                                        rw_sb[:, c, :],
                                        start=(c == 0), stop=(c == DC - 1))
                                mm.then_inc(s_pe, 1)
                    tensor.nop().then_inc(s_fin, 1)

        with ExitStack() as st2:
            sb2 = lambda n, sh, dt: st2.enter_context(nc.sbuf_tensor(n, sh, dt))
            tmpw = sb2("tmpw", [128, NT, E], F32)
            idx1f = sb2("idx1f", [128, NT], F32)
            idx2f = sb2("idx2f", [128, NT], F32)
            ct_ps = bk[3][0:1, 0:16]
            if emit_ffn:
                oh1 = sb2("oh1", [128, NT, E], F32)
                expw = sb2("expw", [128, NT, E], F32)
                e2w = sb2("e2w", [128, NT], F32)
                zw = sb2("zw", [128, NT], F32)
                v1w = sb2("v1w", [128, NT], F32)
                v2w = sb2("v2w", [128, NT], F32)
                r1w = sb2("r1w", [128, NT], F32)
                r2w = sb2("r2w", [128, NT], F32)
                base_sb = sb2("base_sb", [1, E], F32)
                cbase_rep = sb2("cbase_rep", [128, E], F32)
                carry_sb = sb2("carry_sb", [1, E], F32)
                Bw = sb2("Bw", [128, NT, E], F32)
                cTi = sb2("cTi", [16, 1], I32)
                padTi = sb2("padTi", [16, 1], I32)
                padT = sb2("padT", [16, 1], F32)
                poffT = sb2("poffT", [16, 1], F32)
                poffTi = sb2("poffTi", [16, 1], I32)
                pad_row = sb2("pad_row", [1, E], F32)
                poff_row = sb2("poff_row", [1, E], F32)
                poff_rep = sb2("poff_rep", [128, E], F32)
                d1i = sb2("d1i", [128, NT], I32)
                d2i = sb2("d2i", [128, NT], I32)
                dti = sb2("dti", [128, NT], I32)
                zrow = sb2("zrow", [16, PML // 16], I16)
                B_ps = [bk[4][:, 0:16], bk[5][:, 0:16]]
                base_ps = bk[1][0:1, 0:16]
                brep_ps = bk[2][:, 0:16]
                poff_ps = bk[1][0:16, 64:65]
                prow_ps = bk[1][0:1, 96:112]
                prep_ps = bk[2][:, 128:144]

            with nc.Block() as blk2:
                @blk2.vector
                def _(vector):
                    vector.drain()
                    for g in range(NT):
                        vector.max(l8[:, g, :], logits_w[:, g, :])
                        vector.drain()
                        vector.max_index(i8[:, g, :], l8[:, g, :],
                                         logits_w[:, g, :])
                    vector.drain()
                    vector.tensor_copy(idx1f[:], i8[:, :, 0])
                    vector.tensor_copy(idx2f[:], i8[:, :, 1])
                    vector.drain()
                    io_b = iota16[:].unsqueeze(1).broadcast_to([128, NT, E])
                    i1_b = idx1f[:].unsqueeze(2).broadcast_to([128, NT, E])
                    i2_b = idx2f[:].unsqueeze(2).broadcast_to([128, NT, E])
                    if emit_ffn:
                        vector.tensor_tensor(oh1[:], io_b, i1_b, ALU.is_equal)
                        vector.tensor_tensor(tmpw[:], io_b, i2_b, ALU.is_equal)
                        vector.drain()
                        vector.tensor_add(oh12[:], oh1[:], tmpw[:])
                    else:
                        vector.tensor_tensor(oh12[:], io_b, i1_b, ALU.is_equal)
                        vector.tensor_tensor(tmpw[:], io_b, i2_b, ALU.is_equal)
                        vector.drain()
                        vector.tensor_add(oh12[:], oh12[:], tmpw[:])
                    vector.drain().then_inc(s_rtd, 1)
                    if not emit_ffn:
                        vector.memset(run_sb[:], 0.0)
                        vector.drain()
                        for g in range(NT):
                            vector.wait_ge(s_rt, g + 1)
                            vector.tensor_add(run_sb[:], run_sb[:], ct_ps)
                            vector.drain().then_inc(s_rtd, 1)
                        vector.tensor_copy(cnt_sb[:], run_sb[:]).then_inc(s_rtd, 1)
                        vector.nop().then_inc(s_fin, 1)
                        return
                    vector.wait_ge(s_rt, RT("base"))
                    vector.tensor_copy(base_sb[:], base_ps).then_inc(s_rtd, 1)
                    vector.wait_ge(s_rt, RT("brep"))
                    vector.tensor_copy(cbase_rep[:], brep_ps).then_inc(s_rtd, 1)
                    vector.memset(run_sb[:], 0.0)
                    vector.drain()
                    for g in range(NT):
                        vector.tensor_add(carry_sb[:], base_sb[:], run_sb[:])
                        vector.drain().then_inc(s_rtd, 1)
                        vector.wait_ge(s_rt, RT("B", g))
                        vector.tensor_copy(Bw[:, g, :], B_ps[g % 2]).then_inc(s_rtd, 1)
                        vector.wait_ge(s_rt, RT("ct", g))
                        vector.tensor_add(run_sb[:], run_sb[:], ct_ps)
                        vector.drain().then_inc(s_rtd, 1)
                    vector.tensor_mul(tmpw[:], oh1[:], Bw[:])
                    vector.drain()
                    vector.reduce_sum(r1w[:], tmpw[:], axis=AX.X)
                    vector.drain()
                    vector.tensor_sub(tmpw[:], oh12[:], oh1[:])
                    vector.drain()
                    vector.tensor_mul(expw[:], tmpw[:], Bw[:])
                    vector.drain()
                    vector.reduce_sum(r2w[:], expw[:], axis=AX.X)
                    vector.tensor_sub(e2w[:], l8[:, :, 1], l8[:, :, 0])
                    vector.drain().then_inc(s_rtd, 1)
                    l1_b = l8[:, :, 0:1].broadcast_to([128, NT, E])
                    vector.tensor_sub(expw[:], logits_w[:], l1_b)
                    vector.drain().then_inc(s_rtd, 1)
                    vector.wait_ge(s_rta, 2)
                    vector.reduce_sum(zw[:], expw[:], axis=AX.X)
                    vector.drain()
                    vector.scalar_tensor_tensor(zw[:], zw[:], 1e-9, e2w[:],
                                                ALU.mult, ALU.add)
                    vector.drain()
                    vector.tensor_scalar_add(zw[:], zw[:], 1.0)
                    vector.drain()
                    vector.reciprocal(v1w[:], zw[:])
                    vector.drain()
                    vector.tensor_mul(v2w[:], e2w[:], v1w[:])
                    vector.tensor_single_scalar(g1w[:], r1w[:], CAP, ALU.is_lt)
                    vector.tensor_single_scalar(g2w[:], r2w[:], CAP, ALU.is_lt)
                    vector.drain()
                    vector.tensor_mul(g1w[:], g1w[:], v1w[:])
                    vector.tensor_mul(g2w[:], g2w[:], v2w[:])
                    vector.tensor_copy(cTi[:], cntT[:])
                    vector.drain()
                    vector.tensor_scalar_add(padTi[:], cTi[:], M2 - 1)
                    vector.drain()
                    vector.tensor_single_scalar(padTi[:], padTi[:], 9,
                                                ALU.logical_shift_right)
                    vector.drain()
                    vector.tensor_copy(nwin_i[:], padTi[:])
                    vector.drain()
                    vector.tensor_single_scalar(padTi[:], padTi[:], 9,
                                                ALU.logical_shift_left)
                    vector.drain()
                    vector.tensor_copy(padT[:], padTi[:])
                    vector.drain().then_inc(s_rtd, 1)
                    vector.wait_ge(s_rt, RT("poffmm"))
                    vector.tensor_copy(poffT[:], poff_ps)
                    vector.drain()
                    vector.tensor_copy(poffTi[:], poffT[:])
                    vector.drain()
                    vector.tensor_single_scalar(woff_i[:], poffTi[:], 9,
                                                ALU.logical_shift_right)
                    vector.drain().then_inc(s_rtd, 1)
                    vector.wait_ge(s_rt, RT("padrtr"))
                    vector.tensor_copy(pad_row[:], prow_ps)
                    vector.drain().then_inc(s_rtd, 1)
                    vector.wait_ge(s_rt, RT("pofrtr"))
                    vector.tensor_copy(poff_row[:], prow_ps)
                    vector.drain().then_inc(s_rtd, 1)
                    vector.wait_ge(s_rt, RT("prep"))
                    vector.tensor_copy(poff_rep[:], prep_ps)
                    vector.tensor_add(pad_row[0:1, 0:1], poff_row[0:1, 15:16],
                                      pad_row[0:1, 15:16])
                    vector.drain()
                    vector.tensor_copy(nptot_i[0:1, 0:1], pad_row[0:1, 0:1])
                    vector.drain()
                    vector.tensor_single_scalar(nptot_i[0:1, 0:1], nptot_i[0:1, 0:1],
                                                9, ALU.logical_shift_right)
                    pr_b = poff_rep[:].unsqueeze(1).broadcast_to([128, NT, E])
                    cb_b = cbase_rep[:].unsqueeze(1).broadcast_to([128, NT, E])
                    vector.tensor_mul(tmpw[:], oh1[:], cb_b)
                    vector.drain()
                    vector.reduce_sum(v1w[:], tmpw[:], axis=AX.X)
                    vector.drain()
                    vector.tensor_sub(r1w[:], r1w[:], v1w[:])
                    vector.tensor_mul(tmpw[:], oh1[:], pr_b)
                    vector.drain()
                    vector.reduce_sum(v1w[:], tmpw[:], axis=AX.X)
                    vector.drain()
                    vector.tensor_add(r1w[:], r1w[:], v1w[:])
                    vector.drain()
                    vector.tensor_copy(p1i[:], r1w[:])
                    vector.tensor_sub(tmpw[:], oh12[:], oh1[:])
                    vector.drain()
                    vector.tensor_mul(expw[:], tmpw[:], cb_b)
                    vector.drain()
                    vector.reduce_sum(v2w[:], expw[:], axis=AX.X)
                    vector.drain()
                    vector.tensor_sub(r2w[:], r2w[:], v2w[:])
                    vector.tensor_mul(expw[:], tmpw[:], pr_b)
                    vector.drain()
                    vector.reduce_sum(v2w[:], expw[:], axis=AX.X)
                    vector.drain()
                    vector.tensor_add(r2w[:], r2w[:], v2w[:])
                    vector.drain()
                    vector.tensor_copy(p2i[:], r2w[:])
                    for pi, ddst in ((p1i, d1i), (p2i, d2i)):
                        vector.drain()
                        vector.tensor_single_scalar(dti[:], pi[:], 15, ALU.bitwise_and)
                        vector.drain()
                        vector.tensor_single_scalar(dti[:], dti[:], PML // 16, ALU.mult)
                        vector.tensor_single_scalar(ddst[:], pi[:], 4,
                                                    ALU.logical_shift_right)
                        vector.drain()
                        vector.tensor_add(ddst[:], ddst[:], dti[:])
                    vector.memset(zrow[:], 0)
                    vector.drain().then_inc(s_rtd, 1)
                    vector.nop().then_inc(s_fin, 1)

                @blk2.scalar
                def _(scalar):
                    if emit_ffn:
                        scalar.wait_ge(s_rtd, RTD("e2"))
                        scalar.activation(out=e2w[:], in_=e2w[:], func=AF.Exp) \
                            .then_inc(s_rta, 1)
                        scalar.wait_ge(s_rtd, RTD("expw"))
                        scalar.activation(out=expw[:].rearrange("p t e -> p (t e)"),
                                          in_=expw[:].rearrange("p t e -> p (t e)"),
                                          func=AF.Exp).then_inc(s_rta, 1)
                    else:
                        scalar.wait_ge(s_rtd, NT + 2)
                        scalar.dma_start(out=counts_out[:, :],
                                         in_=cnt_sb[:]).then_inc(s_st[0], 16)
                        scalar.wait_ge(s_st[0], 16)
                    scalar.nop().then_inc(s_fin, 1)

                @blk2.tensor
                def _(tensor):
                    tensor.wait_ge(s_rtd, RTD("oh12"))
                    if not emit_ffn:
                        for g in range(NT):
                            if g >= 1:
                                tensor.wait_ge(s_rtd, 1 + g)
                            tensor.matmul(ct_ps, ones[:, 0:1], oh12[:, g, :],
                                          start=True, stop=True
                                          ).then_inc(s_rt, 1)
                        tensor.nop().then_inc(s_fin, 1)
                        return
                    tensor.matmul(base_ps, cmask_sb[:], cntall_sb[:],
                                  start=True, stop=True).then_inc(s_rt, 1)
                    tensor.wait_ge(s_rtd, RTD("base"))
                    tensor.matmul(brep_ps, ones[0:1, :], base_sb[:],
                                  start=True, stop=True).then_inc(s_rt, 1)
                    for g in range(NT):
                        tensor.wait_ge(s_rtd, RTD("carry", g))
                        if g >= 2:
                            tensor.wait_ge(s_rtd, RTD("bev", g - 2))
                        tensor.matmul(B_ps[g % 2], strictL[:], oh12[:, g, :],
                                      start=True, stop=False)
                        tensor.matmul(B_ps[g % 2], ones[0:1, :], carry_sb[:],
                                      start=False, stop=True).then_inc(s_rt, 1)
                        if g >= 1:
                            tensor.wait_ge(s_rtd, RTD("runadd", g - 1))
                        tensor.matmul(ct_ps, ones[:, 0:1], oh12[:, g, :],
                                      start=True, stop=True).then_inc(s_rt, 1)
                    tensor.wait_ge(s_rtd, RTD("padt"))
                    tensor.matmul(poff_ps, strictL[0:16, 0:16], padT[:],
                                  start=True, stop=True).then_inc(s_rt, 1)
                    tensor.wait_ge(s_rtd, RTD("pofft"))
                    tensor.matmul(prow_ps, padT[:], idF[0:16, 0:16],
                                  start=True, stop=True).then_inc(s_rt, 1)
                    tensor.wait_ge(s_rtd, RTD("padr"))
                    tensor.matmul(prow_ps, poffT[:], idF[0:16, 0:16],
                                  start=True, stop=True).then_inc(s_rt, 1)
                    tensor.wait_ge(s_rtd, RTD("pofr"))
                    tensor.matmul(prep_ps, ones[0:1, :], poff_row[:],
                                  start=True, stop=True).then_inc(s_rt, 1)
                    tensor.nop().then_inc(s_fin, 1)

                @blk2.gpsimd
                def _(gpsimd):
                    if emit_ffn:
                        gpsimd.load_library(libcfg.mlp)
                        gpsimd.wait_ge(s_rtd, RTD("scat"))
                        gpsimd.dma_start(out=plist_d[:, :],
                                         in_=zrow[:]).then_inc(s_pl, 16)
                        gpsimd.wait_ge(s_pl, 16)
                        for g in range(NT):
                            for ddst in (d1i, d2i):
                                gpsimd.indirect_dma_start(
                                    out=plist_d.rearrange("a b -> (a b)").unsqueeze(1),
                                    out_offset=bass.IndirectOffsetOnAxis(
                                        ddst[:, g:g + 1], 0),
                                    in_=ctid_sb[:, g:g + 1], in_offset=None,
                                ).then_inc(s_pl, 16)
                        gpsimd.wait_ge(s_pl, 16 * (1 + 2 * NT))
                        for j in range(8):
                            gpsimd.dma_start(out=plist_sb[16 * j:16 * j + 16, :],
                                             in_=plist_d[:, :]).then_inc(s_pl, 16)
                        gpsimd.wait_ge(s_pl, 16 * (9 + 2 * NT))
                    gpsimd.nop().then_inc(s_fin, 1)

        if emit_ffn:
            with ExitStack() as st3:
                sb3 = lambda n, sh, dt: st3.enter_context(nc.sbuf_tensor(n, sh, dt))
                wgr = sb3("wgr", [128, 2, DC, F], BF16)
                wur = sb3("wur", [128, 2, DC, F], BF16)
                wdr = sb3("wdr", [128, 2, FC, D], BF16)
                gath = sb3("gath", [128, DC, M2], BF16)
                gu_sb = sb3("gu_sb", [128, FC, M2], BF16)
                dout_sb = sb3("dout_sb", [128, DC, M2], BF16)
                otok = sb3("otok", [128, 4, D], BF16)
                g_ps = [bk[0][:, :], bk[1][:, :]]
                u_ps = [bk[2][:, :], bk[3][:, :]]
                d_ps = [bk[4][:, :], bk[5][:, :]]
                tb_ps = [poolB[:, 0:1024], poolB[:, 1024:2048]]

                class ConstW:
                    def __init__(self, eng):
                        self.eng = eng

                    def begin(self, Pg):
                        self.Pg = Pg

                    def wait(self, sem2, base_stride, off):
                        self.eng.wait_ge(sem2, base_stride * self.Pg + off)

                    def end(self):
                        pass

                class ChainW:
                    def __init__(self, eng, first_pair, cfg):
                        self.eng = eng
                        self.regs = {}
                        self.stride = {}
                        self.first_off = {}
                        self.cur = {}
                        for i, (sem2, stride, foff) in enumerate(cfg):
                            key = id(sem2)
                            r = eng.alloc_register(f"cw{i}_{eng.engine}")
                            eng.reg_mov(r, stride * first_pair + foff)
                            self.regs[key] = r
                            self.stride[key] = stride
                            self.first_off[key] = foff
                            self.cur[key] = foff

                    def begin(self, Pg=None):
                        pass

                    def wait(self, sem2, stride, off):
                        key = id(sem2)
                        r = self.regs[key]
                        d = off - self.cur[key]
                        if d:
                            self.eng.reg_add(r, r, d)
                            self.cur[key] = off
                        self.eng.wait_ge(sem2, r)

                    def end(self):
                        for key, r in self.regs.items():
                            d = self.stride[key] - self.cur[key] + self.first_off[key]
                            if d:
                                self.eng.reg_add(r, r, d)
                            self.cur[key] = self.first_off[key]

                def pair_pe(tensor, slot, W):
                    W.wait(s_gat, 16, 16)
                    mv = gath[:, :, :]
                    for f in range(FC):
                        if f >= 2:
                            W.wait(s_ma, MA, f - 1)
                        for c in range(DC):
                            tensor.ldweights(wgr[:, slot, c, f * 128:(f + 1) * 128])
                            mm = tensor.matmul(g_ps[f % 2],
                                               wgr[:, slot, c, f * 128:(f + 1) * 128],
                                               mv[:, c, :], start=(c == 0),
                                               stop=(c == DC - 1))
                        mm.then_inc(s_mp, 1)
                        if f >= 2:
                            W.wait(s_md, MD, f - 1)
                        for c in range(DC):
                            tensor.ldweights(wur[:, slot, c, f * 128:(f + 1) * 128])
                            mm = tensor.matmul(u_ps[f % 2],
                                               wur[:, slot, c, f * 128:(f + 1) * 128],
                                               mv[:, c, :], start=(c == 0),
                                               stop=(c == DC - 1))
                        mm.then_inc(s_mp, 1)
                    W.wait(s_md, MD, FC)
                    for c in range(DC):
                        if c >= 2:
                            W.wait(s_ma, MA, FC + c - 1)
                        for f in range(FC):
                            tensor.ldweights(wdr[:, slot, f, c * 128:(c + 1) * 128])
                            mm = tensor.matmul(d_ps[c % 2],
                                               wdr[:, slot, f, c * 128:(c + 1) * 128],
                                               gu_sb[:, f, :], start=(f == 0),
                                               stop=(f == FC - 1))
                            if c == 0 and f == 0:
                                mm.then_inc(s_gfree, 1)
                        mm.then_inc(s_mp, 1)
                    W.wait(s_ma, MA, FC + DC)
                    W.wait(s_ost, 16, 0)
                    for tt in range(4):
                        if tt >= 2:
                            W.wait(s_md, MD, FC + tt - 1)
                        for c in range(DC):
                            mm = tensor.transpose(
                                tb_ps[tt % 2][:, c * 128:(c + 1) * 128],
                                dout_sb[:, c, tt * 128:(tt + 1) * 128], idB[:])
                        mm.then_inc(s_mp, 1)

                def pair_act(scalar, W):
                    for f in range(FC):
                        W.wait(s_mp, MP_PE, 2 * f + 1)
                        scalar.activation(out=gu_sb[:, f, :], in_=g_ps[f % 2],
                                          func=AF.Silu).then_inc(s_ma, 1)
                    for c in range(DC):
                        W.wait(s_mp, MP_PE, 2 * FC + c + 1)
                        scalar.copy(out=dout_sb[:, c, :],
                                    in_=d_ps[c % 2]).then_inc(s_ma, 1)

                def pair_dve(vector, W):
                    for f in range(FC):
                        W.wait(s_mp, MP_PE, 2 * f + 2)
                        W.wait(s_ma, MA, f + 1)
                        vector.tensor_mul(gu_sb[:, f, :], gu_sb[:, f, :],
                                          u_ps[f % 2]).then_inc(s_md, 1)
                    for tt in range(4):
                        W.wait(s_mp, MP_PE, 2 * FC + DC + tt + 1)
                        vector.tensor_copy(otok[:, tt, :],
                                           tb_ps[tt % 2]).then_inc(s_md, 1)

                from contextlib import contextmanager
                from concourse.ordered_engine_set import OrderedEngineSet

                @contextmanager
                def reg_loop(eng, r_t, r_end):
                    lid = nc.next_id()
                    ls, le = f"rl{lid}_s", f"rl{lid}_e"
                    engs = OrderedEngineSet([eng.engine])
                    eng.reg_mov(r_t, 0)
                    eng.br(ls)
                    with nc.body(ls, valid_engines=engs):
                        yield
                        eng.reg_add(r_t, r_t, 1)
                        eng.br_lt(r_t, r_end, on_true=ls, on_false=le)
                    nc.switch_bb(le)

                def store_ap(P):
                    if P < NS2:
                        return shout_d[P * M2:(P + 1) * M2, :].rearrange(
                            "(tt p) d2 -> p tt d2", p=128)
                    p = P - NS2
                    return packed_d[p * M2:(p + 1) * M2, :].rearrange(
                        "(tt p) d2 -> p tt d2", p=128)

                with nc.Block() as blk3:
                    @blk3.gpsimd
                    def _(gpsimd):
                        gpsimd.wait_ge(s_pl, 16 * (9 + 2 * NT))
                        npreg = gpsimd.to_reg(gpsimd.value_load(
                            nptot_i[0:1, 0:1], min_val=0, max_val=NPMAX))
                        r_m2 = gpsimd.alloc_register("gat_n")
                        gpsimd.reg_mov(r_m2, M2)
                        NP_ALL = NS2 + NPMAX
                        for P in range(NP_ALL):
                            live = None
                            if P < NS2:
                                if P >= 1:
                                    gpsimd.wait_ge(s_gfree, P)
                                gpsimd.wait_ge(s_hst[0], 16 * (2 * P + 2))
                                gpsimd.wait_ge(s_hst[1], 16 * (2 * P + 2))
                                gpsimd.dma_gather(gath[:, :, :], hrows_d[:, :],
                                                  shidx_sb[:, 32 * P:32 * (P + 1)],
                                                  M2, r_m2, D, elem_step=D,
                                                  transpose=True).then_inc(s_gat, 16)
                            else:
                                p = P - NS2
                                with gpsimd.If_cmp(npreg, p, "IS_GT"):
                                    gpsimd.wait_ge(s_gfree, P)
                                    gpsimd.dma_gather(
                                        gath[:, :, :], hrows_d[:, :],
                                        plist_sb[:, 32 * p:32 * (p + 1)],
                                        M2, r_m2, D, elem_step=D,
                                        transpose=True).then_inc(s_gat, 16)
                                with gpsimd.Else():
                                    gpsimd.wait_ge(s_gat, 16 * P)
                                    gpsimd.inc_swdge_sem([s_gat], [16])
                            if P >= 1:
                                Q = P - 1
                                if Q < NS2:
                                    gpsimd.wait_ge(s_md, MD * (Q + 1))
                                    gpsimd.dma_start(out=store_ap(Q),
                                                     in_=otok[:, :, :]
                                                     ).then_inc(s_ost, 16)
                                else:
                                    with gpsimd.If_cmp(npreg, Q - NS2, "IS_GT"):
                                        gpsimd.wait_ge(s_md, MD * (Q + 1))
                                        gpsimd.dma_start(out=store_ap(Q),
                                                         in_=otok[:, :, :]
                                                         ).then_inc(s_ost, 16)
                                    with gpsimd.Else():
                                        gpsimd.wait_ge(s_ost, 16 * Q)
                                        gpsimd.inc_swdge_sem([s_ost], [16])
                        Q = NP_ALL - 1
                        with gpsimd.If_cmp(npreg, Q - NS2, "IS_GT"):
                            gpsimd.wait_ge(s_md, MD * (Q + 1))
                            gpsimd.dma_start(out=store_ap(Q), in_=otok[:, :, :]
                                             ).then_inc(s_ost, 16)
                        with gpsimd.Else():
                            gpsimd.wait_ge(s_ost, 16 * Q)
                            gpsimd.inc_swdge_sem([s_ost], [16])
                        gpsimd.nop().then_inc(s_fin, 1)

                    @blk3.sync
                    def _(sync):
                        sync.wait_ge(s_pl, 16 * (9 + 2 * NT))
                        sync.wait_ge(s_wcS, 16 * 3)
                        for wsrc, wdst in ((swgbf, wgr), (swubf, wur)):
                            sync.dma_start(
                                out=wdst[:, 0, :, :],
                                in_=wsrc.rearrange("(c p) f -> p c f", p=128)
                            ).then_inc(s_wl[0], 16)
                        sync.dma_start(
                            out=wdr[:, 0, :, :],
                            in_=swdbf.rearrange("(c p) f -> p c f", p=128)
                        ).then_inc(s_wl[0], 16)
                        for e in range(E):
                            slot = (e + 1) % 2
                            sync.wait_ge(s_wcE[e % 2], 48 * (e // 2 + 1))
                            if e >= 1:
                                sync.wait_ge(s_wfree, e)
                            for wsrc, wdst in ((wgbf, wgr), (wubf, wur)):
                                sync.dma_start(
                                    out=wdst[:, slot, :, :],
                                    in_=wsrc[e].rearrange("(c p) f -> p c f", p=128)
                                ).then_inc(s_wl[slot], 16)
                            sync.dma_start(
                                out=wdr[:, slot, :, :],
                                in_=wdbf[e].rearrange("(c p) f -> p c f", p=128)
                            ).then_inc(s_wl[slot], 16)

                    @blk3.tensor
                    def _(tensor):
                        tensor.wait_ge(s_wl[0], 16 * 3)
                        Wc = ConstW(tensor)
                        for k in range(NS2):
                            Wc.begin(k)
                            pair_pe(tensor, 0, Wc)
                            Wc.end()
                        tensor.drain().then_inc(s_wfree, 1)
                        Wx = ChainW(tensor, NS2, [
                            (s_gat, 16, 16), (s_ma, MA, 1), (s_md, MD, 1),
                            (s_ost, 16, 0)])
                        r_t = tensor.alloc_register("pe_rt")
                        r_nw = tensor.alloc_register("pe_rnw")
                        for e in range(E):
                            slot = (e + 1) % 2
                            slot_fills = (1 + (e + 1) // 2) if slot == 0 else (e // 2 + 1)
                            tensor.wait_ge(s_wl[slot], 48 * slot_fills)
                            tensor.reg_load(r_nw, nwin_i[e:e + 1, 0:1])
                            with reg_loop(tensor, r_t, r_nw):
                                Wx.begin()
                                pair_pe(tensor, slot, Wx)
                                Wx.end()
                            tensor.drain().then_inc(s_wfree, 1)
                        tensor.nop().then_inc(s_fin, 1)

                    @blk3.scalar
                    def _(scalar):
                        Wc = ConstW(scalar)
                        for k in range(NS2):
                            Wc.begin(k)
                            pair_act(scalar, Wc)
                            Wc.end()
                        scalar.wait_ge(s_rtd, RTD("padt"))
                        Wx = ChainW(scalar, NS2, [(s_mp, MP_PE, 1)])
                        r_t = scalar.alloc_register("ac_rt")
                        r_nw = scalar.alloc_register("ac_rnw")
                        for e in range(E):
                            scalar.reg_load(r_nw, nwin_i[e:e + 1, 0:1])
                            with reg_loop(scalar, r_t, r_nw):
                                Wx.begin()
                                pair_act(scalar, Wx)
                                Wx.end()
                        scalar.nop().then_inc(s_fin, 1)

                    @blk3.vector
                    def _(vector):
                        Wc = ConstW(vector)
                        for k in range(NS2):
                            Wc.begin(k)
                            pair_dve(vector, Wc)
                            Wc.end()
                        Wx = ChainW(vector, NS2, [(s_mp, MP_PE, 2), (s_ma, MA, 1)])
                        r_t = vector.alloc_register("dv_rt")
                        r_nw = vector.alloc_register("dv_rnw")
                        for e in range(E):
                            vector.reg_load(r_nw, nwin_i[e:e + 1, 0:1])
                            with reg_loop(vector, r_t, r_nw):
                                Wx.begin()
                                pair_dve(vector, Wx)
                                Wx.end()
                        vector.nop().then_inc(s_fin, 1)

            with ExitStack() as st4:
                sb4 = lambda n, sh, dt: st4.enter_context(nc.sbuf_tensor(n, sh, dt))
                row1 = sb4("row1", [128, 2, D], BF16)
                row2 = sb4("row2", [128, 2, D], BF16)
                shr = sb4("shr", [128, 2, D], BF16)
                cacc = sb4("cacc", [128, 2, D], F32)
                ctmp = sb4("ctmp", [128, 2, D], F32)
                sta4 = sb4("sta4", [128, 2, 6], F32)
                ln4tmp = sb4("ln4tmp", [128, D], F32)

                with nc.Block() as blk4:
                    @blk4.sync
                    def _(sync):
                        for g in range(NT):
                            if g >= 1:
                                sync.wait_ge(s_cd, CMB_DVE * g)
                                sync.wait_ge(s_ca, CMB_ACT * g)
                            sync.wait_ge(s_ost, 16 * (g // 32 + 1))
                            sync.wait_ge(s_st[0], 16 * (NT // 2))
                            sync.wait_ge(s_st[1], 16 * (NT // 2))
                            sync.dma_start(out=shr[:, g % 2, :],
                                           in_=shout_d[g * 128:(g + 1) * 128, :]
                                           ).then_inc(s_cl, 16)
                            sync.dma_start(out=xm2[:, g % 2, :],
                                           in_=xmid_d[g * 128:(g + 1) * 128, :]
                                           ).then_inc(s_cl, 16)

                    @blk4.gpsimd
                    def _(gpsimd):
                        nptot = gpsimd.value_load(nptot_i[0:1, 0:1],
                                                  min_val=0, max_val=NPMAX)
                        gpsimd.wait_ge(s_ost, 16 * (NS2 + nptot))
                        for g in range(NT):
                            if g >= 1:
                                gpsimd.wait_ge(s_cd, CMB_DVE * (g - 1) + 2)
                            gpsimd.indirect_dma_start(
                                out=row1[:, g % 2, :], out_offset=None,
                                in_=packed_d[:, :],
                                in_offset=bass.IndirectOffsetOnAxis(p1i[:, g:g + 1], 0),
                            ).then_inc(s_cg, 16)
                            gpsimd.indirect_dma_start(
                                out=row2[:, g % 2, :], out_offset=None,
                                in_=packed_d[:, :],
                                in_offset=bass.IndirectOffsetOnAxis(p2i[:, g:g + 1], 0),
                            ).then_inc(s_cg, 16)
                        gpsimd.nop().then_inc(s_fin, 1)

                    @blk4.vector
                    def _(vector):
                        for g in range(NT):
                            vector.wait_ge(s_cg, 32 * (g + 1))
                            vector.wait_ge(s_cl, 32 * (g + 1))
                            vector.scalar_tensor_tensor(
                                cacc[:, g % 2, :], row1[:, g % 2, :], g1w[:, g:g + 1],
                                shr[:, g % 2, :], ALU.mult, ALU.add).then_inc(s_cd, 1)
                            vector.wait_ge(s_cd, CMB_DVE * g + 1)
                            vector.scalar_tensor_tensor(
                                cacc[:, g % 2, :], row2[:, g % 2, :], g2w[:, g:g + 1],
                                cacc[:, g % 2, :], ALU.mult, ALU.add).then_inc(s_cd, 1)
                            vector.wait_ge(s_ca, CMB_ACT * g + 1)
                            vector.wait_ge(s_cd, CMB_DVE * g + 2)
                            vector.tensor_add(cacc[:, g % 2, :], cacc[:, g % 2, :],
                                              ctmp[:, g % 2, :]).then_inc(s_cd, 1)
                            stw = sta4[:, g % 2, :]
                            ln_stats(vector, s_cd, cacc[:, g % 2, :], stw,
                                     CMB_DVE * g + 3, ln4tmp[:])
                            ln_finish(vector, s_cd, stw, stw[:, 4:5], stw[:, 5:6],
                                      CMB_DVE * g + 3, s_ca, CMB_ACT * g + 2)
                            vector.wait_ge(s_ca, CMB_ACT * g + 3)
                            vector.tensor_add(ctmp[:, g % 2, :], ctmp[:, g % 2, :],
                                              xm2[:, g % 2, :]).then_inc(s_cd, 1)
                        vector.nop().then_inc(s_fin, 1)

                    @blk4.scalar
                    def _(scalar):
                        for g in range(NT):
                            scalar.wait_ge(s_cl, 32 * (g + 1))
                            if g >= 2:
                                scalar.wait_ge(s_so, 16 * (g - 1))
                            scalar.activation(out=ctmp[:, g % 2, :],
                                              in_=xm2[:, g % 2, :], func=AF.Identity,
                                              scale=ln3a[:, g:g + 1],
                                              bias=ln3b[:, g:g + 1]).then_inc(s_ca, 1)
                            scalar.wait_ge(s_cd, CMB_DVE * g + 8)
                            scalar.activation(out=sta4[:, g % 2, 3:4],
                                              in_=sta4[:, g % 2, 2:3], func=AF.Sqrt,
                                              scale=INV_D,
                                              bias=eps_sb[:, 0:1]).then_inc(s_ca, 1)
                            scalar.wait_ge(s_cd, CMB_DVE * g + 10)
                            scalar.activation(out=ctmp[:, g % 2, :],
                                              in_=cacc[:, g % 2, :], func=AF.Identity,
                                              scale=sta4[:, g % 2, 4:5],
                                              bias=sta4[:, g % 2, 5:6]).then_inc(s_ca, 1)
                            scalar.wait_ge(s_cd, CMB_DVE * (g + 1))
                            if g >= 1:
                                scalar.wait_ge(s_so, 16 * g)
                            scalar.dma_start(out=out_d[g * 128:(g + 1) * 128, :],
                                             in_=ctmp[:, g % 2, :]).then_inc(s_so, 16)
                        scalar.wait_ge(s_so, 16 * NT)
                        scalar.nop().then_inc(s_fin, 1)

    mybir.codegen_inst_isa_subclasses(nc)
    return nc


def make_consts():
    c = {}
    c["cidB"] = np.eye(128, dtype=mld.bfloat16)
    c["cidF"] = np.eye(128, dtype=np.float32)
    c["cstrict"] = np.triu(np.ones((128, 128), np.float32), 1)
    c["cones"] = np.ones((128, 128), np.float32)
    c["ciota16"] = np.tile(np.arange(E, dtype=np.float32), (128, 1))
    e32 = np.zeros((32, DC * 128), mld.bfloat16)
    for cc in range(DC):
        for p in range(128):
            e32[4 * cc + p // 32, cc * 128 + p] = 1
    c["cexp32"] = e32
    c["ceps"] = np.tile(np.array([[EPS_LN, EPS_ATT]], np.float32), (128, 1))
    tid = np.zeros((128, NT), np.int16)
    for g in range(NT):
        tid[:, g] = g * 128 + np.arange(128)
    c["ctid"] = tid
    ident = np.arange(T, dtype=np.int16)
    c["cshidx"] = np.tile(ident.reshape(-1, 16).T, (8, 1))
    return c


_CONSTS = None


def core_inputs_a(inp, core):
    global _CONSTS
    if _CONSTS is None:
        _CONSTS = make_consts()
    c = _CONSTS
    return {
        "xin": np.ascontiguousarray(inp["L0_attn_x"][NSEQ * core:NSEQ * (core + 1)]
                                    ).reshape(T, D),
        "wqkvT": np.ascontiguousarray(inp["L0_attn_qkv_weight"].T),
        "wprojT": np.ascontiguousarray(inp["L0_attn_proj_weight"].T),
        "pbias": np.ascontiguousarray(np.asarray(inp["L0_attn_proj_bias"]).reshape(8, 128).T),
        "rw": np.asarray(inp["L0_ffn_router_w"]),
        "rbrep": np.ascontiguousarray(np.tile(
            np.asarray(inp["L0_ffn_router_b"]).reshape(1, 16), (128, 1))),
        "cidB": c["cidB"], "cidF": c["cidF"], "cstrict": c["cstrict"],
        "cones": c["cones"], "ciota16": c["ciota16"], "cexp32": c["cexp32"],
        "ceps": c["ceps"],
    }


def core_inputs_b(inp, core, counts_all):
    m = core_inputs_a(inp, core)
    c = _CONSTS
    mask = np.zeros((8, 1), np.float32)
    mask[:core, 0] = 1.0
    m.update({
        "wg": np.asarray(inp["L0_ffn_w_gate"]),
        "wu": np.asarray(inp["L0_ffn_w_up"]),
        "wdn": np.asarray(inp["L0_ffn_w_down"]),
        "swg": np.asarray(inp["L0_ffn_sw_gate"]),
        "swu": np.asarray(inp["L0_ffn_sw_up"]),
        "swd": np.asarray(inp["L0_ffn_sw_down"]),
        "ctid": c["ctid"], "cshidx": c["cshidx"],
        "cmask": mask,
        "cntall": counts_all,
        "cntmeT": np.ascontiguousarray(counts_all[core].reshape(16, 1)),
    })
    return m


from concourse.bass_utils import run_bass_kernel_spmd


def kernel(**inputs):
    ncA = build(emit_ffn=False)
    mapsA = [core_inputs_a(inputs, c) for c in range(8)]
    resA = run_bass_kernel_spmd(ncA, mapsA, core_ids=list(range(8)))
    counts_all = np.stack([np.asarray(resA.results[c]["counts_out"])[0]
                           for c in range(8)]).astype(np.float32)

    ncB = build(emit_ffn=True)
    mapsB = [core_inputs_b(inputs, c, counts_all) for c in range(8)]
    resB = run_bass_kernel_spmd(ncB, mapsB, core_ids=list(range(8)))
    out = np.concatenate([np.asarray(resB.results[c]["out"], dtype=np.float32)
                          for c in range(8)], axis=0)
    return out.reshape(32, 2048, 1024)


# version 03
# speedup vs baseline: 1.2225x; 1.0493x over previous
import numpy as np
import ml_dtypes as mld
import concourse.bass as bass
import concourse.mybir as mybir
import concourse.library_config as libcfg
from contextlib import ExitStack

F32 = mybir.dt.float32
BF16 = mybir.dt.bfloat16
I16 = mybir.dt.int16
I32 = mybir.dt.int32
U16 = mybir.dt.uint16
AF = mybir.ActivationFunctionType
ALU = mybir.AluOpType
AX = mybir.AxisListType

NSEQ, S, D = 4, 2048, 1024
T = NSEQ * S
NT = T // 128


def set_dims(nseq):
    global NSEQ, T, NT, NPMAX, PML, DVE_SEQ, ACT_SEQ, PE_SEQ
    NSEQ = nseq
    T = NSEQ * S
    NT = T // 128
    NPMAX = 2 * T // M2 + E
    PML = NPMAX * M2
    DVE_SEQ = 16 * A_DVE + VK_DVE + 4 * (B_DVE + 4 * EP_DVE)
    ACT_SEQ = 16 * A_ACT + Q_ACT + 4 * (B_ACT + 4 * EP_ACT)
    PE_SEQ = 16 * 3 + 8 + 4 * (17 + 12)
H = 32
E, F = 16, 1408
FC, DC = 11, 8
M2 = 512
NPMAX = 2 * T // M2 + E
PML = NPMAX * M2
CAP = 10240.0
EPS_LN = 1e-5
EPS_ATT = 1e-15
INV_D = 1.0 / D
RACE_CHECK = False

A_DVE, A_ACT = 7, 4
EP_DVE, EP_ACT = 17, 5
B_DVE, B_ACT = 9, 17
VK_DVE = 1
Q_ACT = 8

DVE_SEQ = 16 * A_DVE + VK_DVE + 4 * (B_DVE + 4 * EP_DVE)
ACT_SEQ = 16 * A_ACT + Q_ACT + 4 * (B_ACT + 4 * EP_ACT)
PE_SEQ = 16 * 3 + 8 + 4 * (17 + 12)


def dve_A(g, k=A_DVE):
    s, t = divmod(g, 16)
    return s * DVE_SEQ + t * A_DVE + k


def dve_vk(s):
    return s * DVE_SEQ + 16 * A_DVE + VK_DVE


def dve_B(s, gq, k=B_DVE):
    return dve_vk(s) + gq * (B_DVE + 4 * EP_DVE) + k


def dve_ep(g, k=EP_DVE):
    s, r = divmod(g, 16)
    gq, tt = divmod(r, 4)
    return dve_B(s, gq) + tt * EP_DVE + k


def act_A(g, k=A_ACT):
    s, t = divmod(g, 16)
    return s * ACT_SEQ + t * A_ACT + k


def act_q(s, o):
    return s * ACT_SEQ + 16 * A_ACT + (o + 1)


def act_B(s, gq, k=B_ACT):
    return act_q(s, Q_ACT - 1) + gq * (B_ACT + 4 * EP_ACT) + k


def act_ep(g, k=EP_ACT):
    s, r = divmod(g, 16)
    gq, tt = divmod(r, 4)
    return act_B(s, gq) + tt * EP_ACT + k


def pe_A(g, k):
    s, t = divmod(g, 16)
    return s * PE_SEQ + t * 3 + k


def pe_q(s, o):
    return s * PE_SEQ + 48 + (o + 1)


def pe_B(s, gq, k=17):
    return pe_q(s, 7) + gq * 29 + k


def pe_ep(g, k):
    s, r = divmod(g, 16)
    gq, tt = divmod(r, 4)
    return pe_B(s, gq) + tt * 3 + k


def RTD(name, g=0):
    tbl = {"oh12": 1, "base": 2, "cbrep": 3}
    if name in tbl:
        return tbl[name]
    if name == "carry":
        return 4 + 3 * g
    if name == "bev":
        return 5 + 3 * g
    if name == "runadd":
        return 6 + 3 * g
    off = 3 + 3 * NT
    return off + {"e2": 1, "expw": 2, "padt": 3, "pofft": 4,
                  "padr": 5, "pofr": 6, "scat": 7}[name]


def RT(name, g=0):
    if name == "base":
        return 1
    if name == "brep":
        return 2
    if name == "B":
        return 3 + 2 * g
    if name == "ct":
        return 4 + 2 * g
    off = 2 + 2 * NT
    return off + {"poffmm": 1, "padrtr": 2, "pofrtr": 3, "prep": 4}[name]


MP_PE = 34
MA = 19
MD = 15
CMB_DVE, CMB_ACT = 11, 3


def build(emit_ffn: bool):
    nc = bass.Bass("TRN2", target_bir_lowering=False,
                   detect_race_conditions=RACE_CHECK)
    din = lambda n, sh, dt: nc.dram_tensor(n, sh, dt, kind="ExternalInput").ap()
    dou = lambda n, sh, dt: nc.dram_tensor(n, sh, dt, kind="ExternalOutput").ap()
    dit = lambda n, sh, dt: nc.dram_tensor(n, sh, dt, kind="Internal").ap()

    xin = din("xin", [T, D], F32)
    wqkvT = din("wqkvT", [D, 3 * D], F32)
    wprojT = din("wprojT", [D, D], F32)
    pbias = din("pbias", [128, DC], F32)
    rw = din("rw", [D, E], F32)
    rbrep = din("rbrep", [128, E], F32)
    cidB = din("cidB", [128, 128], BF16)
    cidF = din("cidF", [128, 128], F32)
    cstrict = din("cstrict", [128, 128], F32)
    cones = din("cones", [128, 128], F32)
    ciota16 = din("ciota16", [128, E], F32)
    cexp32 = din("cexp32", [32, DC * 128], BF16)
    ceps = din("ceps", [128, 2], F32)
    if emit_ffn:
        wg = din("wg", [E, D, F], F32)
        wu = din("wu", [E, D, F], F32)
        wdn = din("wdn", [E, F, D], F32)
        swg = din("swg", [D, F], F32)
        swu = din("swu", [D, F], F32)
        swd = din("swd", [F, D], F32)
        ctid = din("ctid", [128, NT], I16)
        cshidx = din("cshidx", [128, T // 16], I16)
        cmask = din("cmask", [8, 1], F32)
        cntall = din("cntall", [8, E], F32)
        cntmeT = din("cntmeT", [16, 1], F32)
        wgbf = dit("wgbf", [E, D, F], BF16)
        wubf = dit("wubf", [E, D, F], BF16)
        wdbf = dit("wdbf", [E, F, D], BF16)
        swgbf = dit("swgbf", [D, F], BF16)
        swubf = dit("swubf", [D, F], BF16)
        swdbf = dit("swdbf", [F, D], BF16)
        xmid_d = dit("xmid_d", [T, D], F32)
        hrows_d = dit("hrows_d", [T, D], BF16)
        shout_d = dit("shout_d", [T, D], BF16)
        packed_d = dit("packed_d", [PML, D], BF16)
        plist_d = dit("plist_d", [16, PML // 16], I16)
        out_d = dou("out", [T, D], F32)
    else:
        counts_out = dou("counts_out", [1, E], F32)

    est = ExitStack()
    sb = lambda n, sh, dt: est.enter_context(nc.sbuf_tensor(n, sh, dt))
    psa = lambda n, sh, dt: est.enter_context(nc.psum_tensor(n, sh, dt))
    sem = lambda n: est.enter_context(nc.semaphore(n))

    with est:
        idB = sb("idB", [128, 128], BF16)
        idF = sb("idF", [128, 128], F32)
        strictL = sb("strictL", [128, 128], F32)
        ones = sb("ones", [128, 128], F32)
        iota16 = sb("iota16", [128, E], F32)
        exp32_sb = sb("exp32_sb", [32, DC * 128], BF16)
        rbrep_sb = sb("rbrep_sb", [128, E], F32)
        eps_sb = sb("eps_sb", [128, 2], F32)
        logits_w = sb("logits_w", [128, NT, E], F32)
        ln3a = sb("ln3a", [128, NT], F32)
        ln3b = sb("ln3b", [128, NT], F32)
        xm2 = sb("xm2", [128, 2, D], F32)
        i8 = sb("i8", [128, NT, 8], U16)
        l8 = sb("l8", [128, NT, 8], F32)
        oh12 = sb("oh12", [128, NT, E], F32)
        cnt_sb = sb("cnt_sb", [1, E], F32)
        run_sb = sb("run_sb", [1, E], F32)
        bk = [psa(f"bk{i}", [128, 512], F32) for i in range(6)]
        poolB = psa("poolB", [128, 2048], BF16)

        s_w0 = sem("s_w0")
        s_wc = sem("s_wc")
        s_x = [sem("s_x0"), sem("s_x1")]
        s_x2 = [sem("s_x2a"), sem("s_x2b")]
        s_dve = sem("s_dve")
        s_act = sem("s_act")
        s_pe = sem("s_pe")
        s_st = [sem("s_st0"), sem("s_st1")]
        s_hst = [sem("s_hst0"), sem("s_hst1")]
        s_rt = sem("s_rt")
        s_rtd = sem("s_rtd")
        s_rta = sem("s_rta")
        s_fin = sem("s_fin")
        s_ini = sem("s_ini")
        if emit_ffn:
            g1w = sb("g1w", [128, NT], F32)
            g2w = sb("g2w", [128, NT], F32)
            p1i = sb("p1i", [128, NT], I32)
            p2i = sb("p2i", [128, NT], I32)
            plist_sb = sb("plist_sb", [128, PML // 16], I16)
            nwin_i = sb("nwin_i", [16, 1], I32)
            woff_i = sb("woff_i", [16, 1], I32)
            nptot_i = sb("nptot_i", [1, 4], I32)
            cntT = sb("cntT", [16, 1], F32)
            cntall_sb = sb("cntall_sb", [8, E], F32)
            cmask_sb = sb("cmask_sb", [8, 1], F32)
            ctid_sb = sb("ctid_sb", [128, NT], I16)
            shidx_sb = sb("shidx_sb", [128, T // 16], I16)
            s_pl = sem("s_pl")
            s_wl = [sem("s_wl0"), sem("s_wl1")]
            s_wcS = sem("s_wcS")
            s_wcE = [sem("s_wcE0"), sem("s_wcE1")]
            s_wfree = sem("s_wfree")
            s_gat = sem("s_gat")
            s_gfree = sem("s_gfree")
            s_mp = sem("s_mp")
            s_ma = sem("s_ma")
            s_md = sem("s_md")
            s_ost = sem("s_ost")
            s_cg = sem("s_cg")
            s_cl = sem("s_cl")
            s_cd = sem("s_cd")
            s_ca = sem("s_ca")
            s_so = sem("s_so")

        NW0 = 9 + (5 if emit_ffn else 0)
        NS2 = T // M2
        NWC = 3

        with ExitStack() as st1:
            sb1 = lambda n, sh, dt: st1.enter_context(nc.sbuf_tensor(n, sh, dt))
            wqkv_sb = sb1("wqkv_sb", [128, DC, 3 * D], BF16)
            wproj_sb = sb1("wproj_sb", [128, DC, D], BF16)
            pbias_sb = sb1("pbias_sb", [128, DC], F32)
            rw_sb = sb1("rw_sb", [128, DC, E], BF16)
            lnx_t = sb1("lnx_t", [128, 2, D], BF16)
            lnx_fm = sb1("lnx_fm", [128, DC, S], BF16)
            q_fm = sb1("q_fm", [128, DC, S], BF16)
            xring = sb1("xring", [128, 2, D], F32)
            k_tok = sb1("k_tok", [128, D], BF16)
            vpad = sb1("vpad", [128, H * 33], BF16)
            nstat = sb1("nstat", [128, DC, 128], BF16)
            dstat = sb1("dstat", [128, DC, 32], BF16)
            den_sb = sb1("den_sb", [32, 512], BF16)
            den2_sb = sb1("den2_sb", [32, 512], BF16)
            rep_sb = sb1("rep_sb", [128, 2, 512], BF16)
            proj_sb = sb1("proj_sb", [128, DC, 512], BF16)
            y_fm = sb1("y_fm", [128, DC, 512], BF16)
            ytok = sb1("ytok", [128, D], F32)
            hbf = sb1("hbf", [128, 2, D], BF16)
            sta = sb1("sta", [128, 4, 18], F32)

            kv_ps = [bk[0][:, :], bk[1][:, :], bk[2][:, :], bk[3][:, :]]
            vk_ps = bk[4][:, 0:264]
            den_ps = bk[4][0:32, 0:512]
            rep_ps = [bk[0][:, :], bk[1][:, :]]
            num_ps = [bk[2][:, :], bk[3][:, :]]
            y_ps = [bk[5][:, :], bk[4][:, :]]
            lgt_ps = bk[3][:, 128:144]
            tp_ps = [poolB[:, 0:1024], poolB[:, 1024:2048]]

            def ln_stats(vector, sem_, src_ap, stw, base, tmp):
                vector.wait_ge(sem_, base)
                vector.bn_stats(stw[:, 6:12], src_ap[:, 0:512]).then_inc(sem_, 1)
                vector.bn_stats(stw[:, 12:18],
                                src_ap[:, 512:1024]).then_inc(sem_, 1)
                vector.wait_ge(sem_, base + 2)
                vector.bn_aggr(stw[:, 0:2], stw[:, 6:18]).then_inc(sem_, 1)
                vector.nop().then_inc(sem_, 1)
                vector.nop().then_inc(sem_, 1)

            def ln_finish(vector, sem_, stw, dsta, dstb, base,
                          sqrt_sem, sqrt_done):
                vector.wait_ge(sqrt_sem, sqrt_done)
                vector.reciprocal(dsta, stw[:, 2:3]).then_inc(sem_, 1)
                vector.wait_ge(sem_, base + 6)
                vector.scalar_tensor_tensor(dstb, stw[:, 0:1], -1.0, dsta,
                                            ALU.mult, ALU.mult).then_inc(sem_, 1)

            with nc.Block() as blk:
                @blk.sync
                def _(sync):
                    for dst, src in [
                        (idB[:], cidB[:, :]), (idF[:], cidF[:, :]),
                        (strictL[:], cstrict[:, :]), (ones[:], cones[:, :]),
                        (iota16[:], ciota16[:, :]), (exp32_sb[:], cexp32[:, :]),
                        (pbias_sb[:], pbias[:, :]),
                        (rbrep_sb[:], rbrep[:, :]), (eps_sb[:], ceps[:, :]),
                    ]:
                        sync.dma_start(out=dst, in_=src).then_inc(s_w0, 16)
                    if emit_ffn:
                        for dst, src in [
                            (ctid_sb[:], ctid[:, :]), (shidx_sb[:], cshidx[:, :]),
                            (cmask_sb[:], cmask[:, :]), (cntall_sb[:], cntall[:, :]),
                            (cntT[:], cntmeT[:, :]),
                        ]:
                            sync.dma_start(out=dst, in_=src).then_inc(s_w0, 16)
                    for s in range(NSEQ):
                        for t in range(16):
                            g = 16 * s + t
                            if g >= 2:
                                sync.wait_ge(s_act, act_A(g - 2, 2))
                            sync.dma_start(out=xring[:, g % 2, :],
                                           in_=xin[g * 128:(g + 1) * 128, :]
                                           ).then_inc(s_x[g % 2], 16)
                        for t in range(16):
                            g = 16 * s + t
                            if g >= 2:
                                sync.wait_ge(s_act, act_ep(g - 2, 4))
                                if emit_ffn:
                                    sync.wait_ge(s_st[g % 2], 16 * (g // 2))
                            sync.dma_start(out=xm2[:, g % 2, :],
                                           in_=xin[g * 128:(g + 1) * 128, :]
                                           ).then_inc(s_x2[g % 2], 16)

                @blk.gpsimd
                def _(gpsimd):
                    gpsimd.dma_start(out=wqkv_sb[:, :, :],
                                     in_=wqkvT.rearrange("(c p) o -> p c o", p=128)
                                     ).then_inc(s_wc, 16)
                    gpsimd.dma_start(out=wproj_sb[:, :, :],
                                     in_=wprojT.rearrange("(c p) o -> p c o", p=128)
                                     ).then_inc(s_wc, 16)
                    gpsimd.dma_start(out=rw_sb[:, :, :],
                                     in_=rw.rearrange("(c p) o -> p c o", p=128)
                                     ).then_inc(s_wc, 16)
                    if emit_ffn:
                        for src_, dst_ in ((swg, swgbf), (swu, swubf),
                                           (swd, swdbf)):
                            gpsimd.dma_start(out=dst_[:, :],
                                             in_=src_[:, :]).then_inc(s_wcS, 16)
                        for e in range(E):
                            if e >= 2:
                                gpsimd.wait_ge(s_wcE[e % 2], 48 * (e // 2))
                            for src_, dst_ in ((wg, wgbf), (wu, wubf),
                                               (wdn, wdbf)):
                                gpsimd.dma_start(out=dst_[e], in_=src_[e]
                                                 ).then_inc(s_wcE[e % 2], 16)

                @blk.vector
                def _(vector):
                    vector.memset(vpad[:], 1.0)
                    vector.memset(nstat[:].rearrange("p c o -> p (c o)"), 0.0)
                    vector.memset(dstat[:].rearrange("p c o -> p (c o)"), 0.0)
                    vector.drain().then_inc(s_ini, 1)
                    for s in range(NSEQ):
                        for t in range(16):
                            g = 16 * s + t
                            vector.wait_ge(s_x[g % 2], 16 * (g // 2 + 1))
                            ln_stats(vector, s_dve, xring[:, g % 2, :],
                                     sta[:, g % 2, :], dve_A(g, 0), None)
                            ln_finish(vector, s_dve, sta[:, g % 2, :],
                                      sta[:, g % 2, 4:5], sta[:, g % 2, 5:6],
                                      dve_A(g, 0), s_act, act_A(g, 1))
                        vector.wait_ge(s_pe, pe_A(16 * s + 15, 3))
                        for h in range(H):
                            c, j = h // 4, h % 4
                            vector.tensor_copy(
                                nstat[32 * j:32 * j + 32, c, 32 * j:32 * j + 32],
                                vk_ps[32 * j:32 * j + 32, 33 * c:33 * c + 32])
                            vv = vector.tensor_copy(
                                dstat[32 * j:32 * j + 32, c, 4 * c + j:4 * c + j + 1],
                                vk_ps[32 * j:32 * j + 32, 33 * c + 32:33 * c + 33])
                        vv.then_inc(s_dve, VK_DVE)
                        for gq in range(4):
                            vector.wait_ge(s_act, act_B(s, gq, 1))
                            with nc.allow_low_precision(reason="bf16 matmul operand"):
                                vector.reciprocal(den2_sb[:], den_sb[:]).then_inc(s_dve, 1)
                            for c in range(DC):
                                vector.wait_ge(s_pe, pe_B(s, gq, 1 + (c + 1)))
                                vector.wait_ge(s_act, act_B(s, gq, 1 + (c + 1)))
                                vector.tensor_tensor(
                                    proj_sb[:, c, :], num_ps[c % 2],
                                    rep_sb[:, c % 2, :], ALU.mult).then_inc(s_dve, 1)
                            for tt in range(4):
                                g = 16 * s + 4 * gq + tt
                                slot = 2 + g % 2
                                stw = sta[:, slot, :]
                                vector.wait_ge(s_pe, pe_ep(g, 1))
                                vector.tensor_copy(ytok[:],
                                                   tp_ps[g % 2]).then_inc(s_dve, 1)
                                ln_stats(vector, s_dve, ytok[:], stw,
                                         dve_ep(g, 1), None)
                                ln_finish(vector, s_dve, stw,
                                          sta[:, slot, 4:5], sta[:, slot, 5:6],
                                          dve_ep(g, 1), s_act, act_ep(g, 1))
                                vector.wait_ge(s_act, act_ep(g, 2))
                                vector.wait_ge(s_x2[g % 2], 16 * (g // 2 + 1))
                                vector.tensor_add(xm2[:, g % 2, :], xm2[:, g % 2, :],
                                                  ytok[:]).then_inc(s_dve, 1)
                                ln_stats(vector, s_dve, xm2[:, g % 2, :], stw,
                                         dve_ep(g, 9), None)
                                ln_finish(vector, s_dve, stw,
                                          ln3a[:, g:g + 1], ln3b[:, g:g + 1],
                                          dve_ep(g, 9), s_act, act_ep(g, 3))
                                vector.wait_ge(s_pe, pe_ep(g, 3))
                                vector.tensor_add(logits_w[:, g, :], lgt_ps,
                                                  rbrep_sb[:]).then_inc(s_dve, 1)
                    vector.nop().then_inc(s_fin, 1)

                @blk.scalar
                def _(scalar):
                    scalar.wait_ge(s_w0, 16 * NW0)
                    for s in range(NSEQ):
                        for t in range(16):
                            g = 16 * s + t
                            scalar.wait_ge(s_dve, dve_A(g, 5))
                            scalar.activation(out=sta[:, g % 2, 2:3],
                                              in_=sta[:, g % 2, 1:2], func=AF.Sqrt,
                                              scale=1.0,
                                              bias=eps_sb[:, 0:1]).then_inc(s_act, 1)
                            scalar.wait_ge(s_dve, dve_A(g, 7))
                            if t < 2 and s > 0:
                                scalar.wait_ge(s_pe, pe_ep(16 * s - 2 + t, 3))
                            elif g >= 2:
                                scalar.wait_ge(s_pe, pe_A(g - 2, 1))
                            scalar.activation(out=lnx_t[:, g % 2, :],
                                              in_=xring[:, g % 2, :], func=AF.Identity,
                                              scale=sta[:, g % 2, 4:5],
                                              bias=sta[:, g % 2, 5:6]).then_inc(s_act, 1)
                            scalar.wait_ge(s_pe, pe_A(g, 1))
                            scalar.copy(out=lnx_fm[:, :, t * 128:(t + 1) * 128],
                                        in_=tp_ps[g % 2].rearrange(
                                            "p (c x) -> p c x", x=128)).then_inc(s_act, 1)
                            scalar.wait_ge(s_pe, pe_A(g, 2))
                            if t > 0:
                                scalar.wait_ge(s_pe, pe_A(g - 1, 3))
                            scalar.activation(out=k_tok[:, 0:512], in_=bk[0][:, :],
                                              func=AF.Relu)
                            scalar.activation(out=k_tok[:, 512:1024], in_=bk[1][:, :],
                                              func=AF.Relu)
                            scalar.activation(
                                out=vpad[:].rearrange("p (h e) -> p h e", e=33)
                                [:, 0:16, 0:32],
                                in_=bk[2][:, :].rearrange("p (h e) -> p h e", e=32),
                                func=AF.Relu)
                            scalar.activation(
                                out=vpad[:].rearrange("p (h e) -> p h e", e=33)
                                [:, 16:32, 0:32],
                                in_=bk[3][:, :].rearrange("p (h e) -> p h e", e=32),
                                func=AF.Relu).then_inc(s_act, 1)
                        for o in range(DC):
                            scalar.wait_ge(s_pe, pe_q(s, o))
                            for gq in range(4):
                                aa = scalar.activation(
                                    out=q_fm[:, o, gq * 512:(gq + 1) * 512],
                                    in_=kv_ps[gq], func=AF.Relu)
                            aa.then_inc(s_act, 1)
                        for gq in range(4):
                            scalar.wait_ge(s_pe, pe_B(s, gq, 1))
                            scalar.activation(out=den_sb[:], in_=den_ps,
                                              func=AF.Identity,
                                              bias=eps_sb[0:32, 1:2]).then_inc(s_act, 1)
                            for c in range(DC):
                                scalar.wait_ge(s_pe, pe_B(s, gq, 1 + (c + 1)))
                                scalar.copy(out=rep_sb[:, c % 2, :],
                                            in_=rep_ps[c % 2]).then_inc(s_act, 1)
                            for og in range(DC):
                                scalar.wait_ge(s_pe, pe_B(s, gq, 9 + (og + 1)))
                                scalar.activation(out=y_fm[:, og, :],
                                                  in_=y_ps[og % 2], func=AF.Identity,
                                                  bias=pbias_sb[:, og:og + 1]
                                                  ).then_inc(s_act, 1)
                            for tt in range(4):
                                g = 16 * s + 4 * gq + tt
                                slot = 2 + g % 2
                                scalar.wait_ge(s_dve, dve_ep(g, 6))
                                scalar.activation(out=sta[:, slot, 2:3],
                                                  in_=sta[:, slot, 1:2], func=AF.Sqrt,
                                                  scale=1.0,
                                                  bias=eps_sb[:, 0:1]).then_inc(s_act, 1)
                                scalar.wait_ge(s_dve, dve_ep(g, 8))
                                scalar.activation(out=ytok[:],
                                                  in_=ytok[:],
                                                  func=AF.Identity,
                                                  scale=sta[:, slot, 4:5],
                                                  bias=sta[:, slot, 5:6]).then_inc(s_act, 1)
                                if emit_ffn:
                                    scalar.wait_ge(s_dve, dve_ep(g, 9))
                                    scalar.dma_start(
                                        out=xmid_d[g * 128:(g + 1) * 128, :],
                                        in_=xm2[:, g % 2, :]).then_inc(s_st[g % 2], 16)
                                scalar.wait_ge(s_dve, dve_ep(g, 14))
                                scalar.activation(out=sta[:, slot, 2:3],
                                                  in_=sta[:, slot, 1:2], func=AF.Sqrt,
                                                  scale=1.0,
                                                  bias=eps_sb[:, 0:1]).then_inc(s_act, 1)
                                scalar.wait_ge(s_dve, dve_ep(g, 16))
                                if g >= 2:
                                    scalar.wait_ge(s_pe, pe_ep(g - 2, 2))
                                    if emit_ffn:
                                        scalar.wait_ge(s_hst[g % 2], 16 * (g // 2))
                                scalar.activation(out=hbf[:, g % 2, :],
                                                  in_=xm2[:, g % 2, :],
                                                  func=AF.Identity,
                                                  scale=ln3a[:, g:g + 1],
                                                  bias=ln3b[:, g:g + 1]).then_inc(s_act, 1)
                                if emit_ffn:
                                    scalar.wait_ge(s_act, act_ep(g, 4))
                                    scalar.dma_start(
                                        out=hrows_d[g * 128:(g + 1) * 128, :],
                                        in_=hbf[:, g % 2, :]).then_inc(s_hst[g % 2], 16)
                                scalar.wait_ge(s_pe, pe_ep(g, 2))
                                scalar.copy(out=lnx_t[:, g % 2, :],
                                            in_=tp_ps[g % 2]).then_inc(s_act, 1)
                    scalar.nop().then_inc(s_fin, 1)

                @blk.tensor
                def _(tensor):
                    tensor.wait_ge(s_wc, 16 * NWC)
                    tensor.wait_ge(s_w0, 16 * NW0)
                    tensor.wait_ge(s_ini, 1)
                    for s in range(NSEQ):
                        if s > 0:
                            tensor.wait_ge(s_dve, s * DVE_SEQ)
                            tensor.wait_ge(s_act, s * ACT_SEQ)
                        for t in range(16):
                            g = 16 * s + t
                            tensor.wait_ge(s_act, act_A(g, 2))
                            if g >= 2:
                                tensor.wait_ge(s_act, act_A(g - 2, 3))
                            for c in range(DC):
                                mm = tensor.transpose(
                                    tp_ps[g % 2][:, c * 128:(c + 1) * 128],
                                    lnx_t[:, g % 2, c * 128:(c + 1) * 128],
                                    idB[:])
                            mm.then_inc(s_pe, 1)
                            tensor.wait_ge(s_act, act_A(g, 3))
                            if t > 0:
                                tensor.wait_ge(s_act, act_A(g - 1, 4))
                            for c in range(DC):
                                tensor.ldweights(lnx_fm[:, c, t * 128:(t + 1) * 128])
                                for og in range(4):
                                    mm = tensor.matmul(
                                        kv_ps[og],
                                        lnx_fm[:, c, t * 128:(t + 1) * 128],
                                        wqkv_sb[:, c, D + og * 512:D + (og + 1) * 512],
                                        start=(c == 0), stop=(c == DC - 1))
                            mm.then_inc(s_pe, 1)
                            tensor.wait_ge(s_act, act_A(g, 4))
                            if t == 0 and s > 0:
                                tensor.wait_ge(s_dve, dve_vk(s - 1))
                            for h in range(H):
                                c, j = h // 4, h % 4
                                mm = tensor.matmul(vk_ps[32 * j:32 * j + 32,
                                                         33 * c:33 * c + 33],
                                                   k_tok[:, 32 * h:32 * h + 32],
                                                   vpad[:, 33 * h:33 * h + 33],
                                                   start=(t == 0 and c == 0),
                                                   stop=(t == 15 and c == DC - 1),
                                                   skip_group_check=True,
                                                   tile_position=(0, 32 * j))
                            mm.then_inc(s_pe, 1)
                        for o in range(DC):
                            if o >= 1:
                                tensor.wait_ge(s_act, act_q(s, o - 1))
                            else:
                                tensor.wait_ge(s_act, act_A(16 * s + 15, 4))
                            for c in range(DC):
                                tensor.ldweights(wqkv_sb[:, c, o * 128:(o + 1) * 128])
                                for gq in range(4):
                                    mm = tensor.matmul(
                                        kv_ps[gq],
                                        wqkv_sb[:, c, o * 128:(o + 1) * 128],
                                        lnx_fm[:, c, gq * 512:(gq + 1) * 512],
                                        start=(c == 0), stop=(c == DC - 1))
                            mm.then_inc(s_pe, 1)
                        for gq in range(4):
                            tensor.wait_ge(s_dve, dve_vk(s))
                            tensor.wait_ge(s_act, act_q(s, 7))
                            if gq > 0:
                                tensor.wait_ge(s_dve, dve_ep(16 * s + 4 * gq - 1, 17))
                            for c in range(DC):
                                tensor.ldweights(dstat[:, c, :])
                                mm = tensor.matmul(den_ps, dstat[:, c, :],
                                                   q_fm[:, c, gq * 512:(gq + 1) * 512],
                                                   start=(c == 0), stop=(c == DC - 1))
                            mm.then_inc(s_pe, 1)
                            for c in range(DC):
                                tensor.wait_ge(s_dve, dve_B(s, gq, 1))
                                if c >= 2:
                                    tensor.wait_ge(s_act, act_B(s, gq, c))
                                    tensor.wait_ge(s_dve, dve_B(s, gq, 0) + c)
                                tensor.ldweights(exp32_sb[:, c * 128:(c + 1) * 128])
                                tensor.matmul(rep_ps[c % 2],
                                              exp32_sb[:, c * 128:(c + 1) * 128],
                                              den2_sb[:], start=True, stop=True)
                                tensor.ldweights(nstat[:, c, :])
                                tensor.matmul(num_ps[c % 2], nstat[:, c, :],
                                              q_fm[:, c, gq * 512:(gq + 1) * 512],
                                              start=True, stop=True
                                              ).then_inc(s_pe, 1)
                            for og in range(DC):
                                tensor.wait_ge(s_dve, dve_B(s, gq, 9))
                                if og >= 2:
                                    tensor.wait_ge(s_act, act_B(s, gq, 9 + og - 1))
                                for c in range(DC):
                                    tensor.ldweights(wproj_sb[:, c,
                                                              og * 128:(og + 1) * 128])
                                    mm = tensor.matmul(
                                        y_ps[og % 2],
                                        wproj_sb[:, c, og * 128:(og + 1) * 128],
                                        proj_sb[:, c, :],
                                        start=(c == 0), stop=(c == DC - 1))
                                mm.then_inc(s_pe, 1)
                            for tt in range(4):
                                g = 16 * s + 4 * gq + tt
                                tensor.wait_ge(s_act, act_B(s, gq, 17))
                                if g >= 2:
                                    tensor.wait_ge(s_act, act_ep(g - 2, 5))
                                for c in range(DC):
                                    mm = tensor.transpose(
                                        tp_ps[g % 2][:, c * 128:(c + 1) * 128],
                                        y_fm[:, c, tt * 128:(tt + 1) * 128], idB[:])
                                mm.then_inc(s_pe, 1)
                                tensor.wait_ge(s_dve, dve_ep(g, 1))
                                tensor.wait_ge(s_act, act_ep(g, 4))
                                for c in range(DC):
                                    mm = tensor.transpose(
                                        tp_ps[g % 2][:, c * 128:(c + 1) * 128],
                                        hbf[:, g % 2, c * 128:(c + 1) * 128], idB[:])
                                mm.then_inc(s_pe, 1)
                                tensor.wait_ge(s_act, act_ep(g, 5))
                                if g >= 1:
                                    tensor.wait_ge(s_dve, dve_ep(g - 1, 17))
                                for c in range(DC):
                                    tensor.ldweights(
                                        lnx_t[:, g % 2, c * 128:(c + 1) * 128])
                                    mm = tensor.matmul(
                                        lgt_ps,
                                        lnx_t[:, g % 2, c * 128:(c + 1) * 128],
                                        rw_sb[:, c, :],
                                        start=(c == 0), stop=(c == DC - 1))
                                mm.then_inc(s_pe, 1)
                    tensor.nop().then_inc(s_fin, 1)

        with ExitStack() as st2:
            sb2 = lambda n, sh, dt: st2.enter_context(nc.sbuf_tensor(n, sh, dt))
            tmpw = sb2("tmpw", [128, NT, E], F32)
            idx1f = sb2("idx1f", [128, NT], F32)
            idx2f = sb2("idx2f", [128, NT], F32)
            ct_ps = bk[3][0:1, 0:16]
            if emit_ffn:
                oh1 = sb2("oh1", [128, NT, E], F32)
                expw = sb2("expw", [128, NT, E], F32)
                e2w = sb2("e2w", [128, NT], F32)
                zw = sb2("zw", [128, NT], F32)
                v1w = sb2("v1w", [128, NT], F32)
                v2w = sb2("v2w", [128, NT], F32)
                r1w = sb2("r1w", [128, NT], F32)
                r2w = sb2("r2w", [128, NT], F32)
                base_sb = sb2("base_sb", [1, E], F32)
                cbase_rep = sb2("cbase_rep", [128, E], F32)
                carry_sb = sb2("carry_sb", [1, E], F32)
                Bw = sb2("Bw", [128, NT, E], F32)
                cTi = sb2("cTi", [16, 1], I32)
                padTi = sb2("padTi", [16, 1], I32)
                padT = sb2("padT", [16, 1], F32)
                poffT = sb2("poffT", [16, 1], F32)
                poffTi = sb2("poffTi", [16, 1], I32)
                pad_row = sb2("pad_row", [1, E], F32)
                poff_row = sb2("poff_row", [1, E], F32)
                poff_rep = sb2("poff_rep", [128, E], F32)
                d1i = sb2("d1i", [128, NT], I32)
                d2i = sb2("d2i", [128, NT], I32)
                dti = sb2("dti", [128, NT], I32)
                zrow = sb2("zrow", [16, PML // 16], I16)
                B_ps = [bk[4][:, 0:16], bk[5][:, 0:16]]
                base_ps = bk[1][0:1, 0:16]
                brep_ps = bk[2][:, 0:16]
                poff_ps = bk[1][0:16, 64:65]
                prow_ps = bk[1][0:1, 96:112]
                prep_ps = bk[2][:, 128:144]

            with nc.Block() as blk2:
                @blk2.vector
                def _(vector):
                    vector.drain()
                    for g in range(NT):
                        vector.max(l8[:, g, :], logits_w[:, g, :])
                        vector.drain()
                        vector.max_index(i8[:, g, :], l8[:, g, :],
                                         logits_w[:, g, :])
                    vector.drain()
                    vector.tensor_copy(idx1f[:], i8[:, :, 0])
                    vector.tensor_copy(idx2f[:], i8[:, :, 1])
                    vector.drain()
                    io_b = iota16[:].unsqueeze(1).broadcast_to([128, NT, E])
                    i1_b = idx1f[:].unsqueeze(2).broadcast_to([128, NT, E])
                    i2_b = idx2f[:].unsqueeze(2).broadcast_to([128, NT, E])
                    if emit_ffn:
                        vector.tensor_tensor(oh1[:], io_b, i1_b, ALU.is_equal)
                        vector.tensor_tensor(tmpw[:], io_b, i2_b, ALU.is_equal)
                        vector.drain()
                        vector.tensor_add(oh12[:], oh1[:], tmpw[:])
                    else:
                        vector.tensor_tensor(oh12[:], io_b, i1_b, ALU.is_equal)
                        vector.tensor_tensor(tmpw[:], io_b, i2_b, ALU.is_equal)
                        vector.drain()
                        vector.tensor_add(oh12[:], oh12[:], tmpw[:])
                    vector.drain().then_inc(s_rtd, 1)
                    if not emit_ffn:
                        vector.memset(run_sb[:], 0.0)
                        vector.drain()
                        for g in range(NT):
                            vector.wait_ge(s_rt, g + 1)
                            vector.tensor_add(run_sb[:], run_sb[:], ct_ps)
                            vector.drain().then_inc(s_rtd, 1)
                        vector.tensor_copy(cnt_sb[:], run_sb[:]).then_inc(s_rtd, 1)
                        vector.nop().then_inc(s_fin, 1)
                        return
                    vector.wait_ge(s_rt, RT("base"))
                    vector.tensor_copy(base_sb[:], base_ps).then_inc(s_rtd, 1)
                    vector.wait_ge(s_rt, RT("brep"))
                    vector.tensor_copy(cbase_rep[:], brep_ps).then_inc(s_rtd, 1)
                    vector.memset(run_sb[:], 0.0)
                    vector.drain()
                    for g in range(NT):
                        vector.tensor_add(carry_sb[:], base_sb[:], run_sb[:])
                        vector.drain().then_inc(s_rtd, 1)
                        vector.wait_ge(s_rt, RT("B", g))
                        vector.tensor_copy(Bw[:, g, :], B_ps[g % 2]).then_inc(s_rtd, 1)
                        vector.wait_ge(s_rt, RT("ct", g))
                        vector.tensor_add(run_sb[:], run_sb[:], ct_ps)
                        vector.drain().then_inc(s_rtd, 1)
                    vector.tensor_mul(tmpw[:], oh1[:], Bw[:])
                    vector.drain()
                    vector.reduce_sum(r1w[:], tmpw[:], axis=AX.X)
                    vector.drain()
                    vector.tensor_sub(tmpw[:], oh12[:], oh1[:])
                    vector.drain()
                    vector.tensor_mul(expw[:], tmpw[:], Bw[:])
                    vector.drain()
                    vector.reduce_sum(r2w[:], expw[:], axis=AX.X)
                    vector.tensor_sub(e2w[:], l8[:, :, 1], l8[:, :, 0])
                    vector.drain().then_inc(s_rtd, 1)
                    l1_b = l8[:, :, 0:1].broadcast_to([128, NT, E])
                    vector.tensor_sub(expw[:], logits_w[:], l1_b)
                    vector.drain().then_inc(s_rtd, 1)
                    vector.wait_ge(s_rta, 2)
                    vector.reduce_sum(zw[:], expw[:], axis=AX.X)
                    vector.drain()
                    vector.scalar_tensor_tensor(zw[:], zw[:], 1e-9, e2w[:],
                                                ALU.mult, ALU.add)
                    vector.drain()
                    vector.tensor_scalar_add(zw[:], zw[:], 1.0)
                    vector.drain()
                    vector.reciprocal(v1w[:], zw[:])
                    vector.drain()
                    vector.tensor_mul(v2w[:], e2w[:], v1w[:])
                    vector.tensor_single_scalar(g1w[:], r1w[:], CAP, ALU.is_lt)
                    vector.tensor_single_scalar(g2w[:], r2w[:], CAP, ALU.is_lt)
                    vector.drain()
                    vector.tensor_mul(g1w[:], g1w[:], v1w[:])
                    vector.tensor_mul(g2w[:], g2w[:], v2w[:])
                    vector.tensor_copy(cTi[:], cntT[:])
                    vector.drain()
                    vector.tensor_scalar_add(padTi[:], cTi[:], M2 - 1)
                    vector.drain()
                    vector.tensor_single_scalar(padTi[:], padTi[:], 9,
                                                ALU.logical_shift_right)
                    vector.drain()
                    vector.tensor_copy(nwin_i[:], padTi[:])
                    vector.drain()
                    vector.tensor_single_scalar(padTi[:], padTi[:], 9,
                                                ALU.logical_shift_left)
                    vector.drain()
                    vector.tensor_copy(padT[:], padTi[:])
                    vector.drain().then_inc(s_rtd, 1)
                    vector.wait_ge(s_rt, RT("poffmm"))
                    vector.tensor_copy(poffT[:], poff_ps)
                    vector.drain()
                    vector.tensor_copy(poffTi[:], poffT[:])
                    vector.drain()
                    vector.tensor_single_scalar(woff_i[:], poffTi[:], 9,
                                                ALU.logical_shift_right)
                    vector.drain().then_inc(s_rtd, 1)
                    vector.wait_ge(s_rt, RT("padrtr"))
                    vector.tensor_copy(pad_row[:], prow_ps)
                    vector.drain().then_inc(s_rtd, 1)
                    vector.wait_ge(s_rt, RT("pofrtr"))
                    vector.tensor_copy(poff_row[:], prow_ps)
                    vector.drain().then_inc(s_rtd, 1)
                    vector.wait_ge(s_rt, RT("prep"))
                    vector.tensor_copy(poff_rep[:], prep_ps)
                    vector.tensor_add(pad_row[0:1, 0:1], poff_row[0:1, 15:16],
                                      pad_row[0:1, 15:16])
                    vector.drain()
                    vector.tensor_copy(nptot_i[0:1, 0:1], pad_row[0:1, 0:1])
                    vector.drain()
                    vector.tensor_single_scalar(nptot_i[0:1, 0:1], nptot_i[0:1, 0:1],
                                                9, ALU.logical_shift_right)
                    pr_b = poff_rep[:].unsqueeze(1).broadcast_to([128, NT, E])
                    cb_b = cbase_rep[:].unsqueeze(1).broadcast_to([128, NT, E])
                    vector.tensor_mul(tmpw[:], oh1[:], cb_b)
                    vector.drain()
                    vector.reduce_sum(v1w[:], tmpw[:], axis=AX.X)
                    vector.drain()
                    vector.tensor_sub(r1w[:], r1w[:], v1w[:])
                    vector.tensor_mul(tmpw[:], oh1[:], pr_b)
                    vector.drain()
                    vector.reduce_sum(v1w[:], tmpw[:], axis=AX.X)
                    vector.drain()
                    vector.tensor_add(r1w[:], r1w[:], v1w[:])
                    vector.drain()
                    vector.tensor_copy(p1i[:], r1w[:])
                    vector.tensor_sub(tmpw[:], oh12[:], oh1[:])
                    vector.drain()
                    vector.tensor_mul(expw[:], tmpw[:], cb_b)
                    vector.drain()
                    vector.reduce_sum(v2w[:], expw[:], axis=AX.X)
                    vector.drain()
                    vector.tensor_sub(r2w[:], r2w[:], v2w[:])
                    vector.tensor_mul(expw[:], tmpw[:], pr_b)
                    vector.drain()
                    vector.reduce_sum(v2w[:], expw[:], axis=AX.X)
                    vector.drain()
                    vector.tensor_add(r2w[:], r2w[:], v2w[:])
                    vector.drain()
                    vector.tensor_copy(p2i[:], r2w[:])
                    for pi, ddst in ((p1i, d1i), (p2i, d2i)):
                        vector.drain()
                        vector.tensor_single_scalar(dti[:], pi[:], 15, ALU.bitwise_and)
                        vector.drain()
                        vector.tensor_single_scalar(dti[:], dti[:], PML // 16, ALU.mult)
                        vector.tensor_single_scalar(ddst[:], pi[:], 4,
                                                    ALU.logical_shift_right)
                        vector.drain()
                        vector.tensor_add(ddst[:], ddst[:], dti[:])
                    vector.memset(zrow[:], 0)
                    vector.drain().then_inc(s_rtd, 1)
                    vector.nop().then_inc(s_fin, 1)

                @blk2.scalar
                def _(scalar):
                    if emit_ffn:
                        scalar.wait_ge(s_rtd, RTD("e2"))
                        scalar.activation(out=e2w[:], in_=e2w[:], func=AF.Exp) \
                            .then_inc(s_rta, 1)
                        scalar.wait_ge(s_rtd, RTD("expw"))
                        scalar.activation(out=expw[:].rearrange("p t e -> p (t e)"),
                                          in_=expw[:].rearrange("p t e -> p (t e)"),
                                          func=AF.Exp).then_inc(s_rta, 1)
                    else:
                        scalar.wait_ge(s_rtd, NT + 2)
                        scalar.dma_start(out=counts_out[:, :],
                                         in_=cnt_sb[:]).then_inc(s_st[0], 16)
                        scalar.wait_ge(s_st[0], 16)
                    scalar.nop().then_inc(s_fin, 1)

                @blk2.tensor
                def _(tensor):
                    tensor.wait_ge(s_rtd, RTD("oh12"))
                    if not emit_ffn:
                        for g in range(NT):
                            if g >= 1:
                                tensor.wait_ge(s_rtd, 1 + g)
                            tensor.matmul(ct_ps, ones[:, 0:1], oh12[:, g, :],
                                          start=True, stop=True
                                          ).then_inc(s_rt, 1)
                        tensor.nop().then_inc(s_fin, 1)
                        return
                    tensor.matmul(base_ps, cmask_sb[:], cntall_sb[:],
                                  start=True, stop=True).then_inc(s_rt, 1)
                    tensor.wait_ge(s_rtd, RTD("base"))
                    tensor.matmul(brep_ps, ones[0:1, :], base_sb[:],
                                  start=True, stop=True).then_inc(s_rt, 1)
                    for g in range(NT):
                        tensor.wait_ge(s_rtd, RTD("carry", g))
                        if g >= 2:
                            tensor.wait_ge(s_rtd, RTD("bev", g - 2))
                        tensor.matmul(B_ps[g % 2], strictL[:], oh12[:, g, :],
                                      start=True, stop=False)
                        tensor.matmul(B_ps[g % 2], ones[0:1, :], carry_sb[:],
                                      start=False, stop=True).then_inc(s_rt, 1)
                        if g >= 1:
                            tensor.wait_ge(s_rtd, RTD("runadd", g - 1))
                        tensor.matmul(ct_ps, ones[:, 0:1], oh12[:, g, :],
                                      start=True, stop=True).then_inc(s_rt, 1)
                    tensor.wait_ge(s_rtd, RTD("padt"))
                    tensor.matmul(poff_ps, strictL[0:16, 0:16], padT[:],
                                  start=True, stop=True).then_inc(s_rt, 1)
                    tensor.wait_ge(s_rtd, RTD("pofft"))
                    tensor.matmul(prow_ps, padT[:], idF[0:16, 0:16],
                                  start=True, stop=True).then_inc(s_rt, 1)
                    tensor.wait_ge(s_rtd, RTD("padr"))
                    tensor.matmul(prow_ps, poffT[:], idF[0:16, 0:16],
                                  start=True, stop=True).then_inc(s_rt, 1)
                    tensor.wait_ge(s_rtd, RTD("pofr"))
                    tensor.matmul(prep_ps, ones[0:1, :], poff_row[:],
                                  start=True, stop=True).then_inc(s_rt, 1)
                    tensor.nop().then_inc(s_fin, 1)

                @blk2.gpsimd
                def _(gpsimd):
                    if emit_ffn:
                        gpsimd.load_library(libcfg.mlp)
                        gpsimd.wait_ge(s_rtd, RTD("scat"))
                        gpsimd.dma_start(out=plist_d[:, :],
                                         in_=zrow[:]).then_inc(s_pl, 16)
                        gpsimd.wait_ge(s_pl, 16)
                        for g in range(NT):
                            for ddst in (d1i, d2i):
                                gpsimd.indirect_dma_start(
                                    out=plist_d.rearrange("a b -> (a b)").unsqueeze(1),
                                    out_offset=bass.IndirectOffsetOnAxis(
                                        ddst[:, g:g + 1], 0),
                                    in_=ctid_sb[:, g:g + 1], in_offset=None,
                                ).then_inc(s_pl, 16)
                        gpsimd.wait_ge(s_pl, 16 * (1 + 2 * NT))
                        for j in range(8):
                            gpsimd.dma_start(out=plist_sb[16 * j:16 * j + 16, :],
                                             in_=plist_d[:, :]).then_inc(s_pl, 16)
                        gpsimd.wait_ge(s_pl, 16 * (9 + 2 * NT))
                    gpsimd.nop().then_inc(s_fin, 1)

        if emit_ffn:
            with ExitStack() as st3:
                sb3 = lambda n, sh, dt: st3.enter_context(nc.sbuf_tensor(n, sh, dt))
                wgr = sb3("wgr", [128, 2, DC, F], BF16)
                wur = sb3("wur", [128, 2, DC, F], BF16)
                wdr = sb3("wdr", [128, 2, FC, D], BF16)
                gath = sb3("gath", [128, DC, M2], BF16)
                gu_sb = sb3("gu_sb", [128, FC, M2], BF16)
                dout_sb = sb3("dout_sb", [128, DC, M2], BF16)
                otok = sb3("otok", [128, 4, D], BF16)
                g_ps = [bk[0][:, :], bk[1][:, :]]
                u_ps = [bk[2][:, :], bk[3][:, :]]
                d_ps = [bk[4][:, :], bk[5][:, :]]
                tb_ps = [poolB[:, 0:1024], poolB[:, 1024:2048]]

                class ConstW:
                    def __init__(self, eng):
                        self.eng = eng

                    def begin(self, Pg):
                        self.Pg = Pg

                    def wait(self, sem2, base_stride, off):
                        self.eng.wait_ge(sem2, base_stride * self.Pg + off)

                    def end(self):
                        pass

                class ChainW:
                    def __init__(self, eng, first_pair, cfg):
                        self.eng = eng
                        self.regs = {}
                        self.stride = {}
                        self.first_off = {}
                        self.cur = {}
                        for i, (sem2, stride, foff) in enumerate(cfg):
                            key = id(sem2)
                            r = eng.alloc_register(f"cw{i}_{eng.engine}")
                            eng.reg_mov(r, stride * first_pair + foff)
                            self.regs[key] = r
                            self.stride[key] = stride
                            self.first_off[key] = foff
                            self.cur[key] = foff

                    def begin(self, Pg=None):
                        pass

                    def wait(self, sem2, stride, off):
                        key = id(sem2)
                        r = self.regs[key]
                        d = off - self.cur[key]
                        if d:
                            self.eng.reg_add(r, r, d)
                            self.cur[key] = off
                        self.eng.wait_ge(sem2, r)

                    def end(self):
                        for key, r in self.regs.items():
                            d = self.stride[key] - self.cur[key] + self.first_off[key]
                            if d:
                                self.eng.reg_add(r, r, d)
                            self.cur[key] = self.first_off[key]

                def pair_pe(tensor, slot, W):
                    W.wait(s_gat, 16, 16)
                    mv = gath[:, :, :]
                    for f in range(FC):
                        if f >= 2:
                            W.wait(s_ma, MA, f - 1)
                        for c in range(DC):
                            tensor.ldweights(wgr[:, slot, c, f * 128:(f + 1) * 128])
                            mm = tensor.matmul(g_ps[f % 2],
                                               wgr[:, slot, c, f * 128:(f + 1) * 128],
                                               mv[:, c, :], start=(c == 0),
                                               stop=(c == DC - 1))
                        mm.then_inc(s_mp, 1)
                        if f >= 2:
                            W.wait(s_md, MD, f - 1)
                        for c in range(DC):
                            tensor.ldweights(wur[:, slot, c, f * 128:(f + 1) * 128])
                            mm = tensor.matmul(u_ps[f % 2],
                                               wur[:, slot, c, f * 128:(f + 1) * 128],
                                               mv[:, c, :], start=(c == 0),
                                               stop=(c == DC - 1))
                        mm.then_inc(s_mp, 1)
                    W.wait(s_md, MD, FC)
                    for c in range(DC):
                        if c >= 2:
                            W.wait(s_ma, MA, FC + c - 1)
                        for f in range(FC):
                            tensor.ldweights(wdr[:, slot, f, c * 128:(c + 1) * 128])
                            mm = tensor.matmul(d_ps[c % 2],
                                               wdr[:, slot, f, c * 128:(c + 1) * 128],
                                               gu_sb[:, f, :], start=(f == 0),
                                               stop=(f == FC - 1))
                            if c == 0 and f == 0:
                                mm.then_inc(s_gfree, 1)
                        mm.then_inc(s_mp, 1)
                    W.wait(s_ma, MA, FC + DC)
                    W.wait(s_ost, 16, 0)
                    for tt in range(4):
                        if tt >= 2:
                            W.wait(s_md, MD, FC + tt - 1)
                        for c in range(DC):
                            mm = tensor.transpose(
                                tb_ps[tt % 2][:, c * 128:(c + 1) * 128],
                                dout_sb[:, c, tt * 128:(tt + 1) * 128], idB[:])
                        mm.then_inc(s_mp, 1)

                def pair_act(scalar, W):
                    for f in range(FC):
                        W.wait(s_mp, MP_PE, 2 * f + 1)
                        scalar.activation(out=gu_sb[:, f, :], in_=g_ps[f % 2],
                                          func=AF.Silu).then_inc(s_ma, 1)
                    for c in range(DC):
                        W.wait(s_mp, MP_PE, 2 * FC + c + 1)
                        scalar.copy(out=dout_sb[:, c, :],
                                    in_=d_ps[c % 2]).then_inc(s_ma, 1)

                def pair_dve(vector, W):
                    for f in range(FC):
                        W.wait(s_mp, MP_PE, 2 * f + 2)
                        W.wait(s_ma, MA, f + 1)
                        vector.tensor_mul(gu_sb[:, f, :], gu_sb[:, f, :],
                                          u_ps[f % 2]).then_inc(s_md, 1)
                    for tt in range(4):
                        W.wait(s_mp, MP_PE, 2 * FC + DC + tt + 1)
                        vector.tensor_copy(otok[:, tt, :],
                                           tb_ps[tt % 2]).then_inc(s_md, 1)

                from contextlib import contextmanager
                from concourse.ordered_engine_set import OrderedEngineSet

                @contextmanager
                def reg_loop(eng, r_t, r_end):
                    lid = nc.next_id()
                    ls, le = f"rl{lid}_s", f"rl{lid}_e"
                    engs = OrderedEngineSet([eng.engine])
                    eng.reg_mov(r_t, 0)
                    eng.br(ls)
                    with nc.body(ls, valid_engines=engs):
                        yield
                        eng.reg_add(r_t, r_t, 1)
                        eng.br_lt(r_t, r_end, on_true=ls, on_false=le)
                    nc.switch_bb(le)

                def store_ap(P):
                    if P < NS2:
                        return shout_d[P * M2:(P + 1) * M2, :].rearrange(
                            "(tt p) d2 -> p tt d2", p=128)
                    p = P - NS2
                    return packed_d[p * M2:(p + 1) * M2, :].rearrange(
                        "(tt p) d2 -> p tt d2", p=128)

                with nc.Block() as blk3:
                    @blk3.gpsimd
                    def _(gpsimd):
                        gpsimd.wait_ge(s_pl, 16 * (9 + 2 * NT))
                        npreg = gpsimd.to_reg(gpsimd.value_load(
                            nptot_i[0:1, 0:1], min_val=0, max_val=NPMAX))
                        r_m2 = gpsimd.alloc_register("gat_n")
                        gpsimd.reg_mov(r_m2, M2)
                        NP_ALL = NS2 + NPMAX
                        for P in range(NP_ALL):
                            live = None
                            if P < NS2:
                                if P >= 1:
                                    gpsimd.wait_ge(s_gfree, P)
                                gpsimd.wait_ge(s_hst[0], 16 * (2 * P + 2))
                                gpsimd.wait_ge(s_hst[1], 16 * (2 * P + 2))
                                gpsimd.dma_gather(gath[:, :, :], hrows_d[:, :],
                                                  shidx_sb[:, 32 * P:32 * (P + 1)],
                                                  M2, r_m2, D, elem_step=D,
                                                  transpose=True).then_inc(s_gat, 16)
                            else:
                                p = P - NS2
                                with gpsimd.If_cmp(npreg, p, "IS_GT"):
                                    gpsimd.wait_ge(s_gfree, P)
                                    gpsimd.dma_gather(
                                        gath[:, :, :], hrows_d[:, :],
                                        plist_sb[:, 32 * p:32 * (p + 1)],
                                        M2, r_m2, D, elem_step=D,
                                        transpose=True).then_inc(s_gat, 16)
                                with gpsimd.Else():
                                    gpsimd.wait_ge(s_gat, 16 * P)
                                    gpsimd.inc_swdge_sem([s_gat], [16])
                            if P >= 1:
                                Q = P - 1
                                if Q < NS2:
                                    gpsimd.wait_ge(s_md, MD * (Q + 1))
                                    gpsimd.dma_start(out=store_ap(Q),
                                                     in_=otok[:, :, :]
                                                     ).then_inc(s_ost, 16)
                                else:
                                    with gpsimd.If_cmp(npreg, Q - NS2, "IS_GT"):
                                        gpsimd.wait_ge(s_md, MD * (Q + 1))
                                        gpsimd.dma_start(out=store_ap(Q),
                                                         in_=otok[:, :, :]
                                                         ).then_inc(s_ost, 16)
                                    with gpsimd.Else():
                                        gpsimd.wait_ge(s_ost, 16 * Q)
                                        gpsimd.inc_swdge_sem([s_ost], [16])
                        Q = NP_ALL - 1
                        with gpsimd.If_cmp(npreg, Q - NS2, "IS_GT"):
                            gpsimd.wait_ge(s_md, MD * (Q + 1))
                            gpsimd.dma_start(out=store_ap(Q), in_=otok[:, :, :]
                                             ).then_inc(s_ost, 16)
                        with gpsimd.Else():
                            gpsimd.wait_ge(s_ost, 16 * Q)
                            gpsimd.inc_swdge_sem([s_ost], [16])
                        gpsimd.nop().then_inc(s_fin, 1)

                    @blk3.sync
                    def _(sync):
                        sync.wait_ge(s_pl, 16 * (9 + 2 * NT))
                        sync.wait_ge(s_wcS, 16 * 3)
                        for wsrc, wdst in ((swgbf, wgr), (swubf, wur)):
                            sync.dma_start(
                                out=wdst[:, 0, :, :],
                                in_=wsrc.rearrange("(c p) f -> p c f", p=128)
                            ).then_inc(s_wl[0], 16)
                        sync.dma_start(
                            out=wdr[:, 0, :, :],
                            in_=swdbf.rearrange("(c p) f -> p c f", p=128)
                        ).then_inc(s_wl[0], 16)
                        for e in range(E):
                            slot = (e + 1) % 2
                            sync.wait_ge(s_wcE[e % 2], 48 * (e // 2 + 1))
                            if e >= 1:
                                sync.wait_ge(s_wfree, e)
                            for wsrc, wdst in ((wgbf, wgr), (wubf, wur)):
                                sync.dma_start(
                                    out=wdst[:, slot, :, :],
                                    in_=wsrc[e].rearrange("(c p) f -> p c f", p=128)
                                ).then_inc(s_wl[slot], 16)
                            sync.dma_start(
                                out=wdr[:, slot, :, :],
                                in_=wdbf[e].rearrange("(c p) f -> p c f", p=128)
                            ).then_inc(s_wl[slot], 16)

                    @blk3.tensor
                    def _(tensor):
                        tensor.wait_ge(s_wl[0], 16 * 3)
                        Wc = ConstW(tensor)
                        for k in range(NS2):
                            Wc.begin(k)
                            pair_pe(tensor, 0, Wc)
                            Wc.end()
                        tensor.drain().then_inc(s_wfree, 1)
                        Wx = ChainW(tensor, NS2, [
                            (s_gat, 16, 16), (s_ma, MA, 1), (s_md, MD, 1),
                            (s_ost, 16, 0)])
                        r_t = tensor.alloc_register("pe_rt")
                        r_nw = tensor.alloc_register("pe_rnw")
                        for e in range(E):
                            slot = (e + 1) % 2
                            slot_fills = (1 + (e + 1) // 2) if slot == 0 else (e // 2 + 1)
                            tensor.wait_ge(s_wl[slot], 48 * slot_fills)
                            tensor.reg_load(r_nw, nwin_i[e:e + 1, 0:1])
                            with reg_loop(tensor, r_t, r_nw):
                                Wx.begin()
                                pair_pe(tensor, slot, Wx)
                                Wx.end()
                            tensor.drain().then_inc(s_wfree, 1)
                        tensor.nop().then_inc(s_fin, 1)

                    @blk3.scalar
                    def _(scalar):
                        Wc = ConstW(scalar)
                        for k in range(NS2):
                            Wc.begin(k)
                            pair_act(scalar, Wc)
                            Wc.end()
                        scalar.wait_ge(s_rtd, RTD("padt"))
                        Wx = ChainW(scalar, NS2, [(s_mp, MP_PE, 1)])
                        r_t = scalar.alloc_register("ac_rt")
                        r_nw = scalar.alloc_register("ac_rnw")
                        for e in range(E):
                            scalar.reg_load(r_nw, nwin_i[e:e + 1, 0:1])
                            with reg_loop(scalar, r_t, r_nw):
                                Wx.begin()
                                pair_act(scalar, Wx)
                                Wx.end()
                        scalar.nop().then_inc(s_fin, 1)

                    @blk3.vector
                    def _(vector):
                        Wc = ConstW(vector)
                        for k in range(NS2):
                            Wc.begin(k)
                            pair_dve(vector, Wc)
                            Wc.end()
                        Wx = ChainW(vector, NS2, [(s_mp, MP_PE, 2), (s_ma, MA, 1)])
                        r_t = vector.alloc_register("dv_rt")
                        r_nw = vector.alloc_register("dv_rnw")
                        for e in range(E):
                            vector.reg_load(r_nw, nwin_i[e:e + 1, 0:1])
                            with reg_loop(vector, r_t, r_nw):
                                Wx.begin()
                                pair_dve(vector, Wx)
                                Wx.end()
                        vector.nop().then_inc(s_fin, 1)

            with ExitStack() as st4:
                sb4 = lambda n, sh, dt: st4.enter_context(nc.sbuf_tensor(n, sh, dt))
                row1 = sb4("row1", [128, 2, D], BF16)
                row2 = sb4("row2", [128, 2, D], BF16)
                shr = sb4("shr", [128, 2, D], BF16)
                cacc = sb4("cacc", [128, 2, D], F32)
                ctmp = sb4("ctmp", [128, 2, D], F32)
                sta4 = sb4("sta4", [128, 2, 18], F32)

                with nc.Block() as blk4:
                    @blk4.sync
                    def _(sync):
                        for g in range(NT):
                            if g >= 1:
                                sync.wait_ge(s_cd, CMB_DVE * g)
                                sync.wait_ge(s_ca, CMB_ACT * g)
                            sync.wait_ge(s_ost, 16 * (g // 32 + 1))
                            sync.wait_ge(s_st[0], 16 * (NT // 2))
                            sync.wait_ge(s_st[1], 16 * (NT // 2))
                            sync.dma_start(out=shr[:, g % 2, :],
                                           in_=shout_d[g * 128:(g + 1) * 128, :]
                                           ).then_inc(s_cl, 16)
                            sync.dma_start(out=xm2[:, g % 2, :],
                                           in_=xmid_d[g * 128:(g + 1) * 128, :]
                                           ).then_inc(s_cl, 16)

                    @blk4.gpsimd
                    def _(gpsimd):
                        nptot = gpsimd.value_load(nptot_i[0:1, 0:1],
                                                  min_val=0, max_val=NPMAX)
                        gpsimd.wait_ge(s_ost, 16 * (NS2 + nptot))
                        for g in range(NT):
                            if g >= 1:
                                gpsimd.wait_ge(s_cd, CMB_DVE * (g - 1) + 2)
                            gpsimd.indirect_dma_start(
                                out=row1[:, g % 2, :], out_offset=None,
                                in_=packed_d[:, :],
                                in_offset=bass.IndirectOffsetOnAxis(p1i[:, g:g + 1], 0),
                            ).then_inc(s_cg, 16)
                            gpsimd.indirect_dma_start(
                                out=row2[:, g % 2, :], out_offset=None,
                                in_=packed_d[:, :],
                                in_offset=bass.IndirectOffsetOnAxis(p2i[:, g:g + 1], 0),
                            ).then_inc(s_cg, 16)
                        gpsimd.nop().then_inc(s_fin, 1)

                    @blk4.vector
                    def _(vector):
                        for g in range(NT):
                            vector.wait_ge(s_cg, 32 * (g + 1))
                            vector.wait_ge(s_cl, 32 * (g + 1))
                            vector.scalar_tensor_tensor(
                                cacc[:, g % 2, :], row1[:, g % 2, :], g1w[:, g:g + 1],
                                shr[:, g % 2, :], ALU.mult, ALU.add).then_inc(s_cd, 1)
                            vector.wait_ge(s_cd, CMB_DVE * g + 1)
                            vector.scalar_tensor_tensor(
                                cacc[:, g % 2, :], row2[:, g % 2, :], g2w[:, g:g + 1],
                                cacc[:, g % 2, :], ALU.mult, ALU.add).then_inc(s_cd, 1)
                            vector.wait_ge(s_ca, CMB_ACT * g + 1)
                            vector.wait_ge(s_cd, CMB_DVE * g + 2)
                            vector.tensor_add(cacc[:, g % 2, :], cacc[:, g % 2, :],
                                              ctmp[:, g % 2, :]).then_inc(s_cd, 1)
                            stw = sta4[:, g % 2, :]
                            ln_stats(vector, s_cd, cacc[:, g % 2, :], stw,
                                     CMB_DVE * g + 3, None)
                            ln_finish(vector, s_cd, stw, stw[:, 4:5], stw[:, 5:6],
                                      CMB_DVE * g + 3, s_ca, CMB_ACT * g + 2)
                            vector.wait_ge(s_ca, CMB_ACT * g + 3)
                            vector.tensor_add(ctmp[:, g % 2, :], ctmp[:, g % 2, :],
                                              xm2[:, g % 2, :]).then_inc(s_cd, 1)
                        vector.nop().then_inc(s_fin, 1)

                    @blk4.scalar
                    def _(scalar):
                        for g in range(NT):
                            scalar.wait_ge(s_cl, 32 * (g + 1))
                            if g >= 2:
                                scalar.wait_ge(s_so, 16 * (g - 1))
                            scalar.activation(out=ctmp[:, g % 2, :],
                                              in_=xm2[:, g % 2, :], func=AF.Identity,
                                              scale=ln3a[:, g:g + 1],
                                              bias=ln3b[:, g:g + 1]).then_inc(s_ca, 1)
                            scalar.wait_ge(s_cd, CMB_DVE * g + 8)
                            scalar.activation(out=sta4[:, g % 2, 2:3],
                                              in_=sta4[:, g % 2, 1:2], func=AF.Sqrt,
                                              scale=1.0,
                                              bias=eps_sb[:, 0:1]).then_inc(s_ca, 1)
                            scalar.wait_ge(s_cd, CMB_DVE * g + 10)
                            scalar.activation(out=ctmp[:, g % 2, :],
                                              in_=cacc[:, g % 2, :], func=AF.Identity,
                                              scale=sta4[:, g % 2, 4:5],
                                              bias=sta4[:, g % 2, 5:6]).then_inc(s_ca, 1)
                            scalar.wait_ge(s_cd, CMB_DVE * (g + 1))
                            if g >= 1:
                                scalar.wait_ge(s_so, 16 * g)
                            scalar.dma_start(out=out_d[g * 128:(g + 1) * 128, :],
                                             in_=ctmp[:, g % 2, :]).then_inc(s_so, 16)
                        scalar.wait_ge(s_so, 16 * NT)
                        scalar.nop().then_inc(s_fin, 1)

    mybir.codegen_inst_isa_subclasses(nc)
    return nc


def make_consts():
    c = {}
    c["cidB"] = np.eye(128, dtype=mld.bfloat16)
    c["cidF"] = np.eye(128, dtype=np.float32)
    c["cstrict"] = np.triu(np.ones((128, 128), np.float32), 1)
    c["cones"] = np.ones((128, 128), np.float32)
    c["ciota16"] = np.tile(np.arange(E, dtype=np.float32), (128, 1))
    e32 = np.zeros((32, DC * 128), mld.bfloat16)
    for cc in range(DC):
        for p in range(128):
            e32[4 * cc + p // 32, cc * 128 + p] = 1
    c["cexp32"] = e32
    c["ceps"] = np.tile(np.array([[EPS_LN, EPS_ATT]], np.float32), (128, 1))
    tid = np.zeros((128, NT), np.int16)
    for g in range(NT):
        tid[:, g] = g * 128 + np.arange(128)
    c["ctid"] = tid
    ident = np.arange(T, dtype=np.int16)
    c["cshidx"] = np.tile(ident.reshape(-1, 16).T, (8, 1))
    return c


_CONSTS = None


def core_inputs_a(inp, core):
    global _CONSTS
    if _CONSTS is None:
        _CONSTS = make_consts()
    c = _CONSTS
    return {
        "xin": np.ascontiguousarray(inp["L0_attn_x"][NSEQ * core:NSEQ * (core + 1)]
                                    ).reshape(T, D),
        "wqkvT": np.ascontiguousarray(inp["L0_attn_qkv_weight"].T),
        "wprojT": np.ascontiguousarray(inp["L0_attn_proj_weight"].T),
        "pbias": np.ascontiguousarray(np.asarray(inp["L0_attn_proj_bias"]).reshape(8, 128).T),
        "rw": np.asarray(inp["L0_ffn_router_w"]),
        "rbrep": np.ascontiguousarray(np.tile(
            np.asarray(inp["L0_ffn_router_b"]).reshape(1, 16), (128, 1))),
        "cidB": c["cidB"], "cidF": c["cidF"], "cstrict": c["cstrict"],
        "cones": c["cones"], "ciota16": c["ciota16"], "cexp32": c["cexp32"],
        "ceps": c["ceps"],
    }


def core_inputs_b(inp, core, counts_all):
    m = core_inputs_a(inp, core)
    c = _CONSTS
    mask = np.zeros((8, 1), np.float32)
    mask[:core, 0] = 1.0
    m.update({
        "wg": np.asarray(inp["L0_ffn_w_gate"]),
        "wu": np.asarray(inp["L0_ffn_w_up"]),
        "wdn": np.asarray(inp["L0_ffn_w_down"]),
        "swg": np.asarray(inp["L0_ffn_sw_gate"]),
        "swu": np.asarray(inp["L0_ffn_sw_up"]),
        "swd": np.asarray(inp["L0_ffn_sw_down"]),
        "ctid": c["ctid"], "cshidx": c["cshidx"],
        "cmask": mask,
        "cntall": counts_all,
        "cntmeT": np.ascontiguousarray(counts_all[core].reshape(16, 1)),
    })
    return m


from concourse.bass_utils import run_bass_kernel_spmd


def kernel(**inputs):
    ncA = build(emit_ffn=False)
    mapsA = [core_inputs_a(inputs, c) for c in range(8)]
    resA = run_bass_kernel_spmd(ncA, mapsA, core_ids=list(range(8)))
    counts_all = np.stack([np.asarray(resA.results[c]["counts_out"])[0]
                           for c in range(8)]).astype(np.float32)

    ncB = build(emit_ffn=True)
    mapsB = [core_inputs_b(inputs, c, counts_all) for c in range(8)]
    resB = run_bass_kernel_spmd(ncB, mapsB, core_ids=list(range(8)))
    out = np.concatenate([np.asarray(resB.results[c]["out"], dtype=np.float32)
                          for c in range(8)], axis=0)
    return out.reshape(32, 2048, 1024)
